# version 16
# baseline (speedup 1.0000x reference)
import math
import time
from contextlib import ExitStack

import numpy as np
import ml_dtypes

import concourse.bass as bass
import concourse.mybir as mybir
from concourse.bass_utils import run_bass_kernel_spmd

F32 = mybir.dt.float32
BF16 = mybir.dt.bfloat16
AF = mybir.ActivationFunctionType
ALU = mybir.AluOpType
AX = mybir.AxisListType

B_, S_, D_ = 4, 4096, 1024
T_ = 2048
NH1 = 16
ROT1 = 32
WIN = 512
HID0 = 2730
NBLK = 43
HID2 = 2 * 64 * NBLK
NEG = -30000.0
MAGIC = [0x4D474301, 0x4D474302]
_EPOCH = [(int(time.time_ns() // 1000) & 0x0FFFFFFF) | 0x10000000]


def _fresh_epoch():
    _EPOCH[0] = ((int(time.time_ns() // 1000) + _EPOCH[0] * 2 + 16)
                 & 0x0FFFFFFF) | 0x10000000
    return _EPOCH[0]
POLL_CORES = 8
LNEPS = 1e-5

STAGE = "full"


class _Ev:
    __slots__ = ("key", "val", "sem")
    def __init__(self, key):
        self.key = key
        self.val = None
        self.sem = key

class _Op:
    __slots__ = ("fn", "waits", "signal", "step", "tag")
    def __init__(self, fn, waits, signal, step, tag):
        self.fn = fn; self.waits = waits; self.signal = signal
        self.step = step; self.tag = tag

_COMPUTE_SEM = {"t": "T", "v": "V", "s": "S", "g": "G"}

DMA_POOL = {"DY": 12, "DG": 8, "DS": 4, "C": 2}

class Builder:

    def __init__(self, nc):
        self.nc = nc
        self.q = {k: [] for k in ("t", "v", "s", "g", "y")}
        self.compute_keys = ["T", "V", "S", "G"]
        self.sem_names = list(self.compute_keys)
        for k, n in DMA_POOL.items():
            self.sem_names += [f"{k}{i}" for i in range(n)]
        self.recent = {k: [] for k in list(DMA_POOL) + self.compute_keys}
        self.last = {k: None for k in list(DMA_POOL) + self.compute_keys}

    def ev(self, key):
        return _Ev(key)

    def add(self, eng, fn, waits=(), signal=None, tag=""):
        if signal is not None:
            r = self.recent[signal.key]
            r.append(signal)
            keep = DMA_POOL.get(signal.key, 1)
            del r[:-keep]
            self.last[signal.key] = signal
        self.q[eng].append(_Op(fn, [w for w in waits if w is not None],
                               signal, 0, tag))
        return signal

    def dma(self, eng, out, in_, waits=(), signal=None, tag="", **kw):
        assert eng in ("y", "g", "s", "v")
        def fn(e, out=out, in_=in_, kw=kw):
            return e.dma_start(out=out, in_=in_, **kw)
        return self.add(eng, fn, waits, signal, tag)

    def barrier(self, tag="bar"):
        waits = [e for r in self.recent.values() for e in r]
        evs = []
        for eng, key in (("t", "T"), ("v", "V"), ("s", "S"), ("g", "G")):
            evs.append(self.add(eng, lambda e: e.nop(), waits,
                                signal=self.ev(key), tag=tag + eng))
        self.add("y", lambda e: e.nop(), waits, tag=tag + "y")
        return evs

    def finalize(self, stack):
        waits = [e for r in self.recent.values() for e in r]
        self.add("g", lambda e: e.nop(), waits, tag="final_drain")
        nc = self.nc
        sems = {n: stack.enter_context(nc.semaphore(name=f"sem{n}"))
                for n in self.sem_names + ["PL0", "PL1", "PLB"]}
        self.sems = sems
        counts = {n: 0 for n in self.sem_names}
        pool_i = {k: 0 for k in DMA_POOL}
        for eng in ("v", "s", "g"):
            key = _COMPUTE_SEM[eng]
            prev = None
            for op in self.q[eng]:
                if op.signal is not None and op.signal.key in DMA_POOL:
                    continue
                if op.signal is None:
                    op.signal = _Ev(key)
                if prev is not None:
                    op.waits.append(prev)
                prev = op.signal
        for eng in ("t", "v", "s", "g", "y"):
            for op in self.q[eng]:
                sig = op.signal
                if sig is None:
                    continue
                if sig.key in DMA_POOL:
                    i = pool_i[sig.key]
                    pool_i[sig.key] = (i + 1) % DMA_POOL[sig.key]
                    name = f"{sig.key}{i}"
                    if counts[name] > 0:
                        prev = _Ev(sig.key)
                        prev.sem = name
                        prev.val = counts[name]
                        op.waits.append(prev)
                    op.step = 16
                else:
                    name = sig.key
                    op.step = 1
                counts[name] += op.step
                sig.sem = name
                sig.val = counts[name]
        self._check()
        block = stack.enter_context(nc.Block())
        watermark = {eng: {n: 0 for n in self.sem_names}
                     for eng in ("t", "v", "s", "g", "y")}

        def emit(eng_name, engine):
            wm = watermark[eng_name]
            own = _COMPUTE_SEM.get(eng_name)
            for op in self.q[eng_name]:
                for evt in op.waits:
                    assert evt.val is not None, f"unsignaled event: {op.tag}"
                    if evt.val > wm[evt.sem]:
                        engine.wait_ge(sems[evt.sem], evt.val)
                        wm[evt.sem] = evt.val
                ins = op.fn(engine)
                if op.signal is not None:
                    ins.then_inc(sems[op.signal.sem], op.step)

        @block.tensor
        def _(e): emit("t", e)
        @block.vector
        def _(e): emit("v", e)
        @block.scalar
        def _(e): emit("s", e)
        @block.gpsimd
        def _(e): emit("g", e)
        @block.sync
        def _(e): emit("y", e)

    def _check(self):
        idx = {k: 0 for k in self.q}
        done = {n: 0 for n in self.sem_names}
        progress = True
        while progress:
            progress = False
            for eng, ops in self.q.items():
                while idx[eng] < len(ops):
                    op = ops[idx[eng]]
                    if any(e.val is None or done[e.sem] < e.val for e in op.waits):
                        break
                    if op.signal is not None:
                        done[op.signal.sem] += op.step
                    idx[eng] += 1
                    progress = True
        stuck = {e: (i, len(self.q[e])) for e, i in idx.items() if i < len(self.q[e])}
        if stuck:
            lines = []
            for e, (i, n) in stuck.items():
                op = self.q[e][i]
                lines.append(f"  {e}: op {i}/{n} tag={op.tag!r} waits="
                             f"{[(x.sem, x.val, done[x.sem]) for x in op.waits]}")
            raise RuntimeError("builder deadlock:\n" + "\n".join(lines))


class NS:
    pass


def _v3(ap, inner):
    return ap.rearrange("p (c d) -> p c d", d=inner)



def _din(nc, name, shape, dtype):
    if not hasattr(nc, "_my_inputs"):
        nc._my_inputs = set()
    nc._my_inputs.add(name)
    return nc.dram_tensor(name, shape, dtype, kind="ExternalInput").ap()



def _emit_phase1(nc, B, st, H):
    ev = B.ev
    s1 = ExitStack()
    H.O1 = s1.enter_context(nc.sbuf_tensor("O1t", [128, 16 * D_], BF16))
    rcf = s1.enter_context(nc.sbuf_tensor("rcfs", [128, 2 * D_], F32))
    rsf = s1.enter_context(nc.sbuf_tensor("rsfs", [128, 2 * D_], F32))
    rcq = s1.enter_context(nc.sbuf_tensor("rcqs", [128, D_], F32))
    rsq = s1.enter_context(nc.sbuf_tensor("rsqs", [128, D_], F32))
    idb = s1.enter_context(nc.sbuf_tensor("idb", [128, 128], BF16))
    idf32 = s1.enter_context(nc.sbuf_tensor("idf32", [128, 128], F32))

    tb = []
    for dst, src in ((rcf, H.rc1f), (rsf, H.rs1f)):
        tb.append(B.dma("y", _v3(dst.ap(), D_), src.rearrange("(c p) d -> p c d", p=128),
                        signal=ev("DY"), tag="tbl"))
    tb.append(B.dma("y", rcq[:, :], H.rcq, signal=ev("DY"), tag="tbl"))
    tb.append(B.dma("y", rsq[:, :], H.rsq, signal=ev("DY"), tag="tbl"))
    ev_idb = B.dma("y", idb[:, :], H.idbf, signal=ev("DY"), tag="idb")
    ev_idf = B.dma("y", idf32[:, :], H.idf, signal=ev("DY"), tag="idf")
    ev_tbl = tb

    with ExitStack() as sa:
        Wqb = sa.enter_context(nc.sbuf_tensor("Wqb", [128, 8 * D_], BF16))
        Wkb = sa.enter_context(nc.sbuf_tensor("Wkb", [128, 8 * D_], BF16))
        Wvb = sa.enter_context(nc.sbuf_tensor("Wvb", [128, 8 * D_], BF16))
        ebf = sa.enter_context(nc.sbuf_tensor("ebf", [128, 4 * D_], BF16))
        ebq = sa.enter_context(nc.sbuf_tensor("ebq", [128, 2 * D_], BF16))
        embT = sa.enter_context(nc.sbuf_tensor("embT", [128, 32 * 128], BF16))
        qembT = sa.enter_context(nc.sbuf_tensor("qembT", [128, 16 * 128], BF16))
        kro = sa.enter_context(nc.sbuf_tensor("kro", [128, 4 * D_], BF16))
        qro = sa.enter_context(nc.sbuf_tensor("qro", [128, 2 * D_], BF16))
        vbf = sa.enter_context(nc.sbuf_tensor("vbf", [128, 4 * D_], BF16))
        t1f = sa.enter_context(nc.sbuf_tensor("t1f", [128, 512], F32))
        t2f = sa.enter_context(nc.sbuf_tensor("t2f", [128, 512], F32))
        kt = sa.enter_context(nc.sbuf_tensor("kt", [128, 4096], BF16))
        qt = sa.enter_context(nc.sbuf_tensor("qt", [128, 2048], BF16))
        vt = [sa.enter_context(nc.sbuf_tensor(f"vt{i}", [128, 32 * 65], BF16))
              for i in range(2)]
        pab = sa.enter_context(nc.sbuf_tensor("pab", [128, 4 * 1024], BF16))
        svc = sa.enter_context(nc.sbuf_tensor("svc", [128, 1024], F32))
        rsum = sa.enter_context(nc.sbuf_tensor("rsum", [128, 16], F32))
        PT = sa.enter_context(nc.psum_tensor("PT", [128, 2048], BF16))
        PP = sa.enter_context(nc.psum_tensor("PP", [128, 512], F32))
        SP = sa.enter_context(nc.psum_tensor("SP", [128, 1024], F32))
        SVp = sa.enter_context(nc.psum_tensor("SVp", [128, 1024], F32))
        EP = sa.enter_context(nc.psum_tensor("EP", [128, 512], F32))

        ev_w = []
        for W, src in ((Wqb, H.WqT), (Wkb, H.WkT), (Wvb, H.WvT)):
            ev_w.append(B.dma("g", _v3(W.ap(), D_),
                              src.rearrange("(m p) o -> p m o", p=128),
                              signal=ev("DG"), tag="wld"))
        ev_vones = None
        for i in range(2):
            ev_vones = B.add("v", lambda e, t=vt[i]: e.memset(t[:, :], 1.0),
                             signal=ev("V"), tag="vones")
        ev_svz = B.add("v", lambda e: e.memset(svc[:, :], 0.0),
                       signal=ev("V"), tag="svz")

        kt3 = kt.ap().rearrange("p (t c) -> p t c", c=16)
        qt3 = qt.ap().rearrange("p (t c) -> p t c", c=16)

        g = NS()
        g.ebf = g.ebq = None
        g.embT = g.qembT = None
        g.kro = g.qro = None
        g.vbf = None
        g.kt = g.qt = None
        g.vt = [ev_vones, ev_vones]
        g.PT = [ev_idb, ev_idb]
        g.PP = None
        g.SPexp = [None, None]
        g.SVepi = None
        g.EP = ev_idf
        g.pab = [None] * 4
        g.O1done = None

        def rope(dst_ap, psrc, cslc, sslc, tagp, waits=()):
            B.add("v", lambda e: e.tensor_mul(t1f[:, :], psrc, cslc),
                  waits=waits, tag=tagp + "a")
            pv = psrc.rearrange("p (b two) -> p b two", two=2)
            sv_ = sslc.rearrange("p (b two) -> p b two", two=2)
            t2v = t2f.ap().rearrange("p (b two) -> p b two", two=2)
            B.add("v", lambda e: e.tensor_mul(t2v[:, :, 0], pv[:, :, 1], sv_[:, :, 0]),
                  tag=tagp + "b")
            B.add("v", lambda e: e.tensor_mul(t2v[:, :, 1], pv[:, :, 0], sv_[:, :, 1]),
                  tag=tagp + "c")
            return B.add("v", lambda e: e.tensor_add(dst_ap, t1f[:, :], t2f[:, :]),
                         signal=ev("V"), tag=tagp + "d")

        for a in range(8):
            e_ebf = B.dma("g", _v3(ebf.ap(), D_),
                          H.emb[512 * a:512 * (a + 1), :]
                          .rearrange("(c p) d -> p c d", p=128),
                          waits=(g.ebf or []), signal=ev("DG"), tag=f"ebf{a}")
            e_ebq = B.dma("g", _v3(ebq.ap(), D_),
                          H.embq[256 * a:256 * (a + 1), :]
                          .rearrange("(c p) d -> p c d", p=128),
                          waits=(g.ebq or []), signal=ev("DG"), tag=f"ebq{a}")
            lastT = None
            for c4 in range(4):
                for half in range(2):
                    par = (2 * c4 + half) % 2
                    tw = [e_ebf, g.PT[par], g.embT]
                    for mm in range(4):
                        m = 4 * half + mm
                        lastT = B.add(
                            "t", lambda e, c4=c4, m=m, mm=mm, par=par: e.transpose(
                                PT[:, 1024 * par + 128 * mm:1024 * par + 128 * (mm + 1)],
                                ebf[:, 1024 * c4 + 128 * m:1024 * c4 + 128 * (m + 1)],
                                idb[:, :]),
                            waits=tw if mm == 0 else (),
                            signal=ev("T") if mm == 3 else None, tag=f"eT{a}")
                    eng_ = "v" if par == 0 else "s"
                    g.PT[par] = B.add(
                        eng_, lambda e, c4=c4, half=half, par=par, eng_=eng_:
                        (e.tensor_copy if eng_ == "v" else
                         lambda d, s_: e.activation(d, s_, AF.Copy))(
                            embT[:, (8 * c4 + 4 * half) * 128:(8 * c4 + 4 * half + 4) * 128],
                            PT[:, 1024 * par:1024 * par + 512]),
                        waits=[lastT], signal=ev("V" if par == 0 else "S"),
                        tag=f"eTe{a}")
            ev_embT = [g.PT[0], g.PT[1]]
            for ch in range(2):
                for half in range(2):
                    par = (2 * ch + half) % 2
                    tw = [e_ebq, g.PT[par], g.qembT]
                    for mm in range(4):
                        m = 4 * half + mm
                        lastT = B.add(
                            "t", lambda e, ch=ch, m=m, mm=mm, par=par: e.transpose(
                                PT[:, 1024 * par + 128 * mm:1024 * par + 128 * (mm + 1)],
                                ebq[:, 1024 * ch + 128 * m:1024 * ch + 128 * (m + 1)],
                                idb[:, :]),
                            waits=tw if mm == 0 else (),
                            signal=ev("T") if mm == 3 else None, tag=f"qT{a}")
                    eng_ = "v" if par == 0 else "s"
                    g.PT[par] = B.add(
                        eng_, lambda e, ch=ch, half=half, par=par, eng_=eng_:
                        (e.tensor_copy if eng_ == "v" else
                         lambda d, s_: e.activation(d, s_, AF.Copy))(
                            qembT[:, (8 * ch + 4 * half) * 128:(8 * ch + 4 * half + 4) * 128],
                            PT[:, 1024 * par:1024 * par + 512]),
                        waits=[lastT], signal=ev("V" if par == 0 else "S"),
                        tag=f"qTe{a}")
            ev_qembT = [g.PT[0], g.PT[1]]

            ev_kro = ev_qro = ev_vevac = None
            for c4 in range(4):
                for which, Wb in (("k", Wkb), ("v", Wvb)):
                    for eb in range(2):
                        lastm = None
                        for m in range(8):
                            lastm = B.add(
                                "t", lambda e, c4=c4, m=m, eb=eb, Wb=Wb: e.matmul(
                                    PP[:, :],
                                    embT[:, (8 * c4 + m) * 128:(8 * c4 + m + 1) * 128],
                                    Wb[:, 1024 * m + 512 * eb:1024 * m + 512 * (eb + 1)],
                                    start=(m == 0), stop=(m == 7)),
                                waits=ev_embT + [g.PP] + ev_w if m == 0 else (),
                                signal=ev("T") if m == 7 else None, tag=f"kv{a}")
                        if which == "k":
                            g.PP = rope(kro[:, 1024 * c4 + 512 * eb:1024 * c4 + 512 * (eb + 1)],
                                        PP[:, :],
                                        rcf[:, 1024 * (c4 % 2) + 512 * eb:1024 * (c4 % 2) + 512 * (eb + 1)],
                                        rsf[:, 1024 * (c4 % 2) + 512 * eb:1024 * (c4 % 2) + 512 * (eb + 1)],
                                        f"krp{a}",
                                        waits=[lastm] + (g.kro or []) + ev_tbl)
                            ev_kro = g.PP
                        else:
                            g.PP = B.add(
                                "s", lambda e, c4=c4, eb=eb: e.copy(
                                    vbf[:, 1024 * c4 + 512 * eb:1024 * c4 + 512 * (eb + 1)],
                                    PP[:, :]),
                                waits=[lastm, g.vbf], signal=ev("S"), tag=f"vev{a}")
                            ev_vevac = g.PP
            for ch in range(2):
                for eb in range(2):
                    lastm = None
                    for m in range(8):
                        lastm = B.add(
                            "t", lambda e, ch=ch, m=m, eb=eb: e.matmul(
                                PP[:, :],
                                qembT[:, (8 * ch + m) * 128:(8 * ch + m + 1) * 128],
                                Wqb[:, 1024 * m + 512 * eb:1024 * m + 512 * (eb + 1)],
                                start=(m == 0), stop=(m == 7)),
                            waits=ev_qembT + [g.PP] + ev_w if m == 0 else (),
                            signal=ev("T") if m == 7 else None, tag=f"qp{a}")
                    g.PP = rope(qro[:, 1024 * ch + 512 * eb:1024 * ch + 512 * (eb + 1)],
                                PP[:, :],
                                rcq[:, 512 * eb:512 * (eb + 1)],
                                rsq[:, 512 * eb:512 * (eb + 1)], f"qrp{a}",
                                waits=[lastm] + (g.qro or []) + ev_tbl)
                    ev_qro = g.PP

            ev_vs = B.dma("y", H.vscr[512 * a:512 * (a + 1), :]
                          .rearrange("(c p) d -> p c d", p=128),
                          _v3(vbf.ap(), D_), waits=[ev_vevac], signal=ev("DY"), tag=f"vs{a}")
            g.vbf = ev_vs
            ev_vt = []
            for i in range(2):
                h4 = 2 * a + i
                src = H.vscr[256 * h4:256 * (h4 + 1), :] \
                    .rearrange("(cc pp) (pl d) -> (pp pl) cc d", pp=8, pl=16)
                dst = vt[i].ap().rearrange("p (cc e) -> p cc e", e=65)[:, :, 0:64]
                ev_vt.append(B.dma("y", dst, src, waits=[ev_vs, g.vt[i]],
                                   signal=ev("DY"), tag=f"vt{a}{i}"))

            ev_ktb = ev_qtb = None
            for cb in range(16):
                par = cb % 2
                lastT = None
                first = True
                for base, c4s in ((0, (0, 1)), (64, (2, 3))):
                    for jc, c4 in enumerate(c4s):
                        last = (base == 64 and jc == 1)
                        lastT = B.add(
                            "t", lambda e, cb=cb, c4=c4, base=base, jc=jc, par=par:
                            e.transpose(
                                PT[base:base + 64,
                                   1024 * par + 128 * jc:1024 * par + 128 * (jc + 1)],
                                kro[:, 1024 * c4 + 64 * cb:1024 * c4 + 64 * (cb + 1)],
                                idb[:, :],
                                tile_position=(0, base)),
                            waits=[ev_kro, g.PT[par], g.kt] if first else (),
                            signal=ev("T") if last else None, tag=f"ktT{a}")
                        first = False
                eng_ = "v" if par == 0 else "s"
                g.PT[par] = B.add(
                    eng_, lambda e, cb=cb, par=par, eng_=eng_:
                    (e.tensor_copy if eng_ == "v" else
                     lambda d, s_: e.activation(d, s_, AF.Copy))(
                        kt3[:, 0:256, cb], PT[:, 1024 * par:1024 * par + 256]),
                    waits=[lastT], signal=ev("V" if par == 0 else "S"),
                    tag=f"ktE{a}")
            ev_ktb = [g.PT[0], g.PT[1]]
            for cb in range(16):
                par = cb % 2
                lastT = None
                for j, (base, ch) in enumerate(((0, 0), (64, 1))):
                    lastT = B.add(
                        "t", lambda e, cb=cb, ch=ch, base=base, par=par:
                        e.transpose(
                            PT[base:base + 64, 1024 * par:1024 * par + 128],
                            qro[:, 1024 * ch + 64 * cb:1024 * ch + 64 * (cb + 1)],
                            idb[:, :],
                            tile_position=(0, base)),
                        waits=[ev_qro, g.PT[par], g.qt] if j == 0 else (),
                        signal=ev("T") if j == 1 else None, tag=f"qtT{a}")
                eng_ = "v" if par == 0 else "s"
                g.PT[par] = B.add(
                    eng_, lambda e, cb=cb, par=par, eng_=eng_:
                    (e.tensor_copy if eng_ == "v" else
                     lambda d, s_: e.activation(d, s_, AF.Copy))(
                        qt3[:, 0:128, cb], PT[:, 1024 * par:1024 * par + 128]),
                    waits=[lastT], signal=ev("V" if par == 0 else "S"),
                    tag=f"qtE{a}")
            ev_qtb = [g.PT[0], g.PT[1]]

            ev_s = {}
            ev_e = {}
            ev_v = {}
            last_sv = [None, None]
            prev_pair_sv = [g.vt[0], g.vt[1]]

            def emit_sv(qpb, cc):
                slot = cc % 4
                for i in range(2):
                    sw = [ev_e[(qpb, cc, i)]]
                    if cc == 0:
                        sw = sw + [ev_vt[i], g.SVepi]
                    ev_v[(qpb, cc, i)] = B.add(
                        "t", lambda e, cc=cc, slot=slot, i=i, qpb=qpb: e.matmul(
                            SVp[0:65, 512 * i:512 * (i + 1)],
                            vt[i][:, 65 * cc:65 * cc + 65],
                            pab[:, 1024 * slot + 512 * i:1024 * slot + 512 * (i + 1)],
                            start=(cc == 0), stop=(cc == 31)),
                        waits=sw, signal=ev("T"), tag=f"sv{a}")

            for qpb in range(4):
                for c in range(32):
                    for i in range(2):
                        w0 = (ev_ktb + ev_qtb + [g.SPexp[i]]
                              if (qpb == 0 and c == 0) else [])
                        if c > 0:
                            w0 = w0 + [ev_e.get((qpb, c - 1, i))]
                        elif qpb > 0:
                            w0 = w0 + [ev_e.get((qpb - 1, 31, i))]
                        ev_s[(qpb, c, i)] = B.add(
                            "t", lambda e, c=c, qpb=qpb, i=i: e.matmul(
                                SP[:, 512 * i:512 * (i + 1)],
                                kt[64 * i:64 * (i + 1), 128 * c:128 * (c + 1)],
                                qt[64 * i:64 * (i + 1), 512 * qpb:512 * (qpb + 1)],
                                start=True, stop=True),
                            waits=w0, signal=ev("T"), tag=f"s{i}{a}")
                    if c > 0:
                        emit_sv(qpb, c - 1)
                    slot = c % 4
                    for i in range(2):
                        ew = [ev_s[(qpb, c, i)]]
                        if (qpb, c - 4, i) in ev_v:
                            ew.append(ev_v[(qpb, c - 4, i)])
                        elif c < 4 and qpb > 0:
                            ew.append(ev_v[(qpb - 1, 28 + c, i)])
                        elif c < 4:
                            ew.append(prev_pair_sv[i])
                        ev_e[(qpb, c, i)] = B.add(
                            "s", lambda e, slot=slot, i=i: e.activation(
                                pab[:, 1024 * slot + 512 * i:
                                    1024 * slot + 512 * (i + 1)],
                                SP[:, 512 * i:512 * (i + 1)],
                                AF.Exp, scale=0.25),
                            waits=ew, signal=ev("S"), tag=f"ex{a}")
                emit_sv(qpb, 31)
                last_sv = [ev_v[(qpb, 31, 0)], ev_v[(qpb, 31, 1)]]

                EPr = EP.ap().rearrange("p (j x) -> p j x", x=128)
                ecp = []
                for i in range(2):
                    ecp.append(B.add(
                        "v", lambda e, i=i: e.tensor_copy(
                            svc[0:65, 512 * i:512 * (i + 1)],
                            SVp[0:65, 512 * i:512 * (i + 1)]),
                        waits=[last_sv[i], ev_svz], signal=ev("V"),
                        tag=f"svc{a}"))
                g.SVepi = ecp[-1]
                for i in range(2):
                    lastT = None
                    for j in range(4):
                        lastT = B.add(
                            "t", lambda e, i=i, j=j: e.transpose(
                                EP[:, 128 * j:128 * (j + 1)],
                                svc[:, 512 * i + 128 * j:512 * i + 128 * (j + 1)],
                                idf32[:, :]),
                            waits=[ecp[i], ev_idf, g.EP] if j == 0 else (),
                            signal=ev("T") if j == 3 else None,
                            tag=f"epT{a}")
                    ro = 8 * i
                    B.add("v", lambda e, ro=ro: e.tensor_copy(
                        rsum[:, ro:ro + 4], EPr[:, :, 64]),
                        waits=[lastT], tag=f"rs{a}{i}")
                    B.add("v", lambda e, ro=ro: e.reciprocal(
                        rsum[:, ro + 4:ro + 8], rsum[:, ro:ro + 4]),
                        tag=f"rr{a}{i}")
                    evo = None
                    for j in range(4):
                        cidx = 4 * qpb + j
                        hg = H.hoff + 2 * a + i
                        evo = B.add(
                            "v", lambda e, j=j, cidx=cidx, hg=hg, ro=ro:
                            e.tensor_single_scalar(
                                H.O1[:, 1024 * cidx + 64 * hg:1024 * cidx + 64 * (hg + 1)],
                                EP[:, 128 * j:128 * j + 64],
                                rsum[:, ro + 4 + j:ro + 5 + j], ALU.mult),
                            signal=ev("V") if j == 3 else None, tag=f"o1{a}{i}")
                    g.EP = evo
                g.O1done = g.EP

            g.ebf = ev_embT
            g.ebq = ev_qembT
            g.embT = ev_s[(3, 31, 1)]
            g.qembT = ev_s[(3, 31, 1)]
            g.kro = ev_ktb
            g.qro = ev_qtb
            g.kt = ev_s[(3, 31, 1)]
            g.qt = ev_s[(3, 31, 1)]
            g.vt = [last_sv[0], last_sv[1]]
            g.SPexp = [ev_e[(3, 31, 0)], ev_e[(3, 31, 1)]]

        H.ev_p1_attn_done = g.O1done

    bar = B.barrier("p1bar")
    ev = B.ev
    H.ev_ln_chunk = {}
    H.x = st.enter_context(nc.sbuf_tensor("x_stream", [128, 16 * D_], F32,
                                          side="right"))
    with ExitStack() as sb:
        WrTb = sb.enter_context(nc.sbuf_tensor("WrTb", [128, 8 * D_], BF16))
        sqs = sb.enter_context(nc.sbuf_tensor("sqs", [128, D_], F32))
        stt = sb.enter_context(nc.sbuf_tensor("stt", [128, 16], F32))
        PQ = [sb.enter_context(nc.psum_tensor(f"PQ{i}", [128, 512], F32))
              for i in range(2)]

        ev_wr = B.dma("g", _v3(WrTb.ap(), D_),
                      H.WrT.rearrange("(m p) o -> p m o", p=128),
                      waits=bar, signal=ev("DG"), tag="wrld")
        ev_xl = B.dma("y", _v3(H.x.ap(), D_),
                      H.embo.rearrange("(c p) d -> p c d", p=128),
                      waits=bar, signal=ev("DY"), tag="xload")

        g_pq = [bar[0], bar[0]]
        ev_ln = None
        for qloc in range(2):
            for sc in range(8):
                cidx = 8 * qloc + sc
                ev_mm = {}
                for eb in range(2):
                    pq = PQ[eb]
                    lastm = None
                    for jc in range(8):
                        lastm = B.add(
                            "t", lambda e, qloc=qloc, sc=sc, eb=eb, jc=jc, pq=pq: e.matmul(
                                pq[:, :],
                                WrTb[:, 1024 * jc + 128 * sc:1024 * jc + 128 * (sc + 1)],
                                H.O1[:, 1024 * (8 * qloc + jc) + 512 * eb:
                                     1024 * (8 * qloc + jc) + 512 * (eb + 1)],
                                start=(jc == 0), stop=(jc == 7)),
                            waits=[ev_wr, H.ev_p1_attn_done, g_pq[eb]] if jc == 0 else (),
                            signal=ev("T") if jc == 7 else None, tag=f"wr{qloc}{sc}")
                    ev_mm[eb] = lastm
                ra = []
                for eb in range(2):
                    ra.append(B.add(
                        "v", lambda e, cidx=cidx, eb=eb: e.tensor_add(
                            H.x[:, 1024 * cidx + 512 * eb:1024 * cidx + 512 * (eb + 1)],
                            H.x[:, 1024 * cidx + 512 * eb:1024 * cidx + 512 * (eb + 1)],
                            PQ[eb][:, :]),
                        waits=[ev_mm[eb], ev_xl], signal=ev("V"), tag=f"ra{cidx}"))
                    g_pq[eb] = ra[-1]
                xc = H.x[:, 1024 * cidx:1024 * (cidx + 1)]
                B.add("v", lambda e, xc=xc: e.reduce_sum(stt[:, 0:1], xc, axis=AX.X),
                      waits=[ra[1]], tag=f"ls{cidx}")
                ev_sq = B.add("s", lambda e, xc=xc: e.activation(
                    sqs[:, :], xc, AF.Square, accum_out=stt[:, 1:2]),
                    waits=[ra[1]], signal=ev("S"), tag=f"lq{cidx}")
                B.add("v", lambda e: e.tensor_scalar_mul(stt[:, 2:3], stt[:, 0:1],
                                                         -1.0 / D_), tag="l1")
                B.add("v", lambda e: e.tensor_scalar_mul(stt[:, 3:4], stt[:, 1:2],
                                                         1.0 / D_),
                      waits=[ev_sq], tag="l2")
                B.add("v", lambda e: e.tensor_mul(stt[:, 4:5], stt[:, 2:3],
                                                  stt[:, 2:3]), tag="l3")
                ev_var = B.add("v", lambda e: e.scalar_tensor_tensor(
                    stt[:, 5:6], stt[:, 3:4], LNEPS, stt[:, 4:5],
                    ALU.add, ALU.subtract),
                    signal=ev("V"), tag="l4")
                ev_sd = B.add("s", lambda e: e.activation(stt[:, 6:7], stt[:, 5:6],
                                                          AF.Sqrt),
                              waits=[ev_var], signal=ev("S"), tag="l5")
                B.add("v", lambda e: e.reciprocal(stt[:, 7:8], stt[:, 6:7]),
                      waits=[ev_sd], tag="l6")
                ev_ln = B.add("v", lambda e, xc=xc: e.tensor_scalar(
                    xc, xc, stt[:, 2:3], stt[:, 7:8], ALU.add, ALU.mult),
                    signal=ev("V"), tag=f"lapp{cidx}")
                H.ev_ln_chunk[cidx] = ev_ln
                if H.stage == "p1":
                    H.ev_x1 = B.dma(
                        "y", H.out.rearrange("(c p) d -> p c d", p=128)[:, cidx, :],
                        xc, waits=[ev_ln], signal=ev("DY"), tag=f"xout{cidx}")
        H.ev_x1_ln = ev_ln
    s1.close()
    H.p1bar = B.barrier("p1end")
    return H



def _emit_phase2(nc, B, st, H):
    ev = B.ev
    RG = [[0, 1], [2, 3], [4, 5], [6, 7]]
    xh = st.enter_context(nc.sbuf_tensor("xh", [128, 9 * D_], BF16))
    xh8 = st.enter_context(nc.sbuf_tensor("xh8", [128, D_], F32,
                                           side="right"))
    gateg = st.enter_context(nc.sbuf_tensor("gateg", [128, 1], F32))
    wsendg = st.enter_context(nc.sbuf_tensor("wsendg", [128, 1], F32))
    ones1 = st.enter_context(nc.sbuf_tensor("ones1", [128, 128], BF16))
    ones1f = st.enter_context(nc.sbuf_tensor("ones1f", [1, 128], F32))
    idb2 = st.enter_context(nc.sbuf_tensor("idb2", [128, 128], BF16))
    idf2 = st.enter_context(nc.sbuf_tensor("idf2", [128, 128], F32))
    mlo = st.enter_context(nc.sbuf_tensor("mlo", [128, 128], BF16))
    mhi = st.enter_context(nc.sbuf_tensor("mhi", [128, 128], BF16))
    mpo = st.enter_context(nc.sbuf_tensor("mpo", [128, 128], BF16))
    x1scr = nc.dram_tensor("x1scr", [T_, D_], F32, kind="Internal").ap()
    H.shmx = [nc.dram_tensor(f"shmx{i}", [2 * 1152, D_], F32, kind="Internal",
                             addr_space="Shared").ap() for i in range(2)]
    H.shmf = nc.dram_tensor("shmf", [2 * 128, 2], mybir.dt.int32,
                            kind="Internal", addr_space="Shared").ap()
    H.fsb = st.enter_context(nc.sbuf_tensor("fsb", [1, 4], mybir.dt.int32))
    H.xstg = [st.enter_context(nc.sbuf_tensor(f"xstg{i}", [128, D_], BF16,
                                              side="right"))
              for i in range(2)]
    H.fsbt = st.enter_context(nc.sbuf_tensor("fsbt", [1, 16], mybir.dt.int32))
    H.fsbm = st.enter_context(nc.sbuf_tensor("fsbm", [1, 16], mybir.dt.int32))
    H.ixo = st.enter_context(nc.sbuf_tensor("ixo", [128, 19], mybir.dt.int32))
    H.epochv = st.enter_context(nc.sbuf_tensor("epochv", [128, 6],
                                               mybir.dt.int32))

    e_cst = [
        B.dma("y", gateg[:, :], H.gate, waits=H.p1bar, signal=ev("DY"), tag="c0"),
        B.dma("y", wsendg[:, :], H.wsend, waits=H.p1bar, signal=ev("DY"), tag="c1"),
        B.dma("y", ones1[0:1, :], H.ones128, waits=H.p1bar, signal=ev("DY"), tag="c2"),
        B.dma("y", ones1f[0:1, :], H.ones128f, waits=H.p1bar, signal=ev("DY"), tag="c2f"),
        B.dma("y", idb2[:, :], H.idbf, waits=H.p1bar, signal=ev("DY"), tag="c3"),
        B.dma("y", idf2[:, :], H.idf, waits=H.p1bar, signal=ev("DY"), tag="c4"),
        B.dma("y", mlo[:, :], H.msk_lo, waits=H.p1bar, signal=ev("DY"), tag="c5"),
        B.dma("y", mhi[:, :], H.msk_hi, waits=H.p1bar, signal=ev("DY"), tag="c6"),
        B.dma("y", mpo[:, :], H.prevoff, waits=H.p1bar, signal=ev("DY"), tag="c7"),
    ]
    H.e_tbl = [
        B.dma("y", H.fsbt[:, :], H.fthr, waits=H.p1bar, signal=ev("DY"),
              tag="cft"),
        B.dma("y", H.fsbm[:, :], H.fmask, waits=H.p1bar, signal=ev("DY"),
              tag="cfm"),
        B.dma("y", H.ixo[:, :], H.ixoff, waits=H.p1bar, signal=ev("DY"),
              tag="cxo"),
        B.dma("y", H.epochv[:, :], H.epochs, waits=H.p1bar, signal=ev("DY"),
              tag="cep"),
    ]
    e_cst += H.e_tbl
    ev_x1s = B.dma("y", x1scr.rearrange("(c p) d -> p c d", p=128),
                   _v3(H.x.ap(), D_), waits=[H.ev_x1_ln], signal=ev("DY"),
                   tag="x1save")

    def mkpoll(l, accept, wcol):
        def pollfn(e, l=l, accept=accept, wcol=wcol):
            sem = B.sems[f"PL{l}"]
            fh = H.shmf.rearrange("(s p) w -> p s w", p=128)[0:1, :, :]
            with ExitStack() as rs:
                rc = rs.enter_context(e.register(f"rcP{l}"))
                rt = rs.enter_context(e.register(f"rtP{l}"))
                rv = rs.enter_context(e.register(f"rvP{l}"))
                rw = rs.enter_context(e.register(f"rwP{l}"))
                e.reg_load(rw, H.epochv[0:1, wcol:wcol + 1])
                e.reg_mov(rc, 1)
                e.reg_mov(rt, 0)
                with e.While(rc):
                    e.reg_alu(rt, rt, 16, ALU.add)
                    e.dma_start(out=H.fsb.ap()
                                .rearrange("p (s w) -> p s w", w=2)[0:1, :, :],
                                in_=fh, allow_slow_non_contiguous=True
                                ).then_inc(sem, 16)
                    e.wait_ge(sem, rt)
                    e.reg_mov(rc, 0)
                    for k in accept:
                        e.reg_load(rv, H.fsb[0:1, k:k + 1])
                        e.reg_alu(rv, rv, rw, ALU.bitwise_xor)
                        e.reg_alu(rc, rc, rv, ALU.bitwise_or)
            return e.nop()
        return pollfn

    ev_coll_done = None
    for l in range(2):
        H.ev_xready = H.ev_x1_ln if l == 0 else H.ev_l_done
        wev = []
        for j, c in enumerate(range(7, 16)):
            wsrc = H.ev_ln_chunk.get(c) if l == 0 else H.ev_xready
            wev.append(B.add(
                "g", lambda e, j=j, c=c, l=l: e.indirect_dma_start(
                    H.shmx[l][:, :],
                    bass.IndirectOffsetOnAxis(ap=H.ixo[:, j:j + 1], axis=0),
                    H.x[:, 1024 * c:1024 * (c + 1)], None),
                waits=[wsrc] + H.e_tbl, signal=ev("DG"),
                tag=f"xw{l}{c}"))
        ev_fw = B.add(
            "g", lambda e, l=l: e.indirect_dma_start(
                H.shmf[:, :],
                bass.IndirectOffsetOnAxis(ap=H.ixo[:, 18:19], axis=0),
                H.epochv[:, 2 * l:2 * l + 2], None),
            waits=wev, signal=ev("DG"), tag=f"fw{l}")

        ev_poll = B.add("g", mkpoll(l, [0 + l, 2 + l], 2 * l),
                        waits=[ev_fw], signal=ev("G"), tag=f"poll{l}")
        gh = []
        g_stage = [ev_coll_done, ev_coll_done]
        for j in range(9):
            sb = xh8 if j == 8 else H.xstg[j % 2]
            rdev = B.add(
                "g", lambda e, j=j, l=l, sb=sb: e.indirect_dma_start(
                    sb[:, :], None,
                    H.shmx[l][:, :],
                    bass.IndirectOffsetOnAxis(ap=H.ixo[:, 9 + j:10 + j], axis=0)),
                waits=[ev_poll, g_stage[j % 2]], signal=ev("DG"),
                tag=f"xr{l}{j}")
            cst = B.add(
                "v", lambda e, j=j, sb=sb: e.tensor_scalar_mul(
                    xh[:, 1024 * j:1024 * (j + 1)], sb[:, :], gateg[:, 0:1]),
                waits=[rdev, e_cst[0], H.ev_xready], signal=ev("V"),
                tag=f"gx{l}{j}")
            g_stage[j % 2] = cst
            gh.append(cst)
        ev_h8g = B.add("v", lambda e: e.tensor_scalar_mul(
            xh8[:, :], xh8[:, :], gateg[:, 0:1]), waits=[gh[-1]],
            signal=ev("V"), tag=f"g8{l}")
        ev_halo = gh + [ev_h8g]
        ev_coll_done = ev_h8g

        guard = H.p1bar if l == 0 else [H.ev_l_done]
        _emit_layer(nc, B, st, H, l, xh, xh8, ones1, ones1f, idb2, idf2,
                    (mlo, mhi, mpo), ev_halo, guard)

    with ExitStack() as sf:
        xsc = [sf.enter_context(nc.sbuf_tensor(f"xsc{i}", [128, D_], F32))
               for i in range(2)]
        sqs = sf.enter_context(nc.sbuf_tensor("sqsF", [128, D_], F32))
        stt = sf.enter_context(nc.sbuf_tensor("sttF", [128, 16], F32))
        ev_out = None
        g_x = [None, None]
        for c in range(16):
            xc = H.x[:, 1024 * c:1024 * (c + 1)]
            lx = B.dma("y", xsc[c % 2][:, :],
                       x1scr.rearrange("(c p) d -> p c d", p=128)[:, c, :],
                       waits=[ev_x1s, H.ev_l_done, g_x[c % 2]],
                       signal=ev("DY"), tag=f"fx{c}")
            ra = B.add("v", lambda e, xc=xc, c=c: e.tensor_add(
                xc, xc, xsc[c % 2][:, :]), waits=[lx], signal=ev("V"),
                tag=f"fra{c}")
            g_x[c % 2] = ra
            B.add("v", lambda e, xc=xc: e.reduce_sum(stt[:, 0:1], xc, axis=AX.X),
                  waits=[ra], tag=f"fls{c}")
            ev_sq = B.add("s", lambda e, xc=xc: e.activation(
                sqs[:, :], xc, AF.Square, accum_out=stt[:, 1:2]),
                waits=[ra], signal=ev("S"), tag=f"flq{c}")
            B.add("v", lambda e: e.tensor_scalar_mul(stt[:, 2:3], stt[:, 0:1],
                                                     -1.0 / D_), tag="f1")
            B.add("v", lambda e: e.tensor_scalar_mul(stt[:, 3:4], stt[:, 1:2],
                                                     1.0 / D_),
                  waits=[ev_sq], tag="f2")
            B.add("v", lambda e: e.tensor_mul(stt[:, 4:5], stt[:, 2:3],
                                              stt[:, 2:3]), tag="f3")
            ev_var = B.add("v", lambda e: e.scalar_tensor_tensor(
                stt[:, 5:6], stt[:, 3:4], LNEPS, stt[:, 4:5],
                ALU.add, ALU.subtract), signal=ev("V"), tag="f4")
            ev_sd = B.add("s", lambda e: e.activation(stt[:, 6:7], stt[:, 5:6],
                                                      AF.Sqrt),
                          waits=[ev_var], signal=ev("S"), tag="f5")
            B.add("v", lambda e: e.reciprocal(stt[:, 7:8], stt[:, 6:7]),
                  waits=[ev_sd], tag="f6")
            ev_ln = B.add("v", lambda e, xc=xc: e.tensor_scalar(
                xc, xc, stt[:, 2:3], stt[:, 7:8], ALU.add, ALU.mult),
                signal=ev("V"), tag=f"flap{c}")
            ev_out = B.dma(
                "y", H.out.rearrange("(c p) d -> p c d", p=128)[:, c, :],
                xc, waits=[ev_ln], signal=ev("DY"), tag=f"fout{c}")
        H.ev_out = ev_out


def _emit_layer(nc, B, st, H, l, xh, xh8, ones1, ones1f, idb2, idf2, masks, ev_halo,
                guard):
    ev = B.ev
    mlo, mhi, mpo = masks
    NKC, NQC, NIC = 24, 20, 25

    with ExitStack() as sl:
        qbf = sl.enter_context(nc.sbuf_tensor(f"qbf{l}", [128, NQC * 512], BF16))
        kbf = sl.enter_context(nc.sbuf_tensor(f"kbf{l}", [128, NKC * 512], BF16))
        vly = sl.enter_context(nc.sbuf_tensor(f"vly{l}", [128, NKC * 520], BF16))
        ev_vones = B.add("v", lambda e: e.memset(vly[:, :], 1.0),
                         signal=ev("V"), tag=f"von{l}")

        qsb = sl.enter_context(nc.sbuf_tensor(f"qsb{l}", [128, 512], BF16))
        ksb = sl.enter_context(nc.sbuf_tensor(f"ksb{l}", [128, 512], BF16))
        qrow = sl.enter_context(nc.sbuf_tensor(f"qrow{l}", [1, 512], F32))
        with nc.psum_tensor(f"pbc{l}", [128, 512], F32) as PB:
            prev = None
            for dst, srcr in ((qsb, H.qsw), (ksb, H.ksw)):
                e_qr = B.dma("y", qrow[0:1, :], srcr[l:l + 1, :],
                             waits=[prev] + list(guard), signal=ev("DY"),
                             tag=f"qr{l}")
                mm = B.add("t", lambda e: e.matmul(
                    PB[:, :], ones1f[0:1, :], qrow[0:1, :],
                    start=True, stop=True),
                    waits=[e_qr, prev], signal=ev("T"), tag=f"bc{l}")
                prev = B.add("v", lambda e, dst=dst: e.tensor_copy(dst[:, :],
                                                                   PB[:, :]),
                             waits=[mm], signal=ev("V"), tag=f"bce{l}")
            ev_bc = prev

        with ExitStack() as sA:
            wqb = sA.enter_context(nc.sbuf_tensor(f"wqb{l}", [128, 8 * 1536], BF16))
            xTt = [sA.enter_context(nc.sbuf_tensor(f"xT{l}{i}", [128, 8 * 129], BF16))
                   for i in range(2)]
            xbs = sA.enter_context(nc.sbuf_tensor(f"xbs{l}", [128, D_], BF16))
            xT9 = sA.enter_context(nc.sbuf_tensor(f"xT9{l}", [128, 8 * 129], BF16))
            xtw = [sA.enter_context(nc.sbuf_tensor(f"xtw{l}0", [128, 1024], BF16))] * 2
            qraw = sA.enter_context(nc.sbuf_tensor(f"qraw{l}", [128, 1024], F32))
            zq = sA.enter_context(nc.sbuf_tensor(f"zq{l}", [128, 512], BF16))
            t1p = qraw
            t2p = sA.enter_context(nc.sbuf_tensor(f"t2p{l}", [128, 512], BF16))
            nrm = sA.enter_context(nc.sbuf_tensor(f"nrm{l}", [128, 64], F32))
            sqs = sA.enter_context(nc.sbuf_tensor(f"sqs{l}", [128, 1024], BF16))
            PT2 = sA.enter_context(nc.psum_tensor(f"PT2{l}", [128, 2048], BF16))
            PP2 = sA.enter_context(nc.psum_tensor(f"PP2{l}", [128, 1536], F32))

            ev_w = B.dma("g", wqb.ap().rearrange("p (m o) -> p m o", o=1536),
                         H.wqkv[l].rearrange("(m p) o -> p m o", p=128),
                         waits=guard, signal=ev("DG"), tag=f"wq{l}")
            g_PT = [ev_bc, ev_bc]
            g_xT = [None, None]
            g_blk = {0: guard[0], 1: guard[0], 2: guard[0]}
            g_chain = guard[0]
            g_chain2 = {0: guard[0], 1: guard[0]}
            g_srd = {}
            chain_par = 0
            ev_q_done = ev_k_done = ev_v_done = None
            tile9_ev = None
            prev_tile = {}

            def emit_tile(i, buf, srcw, xsrc, gxt, seed):
                lastT = None
                for half in range(2):
                    par = half
                    for mm_ in range(4):
                        m = 4 * half + mm_
                        lastT = B.add(
                            "t", lambda e, m=m, mm_=mm_, par=par, xsrc=xsrc:
                            e.transpose(
                                PT2[:, 1024 * par + 128 * mm_:1024 * par + 128 * (mm_ + 1)],
                                xsrc[:, 128 * m:128 * (m + 1)],
                                idb2[:, :]),
                            waits=(srcw + [g_PT[par], g_srd.get(id(buf))])
                            if mm_ == 0 else (),
                            signal=ev("T") if mm_ == 3 else None,
                            tag=f"xT{l}{i}")
                    if par == 0:
                        g_PT[par] = B.add(
                            "v", lambda e, half=half, par=par, buf=buf:
                            e.tensor_copy(
                                buf.ap().rearrange("p (m c) -> p m c", c=129)
                                [:, 4 * half:4 * half + 4, 1:129],
                                PT2.ap().rearrange("p (m c) -> p m c", c=128)
                                [:, 8 * par:8 * par + 4, :]),
                            waits=[lastT, gxt], signal=ev("V"), tag=f"xTe{l}{i}")
                    else:
                        g_PT[par] = B.add(
                            "s", lambda e, half=half, par=par, buf=buf:
                            e.activation(
                                buf.ap().rearrange("p (m c) -> p m c", c=129)
                                [:, 4 * half:4 * half + 4, 1:129],
                                PT2.ap().rearrange("p (m c) -> p m c", c=128)
                                [:, 8 * par:8 * par + 4, :], AF.Copy),
                            waits=[lastT, gxt], signal=ev("S"), tag=f"xTe{l}{i}")
                evt = g_PT[1]
                if seed is not None:
                    sbuf, sev = seed
                    evt = B.add("v", lambda e, buf=buf, sbuf=sbuf: e.tensor_copy(
                        buf.ap().rearrange("p (m c) -> p m c", c=129)[:, :, 0],
                        sbuf.ap().rearrange("p (m c) -> p m c", c=129)[:, :, 128]),
                        waits=[evt, sev], signal=ev("V"), tag=f"seed{l}{i}")
                    g_srd[id(sbuf)] = evt
                return evt

            wln0 = (getattr(H, "ev_ln_chunk", {}).get(0, H.ev_xready)
                    if l == 0 else H.ev_xready)
            e_src9 = B.add("s", lambda e: e.activation(
                xbs[:, :], H.x[:, 0:1024], AF.Copy), waits=[wln0],
                signal=ev("S"), tag=f"xc{l}9")
            tile9_ev = emit_tile(9, xT9, [e_src9], xbs[:, :], None, None)
            prev_tile[9] = (xT9, tile9_ev)

            order = list(range(10, NIC)) + list(range(0, 9)) + [9]

            def emit_chains(i, buf, ev_tr, ip):
                nonlocal ev_q_done, ev_k_done, ev_v_done, chain_par, g_chain
                kc = i - 1
                qc = i - 5
                blks = ([0] if qc >= 0 else []) + [1, 2]
                ev_mm = {}
                for blk in blks:
                    lastm = None
                    for m in range(8):
                        lastm = B.add(
                            "t", lambda e, m=m, blk=blk, buf=buf: e.matmul(
                                PP2[:, 512 * blk:512 * (blk + 1)],
                                buf[:, 129 * m + (1 if m < 4 else 0):
                                    129 * m + (1 if m < 4 else 0) + 128],
                                wqb[:, 1536 * m + 512 * blk:1536 * m + 512 * (blk + 1)],
                                start=(m == 0), stop=(m == 7)),
                            waits=[ev_tr, ev_w, g_blk[blk]] if m == 0 else (),
                            signal=ev("T") if m == 7 else None, tag=f"qkv{l}{i}")
                    ev_mm[blk] = lastm
                g_xT[ip] = ev_mm[blks[-1]]
                xw = xtw[0]
                for which, blk, dstb, dc, sb, tabof in (
                        ("v", 2, None, kc, None, 0),
                        ("k", 1, kbf, kc, ksb, 1024),
                        ("q", 0, qbf, qc, qsb, 0)):
                    if which == "q" and qc < 0:
                        continue
                    psl = PP2[:, 512 * blk:512 * (blk + 1)]
                    if which == "v":
                        ev_v_done = B.add(
                            "s", lambda e, psl=psl, dc=dc: e.activation(
                                vly.ap().rearrange("p (c e) -> p c e", e=65)
                                [:, 8 * dc:8 * (dc + 1), 0:64],
                                psl.rearrange("p (c e) -> p c e", e=64),
                                AF.Copy),
                            waits=[ev_mm[2], ev_vones], signal=ev("S"),
                            tag=f"vev{l}{i}")
                        g_blk[2] = ev_v_done
                        continue
                    par = chain_par
                    chain_par ^= 1
                    qrw = qraw[:, 512 * par:512 * (par + 1)]
                    t1s = t1p[:, 512 * par:512 * (par + 1)]
                    sqsl = sqs[:, 512 * par:512 * (par + 1)]
                    nr = nrm[:, 32 * par:32 * (par + 1)]
                    ev_evc = B.add("s", lambda e, psl=psl, qrw=qrw: e.activation(
                        qrw, psl, AF.Copy),
                        waits=[ev_mm[blk], g_chain2[par]], signal=ev("S"),
                        tag=f"ev{l}{i}{which}")
                    g_blk[blk] = ev_evc
                    B.add("v", lambda e, sb=sb, qrw=qrw: e.tensor_mul(
                        zq[:, :], qrw, sb[:, :]),
                        waits=[ev_bc, ev_evc], tag=f"z{l}{i}{which}")
                    ev_sq = B.add("s", lambda e, qrw=qrw, sqsl=sqsl: e.activation(
                        sqsl, qrw, AF.Square),
                        waits=[ev_evc], signal=ev("S"), tag=f"ss{l}{i}{which}")
                    ev_red = B.add("v", lambda e, sqsl=sqsl, nr=nr: e.reduce_sum(
                        nr[:, 0:8],
                        sqsl.rearrange("p (h d) -> p h d", d=64), axis=AX.X),
                        waits=[ev_sq], signal=ev("V"), tag=f"rd{l}{i}{which}")
                    ev_nr2 = B.add("s", lambda e, nr=nr: e.activation(
                        nr[:, 8:16], nr[:, 0:8], AF.Sqrt),
                        waits=[ev_red], signal=ev("S"), tag=f"sq{l}{i}")
                    B.add("v", lambda e, nr=nr: e.tensor_scalar_max(
                        nr[:, 16:24], nr[:, 8:16], 1e-12), waits=[ev_nr2],
                        tag=f"cl{l}{i}")
                    B.add("v", lambda e, nr=nr: e.reciprocal(
                        nr[:, 24:32], nr[:, 16:24]),
                          tag=f"rc{l}{i}")
                    e_xt = B.dma(
                        "y", xw.ap().rearrange("p (f d) -> p f d", d=512),
                        H.xtab[128 * i:128 * (i + 1), tabof:tabof + 1024]
                        .rearrange("p (f d) -> p f d", d=512),
                        waits=[g_chain], signal=ev("DY"), tag=f"xt{l}{i}{which}")
                    co = xw[:, 0:512]
                    si = xw[:, 512:1024]
                    zv = zq.ap().rearrange("p (h s d) -> p h s d", s=2, d=32)
                    sv_ = si.rearrange("p (h s d) -> p h s d", s=2, d=32)
                    t2v = t2p.ap().rearrange("p (h s d) -> p h s d", s=2, d=32)
                    B.add("v", lambda e, co=co, t1s=t1s: e.tensor_mul(
                        t1s, zq[:, :], co),
                          waits=[e_xt, ev_nr2], tag=f"r1{l}{i}")
                    B.add("v", lambda e, sv_=sv_, zv=zv: e.tensor_mul(
                        t2v[:, :, 0, :], zv[:, :, 1, :], sv_[:, :, 0, :]),
                        tag=f"r2{l}{i}")
                    B.add("v", lambda e, sv_=sv_, zv=zv: e.tensor_mul(
                        t2v[:, :, 1, :], zv[:, :, 0, :], sv_[:, :, 1, :]),
                        tag=f"r3{l}{i}")
                    ev_r4 = B.add("v", lambda e, t1s=t1s: e.tensor_add(
                        t1s, t1s, t2p[:, :]),
                        signal=ev("V"), tag=f"r4{l}{i}")
                    lastn = None
                    for hh in range(8):
                        lastn = B.add(
                            "v", lambda e, hh=hh, dstb=dstb, dc=dc, t1s=t1s,
                            nr=nr: e.tensor_single_scalar(
                                dstb[:, 512 * dc + 64 * hh:512 * dc + 64 * (hh + 1)],
                                t1s[:, 64 * hh:64 * (hh + 1)],
                                nr[:, 24 + hh:25 + hh], ALU.mult),
                            waits=[ev_r4] if hh == 0 else (),
                            signal=ev("V") if hh == 7 else None,
                            tag=f"rs{l}{i}{which}")
                    if which == "k":
                        ev_k_done = lastn
                    else:
                        ev_q_done = lastn
                    g_chain = lastn
                    g_chain2[par] = lastn

            pending = None
            for idx, i in enumerate(order):
                ip = idx % 2
                buf = xTt[ip]
                if i == 9:
                    buf = xT9
                    ev_tr = B.add("v", lambda e: e.tensor_copy(
                        xT9.ap().rearrange("p (m c) -> p m c", c=129)[:, :, 0],
                        prev_tile[8][0].ap()
                        .rearrange("p (m c) -> p m c", c=129)[:, :, 128]),
                        waits=[tile9_ev, prev_tile[8][1]], signal=ev("V"),
                        tag=f"seed{l}9f")
                    g_srd[id(prev_tile[8][0])] = ev_tr
                else:
                    if i <= 8:
                        xsrc = xh[:, 1024 * i:1024 * (i + 1)]
                        srcw = [ev_halo[i]]
                    else:
                        wln = (getattr(H, "ev_ln_chunk", {}).get(i - 9, H.ev_xready)
                               if l == 0 else H.ev_xready)
                        xbw = [wln, tile9_ev] if idx == 0 else [wln]
                        e_src = B.add("s", lambda e, i=i: e.activation(
                            xbs[:, :], H.x[:, 1024 * (i - 9):1024 * (i - 8)],
                            AF.Copy), waits=xbw, signal=ev("S"),
                            tag=f"xc{l}{i}")
                        xsrc = xbs[:, :]
                        srcw = [e_src]
                    seed = prev_tile.get(i - 1)
                    ev_tr = emit_tile(i, buf, srcw, xsrc, g_xT[ip], seed)
                    prev_tile[i] = (buf, ev_tr)
                if pending is not None:
                    emit_chains(*pending)
                pending = None if i == 0 else (i, buf, ev_tr, ip)
            emit_chains(*pending)
            ev_stageA = [ev_q_done, ev_k_done, ev_v_done]
        obf = sl.enter_context(nc.sbuf_tensor(f"obf{l}", [128, NQC * 512], BF16))
        with ExitStack() as sC:
            qt2 = [sC.enter_context(nc.sbuf_tensor(f"qt2{l}0", [65, NQC * 128], BF16))] * 2
            kt2 = [sC.enter_context(nc.sbuf_tensor(f"kt2{l}0", [65, NKC * 128], BF16))] * 2
            qab = sC.enter_context(nc.sbuf_tensor(f"qab{l}", [64, 2560], BF16))
            kvx = sC.enter_context(nc.sbuf_tensor(f"kvx{l}", [64, 8], BF16))
            pw = [sC.enter_context(nc.sbuf_tensor(f"pw{l}{i}", [128, 640], BF16))
                  for i in range(2)]
            svc2 = sC.enter_context(nc.sbuf_tensor(f"svc2{l}", [128, 512], F32))
            rs2 = sC.enter_context(nc.sbuf_tensor(f"rs2{l}", [128, 16], F32))
            PT2b = sC.enter_context(nc.psum_tensor(f"PT2b{l}", [128, 2048], BF16))
            SP2 = sC.enter_context(nc.psum_tensor(f"SP2{l}", [128, 640], F32))
            SVq = sC.enter_context(nc.psum_tensor(f"SVq{l}", [65, 512], F32))
            EP2 = sC.enter_context(nc.psum_tensor(f"EP2{l}", [128, 512], F32))
            Cp = sC.enter_context(nc.psum_tensor(f"Cp{l}", [65, 512], F32))

            ev_k1 = B.add("v", lambda e: e.memset(kt2[0][64:65, :], 1.0),
                          waits=list(ev_stageA), signal=ev("V"), tag=f"k1a{l}")
            ev_k1b = ev_k1
            ev_svz2 = B.add("v", lambda e: e.memset(svc2[:, :], 0.0),
                            waits=list(ev_stageA), signal=ev("V"),
                            tag=f"svz{l}")
            g_PT = [ev_stageA[0], ev_stageA[1]]
            g_qt = [ev_k1, ev_k1b]
            g_pw = [[None, None], [None, None]]
            g_sp = None
            g_sp1 = None
            g_sv = ev_svz2
            g_ep = None
            g_cm = {w: None for w in range(5)}
            g_cw = None
            ev_obf = None
            for h in range(8):
                hb = h % 2
                qth, kth = qt2[hb], kt2[hb]
                ev_bq = ev_bk = None
                for (src, nch, dst) in ((qbf, NQC, qth), (kbf, NKC, kth)):
                    gi = 0
                    for c0 in range(0, nch, 4):
                        par = gi % 2
                        gi += 1
                        nn = min(4, nch - c0)
                        lastT = None
                        for j in range(nn):
                            lastT = B.add(
                                "t", lambda e, c0=c0, j=j, par=par, src=src, h=h:
                                e.transpose(
                                    PT2b[0:64, 1024 * par + 128 * j:1024 * par + 128 * (j + 1)],
                                    src[:, 512 * (c0 + j) + 64 * h:512 * (c0 + j) + 64 * (h + 1)],
                                    idb2[:, :]),
                                waits=[g_PT[par], g_qt[hb], ev_stageA[0],
                                       ev_stageA[1]] if j == 0 else (),
                                signal=ev("T") if j == nn - 1 else None,
                                tag=f"bT{l}{h}")
                        if par == 0:
                            g_PT[par] = B.add(
                                "v", lambda e, c0=c0, nn=nn, par=par, dst=dst:
                                e.tensor_copy(
                                    dst[0:64, 128 * c0:128 * (c0 + nn)],
                                    PT2b[0:64, 1024 * par:1024 * par + 128 * nn]),
                                waits=[lastT], signal=ev("V"), tag=f"bTe{l}{h}")
                        else:
                            g_PT[par] = B.add(
                                "s", lambda e, c0=c0, nn=nn, par=par, dst=dst:
                                e.activation(
                                    dst[0:64, 128 * c0:128 * (c0 + nn)],
                                    PT2b[0:64, 1024 * par:1024 * par + 128 * nn],
                                    AF.Copy),
                                waits=[lastT], signal=ev("S"), tag=f"bTe{l}{h}")
                    if src is qbf:
                        ev_bq = [g_PT[0], g_PT[1]]
                    else:
                        ev_bk = [g_PT[0], g_PT[1]]
                ev_cw = {}
                for w in range(5):
                    ev_kv = B.add("v", lambda e, kth=kth, w=w: e.reduce_max(
                        kvx[0:64, w:w + 1], kth[0:64, 512 * w:512 * w + 1024],
                        axis=AX.X, apply_absolute_value=True),
                        waits=ev_bk + [g_cm[w]], signal=ev("V"),
                        tag=f"kv{l}{h}{w}")
                    ev_qa = B.add("s", lambda e, qth=qth, w=w: e.activation(
                        qab[0:64, 512 * w:512 * (w + 1)],
                        qth[0:64, 512 * w:512 * (w + 1)], AF.Abs),
                        waits=ev_bq + [g_cm[w]], signal=ev("S"),
                        tag=f"qa{l}{h}{w}")
                    ev_cm = B.add("t", lambda e, w=w: e.matmul(
                        Cp[64:65, :], kvx[0:64, w:w + 1],
                        qab[0:64, 512 * w:512 * (w + 1)],
                        start=True, stop=True, tile_position=(0, 64)),
                        waits=[ev_kv, ev_qa, g_cw], signal=ev("T"),
                        tag=f"cm{l}{h}{w}")
                    g_cm[w] = ev_cm
                    ev_cw[w] = g_cw = B.add(
                        "s", lambda e, qth=qth, w=w: e.activation(
                            qth[64:65, 512 * w:512 * (w + 1)], Cp[64:65, :],
                            AF.Copy, scale=-1.0),
                        waits=[ev_cm], signal=ev("S"), tag=f"cw{l}{h}{w}")
                for w in range(5):
                    for b in range(4):
                        bp = b % 2
                        lastm = None
                        ev_s3 = None
                        for a in range(b, b + 5):
                            sslot = a - b
                            mk = []
                            if a == b:
                                mk.append(mlo)
                            if a == b + 4:
                                mk.append(mhi)
                            if w == 1 and a <= 3:
                                mk.append(mpo)
                            w_st = ()
                            if a == b:
                                w_st = [ev_cw[w], g_sp, g_pw[bp][0]]
                            elif a == b + 4:
                                w_st = [g_sp1, g_pw[bp][1]]
                            sig_a = a in (b + 3, b + 4)
                            lastm = B.add(
                                "t", lambda e, w=w, a=a, b=b, sslot=sslot,
                                kth=kth, qth=qth, stp=(not mk): e.matmul(
                                    SP2[:, 128 * sslot:128 * (sslot + 1)],
                                    kth[0:65, 128 * (4 * w + a):128 * (4 * w + a + 1)],
                                    qth[0:65, 128 * (4 * w + b):128 * (4 * w + b + 1)],
                                    start=True, stop=stp),
                                waits=w_st,
                                signal=ev("T") if (sig_a and not mk) else None,
                                tag=f"st{l}{h}{w}")
                            for q_, mt in enumerate(mk):
                                fin = (sig_a and q_ == len(mk) - 1)
                                lastm = B.add(
                                    "t", lambda e, sslot=sslot, mt=mt,
                                    last=(q_ == len(mk) - 1): e.matmul(
                                        SP2[:, 128 * sslot:128 * (sslot + 1)],
                                        mt[:, :], idb2[:, :],
                                        start=False, stop=last),
                                    signal=ev("T") if fin else None,
                                    tag=f"mk{l}{h}{w}")
                            if a == b + 3:
                                ev_s3 = lastm
                        ev_sm = lastm
                        ev_ex = B.add("s", lambda e, bp=bp: e.activation(
                            pw[bp][:, 0:512], SP2[:, 0:512], AF.Exp, scale=8.0),
                            waits=[ev_s3, g_pw[bp][0]], signal=ev("S"),
                            tag=f"xx{l}{h}{w}")
                        g_sp = ev_ex
                        ev_ex1 = B.add("s", lambda e, bp=bp: e.activation(
                            pw[bp][:, 512:640], SP2[:, 512:640], AF.Exp,
                            scale=8.0),
                            waits=[ev_sm, g_pw[bp][1]], signal=ev("S"),
                            tag=f"xy{l}{h}{w}")
                        g_sp1 = ev_ex1
                        lastv = None
                        sv3 = None
                        for a in range(b, b + 5):
                            sslot = a - b
                            svw = ()
                            if a == b:
                                svw = [ev_ex, g_sv]
                            elif a == b + 4:
                                svw = [ev_ex1]
                            lastv = B.add(
                                "t", lambda e, w=w, a=a, b=b, sslot=sslot, h=h, bp=bp:
                                e.matmul(
                                    SVq[0:65, 128 * b:128 * (b + 1)],
                                    vly[:, 520 * (4 * w + a) + 65 * h:
                                        520 * (4 * w + a) + 65 * (h + 1)],
                                    pw[bp][:, 128 * sslot:128 * (sslot + 1)],
                                    start=(a == b), stop=(a == b + 4)),
                                waits=svw,
                                signal=ev("T") if a in (b + 3, b + 4) else None,
                                tag=f"sv2{l}{h}{w}")
                            if a == b + 3:
                                sv3 = lastv
                        last_sv = lastv
                        g_pw[bp] = [sv3, lastv]
                    ecp = B.add("v", lambda e: e.tensor_copy(
                        svc2[0:65, :], SVq[0:65, :]),
                        waits=[last_sv], signal=ev("V"), tag=f"sve{l}{h}{w}")
                    g_sv = ecp
                    lastT = None
                    for j in range(4):
                        lastT = B.add(
                            "t", lambda e, j=j: e.transpose(
                                EP2[:, 128 * j:128 * (j + 1)],
                                svc2[:, 128 * j:128 * (j + 1)], idf2[:, :]),
                            waits=[ecp, g_ep] if j == 0 else (),
                            signal=ev("T") if j == 3 else None, tag=f"eT2{l}{h}{w}")
                    EPr = EP2.ap().rearrange("p (j x) -> p j x", x=128)
                    B.add("v", lambda e, EPr=EPr: e.tensor_copy(
                        rs2[:, 0:4], EPr[:, :, 64]), waits=[lastT],
                        tag=f"er{l}{h}{w}")
                    B.add("v", lambda e: e.reciprocal(rs2[:, 4:8], rs2[:, 0:4]),
                          tag=f"er2{l}{h}{w}")
                    evo = None
                    for j in range(4):
                        qc = 4 * w + j
                        evo = B.add(
                            "v", lambda e, j=j, qc=qc, h=h: e.tensor_single_scalar(
                                obf[:, 512 * qc + 64 * h:512 * qc + 64 * (h + 1)],
                                EP2[:, 128 * j:128 * j + 64],
                                rs2[:, 4 + j:5 + j], ALU.mult),
                            signal=ev("V") if j == 3 else None, tag=f"ob{l}{h}{w}")
                    g_ep = evo
                    ev_obf = evo
                g_qt[hb] = ev_sm
        with ExitStack() as sD:
            oTt = [sD.enter_context(nc.sbuf_tensor(f"oT{l}{i}", [128, 512], BF16))
                   for i in range(2)]
            wob = sD.enter_context(nc.sbuf_tensor(f"wob{l}", [128, 4 * 1024], BF16))
            PT2c = sD.enter_context(nc.psum_tensor(f"PT2c{l}", [128, 2048], BF16))
            PW = [sD.enter_context(nc.psum_tensor(f"PWD{l}{i}", [128, 512], F32))
                  for i in range(2)]
            ev_wo = B.dma("g", wob.ap().rearrange("p (m o) -> p m o", o=1024),
                          H.wo[l].rearrange("(m p) o -> p m o", p=128),
                          waits=[ev_obf], signal=ev("DG"), tag=f"wo{l}")
            g_PT = [ev_obf, ev_obf]
            g_oT = [None, None]
            g_pw2 = [None, None]
            ev_xmid = None
            H.ev_xmid_chunk = {}
            for qc in range(3, NQC):
                par = qc % 2
                buf = oTt[par]
                lastT = None
                for j in range(4):
                    lastT = B.add(
                        "t", lambda e, qc=qc, j=j, par=par: e.transpose(
                            PT2c[:, 1024 * par + 128 * j:1024 * par + 128 * (j + 1)],
                            obf[:, 512 * qc + 128 * j:512 * qc + 128 * (j + 1)],
                            idb2[:, :]),
                        waits=[ev_obf, g_PT[par], g_oT[par]] if j == 0 else (),
                        signal=ev("T") if j == 3 else None, tag=f"oT{l}{qc}")
                g_PT[par] = B.add(
                    "v", lambda e, par=par, buf=buf: e.tensor_copy(
                        buf[:, :], PT2c[:, 1024 * par:1024 * par + 512]),
                    waits=[lastT], signal=ev("V"), tag=f"oTe{l}{qc}")
                ev_mm = []
                for eb in range(2):
                    lastm = None
                    for ic in range(4):
                        lastm = B.add(
                            "t", lambda e, eb=eb, ic=ic, buf=buf: e.matmul(
                                PW[eb][:, :], buf[:, 128 * ic:128 * (ic + 1)],
                                wob[:, 1024 * ic + 512 * eb:1024 * ic + 512 * (eb + 1)],
                                start=(ic == 0), stop=(ic == 3)),
                            waits=[g_PT[par], ev_wo, g_pw2[eb]] if ic == 0 else (),
                            signal=ev("T") if ic == 3 else None, tag=f"wom{l}{qc}")
                    ev_mm.append(lastm)
                g_oT[par] = ev_mm[-1]
                if qc == 3:
                    dst = xh8
                    dof = 0
                else:
                    dst = H.x
                    dof = 1024 * (qc - 4)
                for eb in range(2):
                    ww = [ev_mm[eb]]
                    if qc == 3:
                        ww.append(ev_halo[-1])
                    ev_xmid = B.add(
                        "v", lambda e, dst=dst, dof=dof, eb=eb: e.tensor_add(
                            dst[:, dof + 512 * eb:dof + 512 * (eb + 1)],
                            dst[:, dof + 512 * eb:dof + 512 * (eb + 1)],
                            PW[eb][:, :]),
                        waits=ww, signal=ev("V"), tag=f"xm{l}{qc}")
                    g_pw2[eb] = ev_xmid
                H.ev_xmid_chunk[qc] = ev_xmid
            H.ev_xmid = ev_xmid
    _emit_ffn(nc, B, st, H, l, xh8, ones1, idb2)


def _emit_ffn(nc, B, st, H, l, xh8, ones1, idb2):
    ev = B.ev
    NB2 = 22
    with ExitStack() as sE:
        h2bf = sE.enter_context(nc.sbuf_tensor(f"h2bf{l}", [128, 17 * D_], BF16))
        h2T = sE.enter_context(nc.sbuf_tensor(f"h2T{l}", [128, 8 * 2049], BF16))
        gsb = sE.enter_context(nc.sbuf_tensor(f"gsb{l}", [128, NB2 * 512], BF16))
        NW = 6
        w1s = [sE.enter_context(nc.sbuf_tensor(f"w1s{l}{i}", [128, 1024], BF16))
               for i in range(NW)]
        w2s = [sE.enter_context(nc.sbuf_tensor(f"w2s{l}{i}", [128, 512], BF16))
               for i in range(NW)]
        st2 = sE.enter_context(nc.sbuf_tensor(f"st2{l}", [128, 4 * 18], F32))
        sh2 = sE.enter_context(nc.sbuf_tensor(f"sh2{l}", [128, 2 * 18], F32))
        stm = sE.enter_context(nc.sbuf_tensor(f"stm{l}", [128, 8 * 18], F32))
        b1pp = sE.enter_context(nc.sbuf_tensor(f"b1pp{l}", [128, 43], F32))
        lgb = sE.enter_context(nc.sbuf_tensor(f"lgb{l}", [128, 16], F32))
        gel = sE.enter_context(nc.sbuf_tensor(f"gel{l}", [64, 512], F32))
        b2r = sE.enter_context(nc.sbuf_tensor(f"b2r{l}", [1, 1024], BF16))
        sqs2 = sE.enter_context(nc.sbuf_tensor(f"sqs2{l}", [128, 512], F32))
        PT2d = sE.enter_context(nc.psum_tensor(f"PT2d{l}", [128, 2048], BF16))
        PW1 = sE.enter_context(nc.psum_tensor(f"PW1{l}", [128, 1024], F32))
        PW2 = sE.enter_context(nc.psum_tensor(f"PW2{l}", [128, 2048], F32))

        e_b1 = B.dma("y", b1pp[:, :],
                     H.b1P[l].rearrange("(k p) -> p k", p=128),
                     waits=[H.ev_xmid], signal=ev("DY"), tag=f"b1{l}",
                     allow_slow_non_contiguous=True)
        e_lg = B.dma("y", lgb[:, 0:8],
                     H.lng[l].rearrange("(m p) -> p m", p=128),
                     waits=[H.ev_xmid], signal=ev("DY"), tag=f"lg{l}",
                     allow_slow_non_contiguous=True)
        e_lb = B.dma("y", lgb[:, 8:16],
                     H.lnb[l].rearrange("(m p) -> p m", p=128),
                     waits=[H.ev_xmid], signal=ev("DY"), tag=f"lb{l}",
                     allow_slow_non_contiguous=True)
        e_b2 = B.dma("g", b2r[0:1, :], H.b2[l:l + 1, :],
                     waits=[H.ev_xmid], signal=ev("DG"), tag=f"b2{l}")

        ev_stz = B.add("v", lambda e: e.memset(st2[:, :], 0.0),
                       signal=ev("V"), tag=f"stz{l}")
        ev_st = None
        for c in range(17):
            src = xh8[:, :] if c == 0 else H.x[:, 1024 * (c - 1):1024 * c]
            wmc = H.ev_xmid_chunk.get(c + 3, H.ev_xmid)
            B.add("v", lambda e, src=src, c=c: e.reduce_sum(
                st2[:, c:c + 1], src[:, 0:512], axis=AX.X),
                waits=[wmc], tag=f"s1a{l}{c}")
            B.add("v", lambda e, src=src, c=c: e.reduce_sum(
                st2[:, 18 + c:19 + c], src[:, 512:1024], axis=AX.X),
                tag=f"s1b{l}{c}")
            B.add("s", lambda e, src=src, c=c: e.activation(
                sqs2[:, :], src[:, 0:512], AF.Square,
                accum_out=st2[:, 36 + c:37 + c]),
                waits=[wmc, ev_stz], tag=f"q1a{l}{c}")
            ev_st = B.add("s", lambda e, src=src, c=c: e.activation(
                sqs2[:, :], src[:, 512:1024], AF.Square,
                accum_out=st2[:, 54 + c:55 + c]),
                signal=ev("S") if c == 16 else None, tag=f"q1b{l}{c}")
        ev_z0 = B.add("v", lambda e: e.memset(sh2[:, :], 0.0),
                      signal=ev("V"), tag=f"shz{l}")
        d1 = B.dma("y", sh2[1:128, 0:18], st2[0:127, 18:36],
                   waits=[ev_st, ev_z0], signal=ev("DY"), tag=f"shS{l}")
        d2 = B.dma("y", sh2[0:1, 1:18], st2[127:128, 18:35],
                   waits=[ev_st, ev_z0], signal=ev("DY"), tag=f"shSw{l}")
        d3 = B.dma("y", sh2[1:128, 18:36], st2[0:127, 54:72],
                   waits=[ev_st, ev_z0], signal=ev("DY"), tag=f"shQ{l}")
        d4 = B.dma("y", sh2[0:1, 19:36], st2[127:128, 54:71],
                   waits=[ev_st, ev_z0], signal=ev("DY"), tag=f"shQw{l}")
        B.add("v", lambda e: e.tensor_add(stm[:, 0:18], sh2[:, 0:18],
                                          st2[:, 0:18]),
              waits=[d1, d2], tag=f"cS{l}")
        B.add("v", lambda e: e.tensor_add(stm[:, 18:36], sh2[:, 18:36],
                                          st2[:, 36:54]),
              waits=[d3, d4], tag=f"cQ{l}")
        B.add("v", lambda e: e.tensor_scalar_mul(stm[:, 36:54], stm[:, 0:18],
                                                 -1.0 / D_), tag=f"mu{l}")
        B.add("v", lambda e: e.tensor_scalar_mul(stm[:, 0:18], stm[:, 18:36],
                                                 1.0 / D_), tag=f"e2{l}")
        B.add("v", lambda e: e.tensor_mul(stm[:, 18:36], stm[:, 36:54],
                                          stm[:, 36:54]), tag=f"m2{l}")
        ev_var = B.add("v", lambda e: e.scalar_tensor_tensor(
            stm[:, 0:18], stm[:, 0:18], LNEPS, stm[:, 18:36],
            ALU.add, ALU.subtract), signal=ev("V"), tag=f"vr{l}")
        ev_sd = B.add("s", lambda e: e.activation(stm[:, 18:36], stm[:, 0:18],
                                                  AF.Sqrt),
                      waits=[ev_var], signal=ev("S"), tag=f"sd{l}")
        B.add("v", lambda e: e.reciprocal(stm[:, 54:72], stm[:, 18:36]),
              waits=[ev_sd], tag=f"rs{l}")
        ev_z1 = B.add("v", lambda e: e.memset(stm[:, 72:108], 1.0),
                      signal=ev("V"), tag=f"suz{l}")
        ev_rs = B.last["V"]
        f1 = B.dma("y", stm[0:127, 72:90], stm[1:128, 36:54],
                   waits=[ev_z1, ev_rs], signal=ev("DY"), tag=f"su1{l}")
        f2 = B.dma("y", stm[0:127, 90:108], stm[1:128, 54:72],
                   waits=[ev_z1, ev_rs], signal=ev("DY"), tag=f"su2{l}")
        f3 = B.dma("y", stm[127:128, 72:89], stm[0:1, 37:54],
                   waits=[ev_z1, ev_rs], signal=ev("DY"), tag=f"su3{l}")
        f4 = B.dma("y", stm[127:128, 90:107], stm[0:1, 55:72],
                   waits=[ev_z1, ev_rs], signal=ev("DY"), tag=f"su4{l}")
        ev_h2 = None
        ev_apply = {}
        for c in range(17):
            src = xh8[:, :] if c == 0 else H.x[:, 1024 * (c - 1):1024 * c]
            B.add("v", lambda e, src=src, c=c: e.tensor_scalar(
                h2bf[:, 1024 * c:1024 * c + 512], src[:, 0:512],
                stm[:, 36 + c:37 + c], stm[:, 54 + c:55 + c],
                ALU.add, ALU.mult), waits=[f1, f2, f3, f4], tag=f"apA{l}{c}")
            ev_h2 = B.add("v", lambda e, src=src, c=c: e.tensor_scalar(
                h2bf[:, 1024 * c + 512:1024 * (c + 1)], src[:, 512:1024],
                stm[:, 72 + c:73 + c], stm[:, 90 + c:91 + c],
                ALU.add, ALU.mult),
                signal=ev("V"), tag=f"apB{l}{c}")
            ev_apply[c] = ev_h2
        g_PT = [ev_apply[0], ev_apply[0]]
        ev_h2T = None
        ev_hT_chunk = {}
        for c in range(17):
            for m in range(8):
                par = m % 2
                tt = B.add("t", lambda e, c=c, m=m, par=par: e.transpose(
                    PT2d[:, 1024 * par:1024 * par + 128],
                    h2bf[:, 1024 * c + 128 * m:1024 * c + 128 * (m + 1)],
                    idb2[:, :]),
                    waits=[ev_apply[c], g_PT[par]], signal=ev("T"),
                    tag=f"hT{l}{c}")
                if c == 0:
                    dst = h2T[:, 2049 * m:2049 * m + 1]
                    srcp = PT2d[:, 1024 * par + 127:1024 * par + 128]
                else:
                    dst = h2T[:, 2049 * m + 1 + 128 * (c - 1):
                              2049 * m + 1 + 128 * c]
                    srcp = PT2d[:, 1024 * par:1024 * par + 128]
                g_PT[par] = B.add(
                    "v", lambda e, dst=dst, srcp=srcp, m=m: e.tensor_scalar(
                        dst, srcp, lgb[:, m:m + 1], lgb[:, 8 + m:9 + m],
                        ALU.mult, ALU.add),
                    waits=[tt, e_lg, e_lb], signal=ev("V"), tag=f"hTe{l}{c}")
                ev_h2T = g_PT[par]
            ev_hT_chunk[c] = ev_h2T
        g_w1 = [H.ev_xmid] * NW
        g_w2 = [H.ev_xmid] * NW
        g_prod = None
        g_pw1 = [None, None]
        g_gsb = None
        ev_res = None
        for tb in range(4):
            for k in range(43):
                par = k % NW
                e_w = B.dma("g", w1s[par].ap().rearrange("p (m c) -> p m c", c=128),
                            H.w1P[l][:, 128 * k:128 * (k + 1)]
                            .rearrange("(m p) c -> p m c", p=128),
                            waits=[g_w1[par]], signal=ev("DG"), tag=f"w1d{l}")
                kb = k % 2
                lastm = None
                for m in range(8):
                    off = 2049 * m + 512 * tb + (1 if m < 4 else 0)
                    lastm = B.add(
                        "t", lambda e, m=m, par=par, kb=kb, off=off: e.matmul(
                            PW1[:, 512 * kb:512 * (kb + 1)],
                            w1s[par][:, 128 * m:128 * (m + 1)],
                            h2T[:, off:off + 512],
                            start=(m == 0), stop=(m == 7)),
                        waits=([e_w, ev_hT_chunk[min(4 * tb + 4, 16)],
                                g_pw1[kb]] if m == 0 else ()),
                        signal=ev("T") if m == 7 else None, tag=f"w1m{l}{k}")
                g_w1[par] = lastm
                eg = B.add("s", lambda e, kb=kb, k=k: e.activation(
                    gel[0:64, :], PW1[64:128, 512 * kb:512 * (kb + 1)],
                    AF.Gelu, bias=b1pp[64:128, k:k + 1]),
                    waits=[lastm, e_b1, g_prod], signal=ev("S"),
                    tag=f"gel{l}{k}")
                mt, half = k // 2, k % 2
                g_prod = g_pw1[kb] = B.add(
                    "v", lambda e, kb=kb, k=k, mt=mt, half=half:
                    e.scalar_tensor_tensor(
                        gsb[64 * half:64 * half + 64, 512 * mt:512 * (mt + 1)],
                        PW1[0:64, 512 * kb:512 * (kb + 1)],
                        b1pp[0:64, k:k + 1], gel[0:64, :],
                        ALU.add, ALU.mult),
                    waits=[eg, g_gsb], signal=ev("V"), tag=f"prd{l}{k}")
            ev_gsb = g_pw1[par]
            ev_z = B.add("v", lambda e: e.memset(
                gsb[64:128, 512 * 21:512 * 22], 0.0),
                waits=[ev_gsb], signal=ev("V"), tag=f"gz{l}")
            for eb in range(2):
                for kp2 in range(NB2):
                    par = kp2 % NW
                    e_w2 = B.dma("g", w2s[par][:, :],
                                 H.w2P[l][128 * kp2:128 * (kp2 + 1),
                                          512 * eb:512 * (eb + 1)],
                                 waits=[g_w2[par]], signal=ev("DG"),
                                 tag=f"w2d{l}")
                    lastm = None
                    for ts_ in range(4):
                        lastm = B.add(
                            "t", lambda e, ts_=ts_, kp2=kp2, par=par: e.matmul(
                                PW2[:, 512 * ts_:512 * (ts_ + 1)],
                                gsb[:, 512 * kp2 + 128 * ts_:512 * kp2 + 128 * (ts_ + 1)],
                                w2s[par][:, :],
                                start=(kp2 == 0), stop=False),
                            waits=([e_w2, ev_z, ev_res] if ts_ == 0 else ()),
                            signal=ev("T") if ts_ == 3 else None,
                            tag=f"w2m{l}{kp2}")
                    g_w2[par] = lastm
                lastb = None
                for ts_ in range(4):
                    lastb = B.add(
                        "t", lambda e, ts_=ts_, eb=eb: e.matmul(
                            PW2[:, 512 * ts_:512 * (ts_ + 1)],
                            ones1[0:1, :],
                            b2r[0:1, 512 * eb:512 * (eb + 1)],
                            start=False, stop=True),
                        waits=[e_b2] if ts_ == 0 else (),
                        signal=ev("T") if ts_ == 3 else None, tag=f"b2m{l}")
                for ts_ in range(4):
                    cc = 4 * tb + ts_
                    ev_res = B.add(
                        "v", lambda e, cc=cc, eb=eb, ts_=ts_: e.tensor_add(
                            H.x[:, 1024 * cc + 512 * eb:1024 * cc + 512 * (eb + 1)],
                            H.x[:, 1024 * cc + 512 * eb:1024 * cc + 512 * (eb + 1)],
                            PW2[:, 512 * ts_:512 * (ts_ + 1)]),
                        waits=[lastb], signal=ev("V"), tag=f"xr{l}{cc}{eb}")
            g_gsb = ev_res
        H.ev_l_done = ev_res


def _build_layer(l):
    nc = bass.Bass("TRN2", target_bir_lowering=False)
    H = NS()
    H.xin = _din(nc, "xin", [T_, D_], F32)
    H.halo = _din(nc, "halo", [1152, D_], F32)
    H.wqkv = _din(nc, "wqkv", [2, D_, 1536], F32)
    H.wo = _din(nc, "woM", [2, 512, D_], F32)
    H.w1P = _din(nc, "w1P", [2, D_, HID2], F32)
    H.b1P = _din(nc, "b1P", [2, HID2], F32)
    H.w2P = _din(nc, "w2P", [2, 22 * 128, D_], F32)
    H.b2 = _din(nc, "b2", [2, D_], F32)
    H.lng = _din(nc, "lng", [2, D_], F32)
    H.lnb = _din(nc, "lnb", [2, D_], F32)
    H.qsw = _din(nc, "qsw", [2, 512], F32)
    H.ksw = _din(nc, "ksw", [2, 512], F32)
    H.xtab = _din(nc, "xtab", [3200, 2048], BF16)
    H.msk_lo = _din(nc, "msk_lo", [128, 128], BF16)
    H.msk_hi = _din(nc, "msk_hi", [128, 128], BF16)
    H.prevoff = _din(nc, "prevoff", [128, 128], BF16)
    H.idbf = _din(nc, "identbf", [128, 128], BF16)
    H.idf = _din(nc, "identf32", [128, 128], F32)
    H.ones128 = _din(nc, "ones128", [1, 128], BF16)
    H.ones128f = _din(nc, "ones128f", [1, 128], F32)
    if l == 1:
        H.x1own = _din(nc, "x1own", [T_, D_], F32)
    H.out = nc.dram_tensor("out", [T_, D_], F32, kind="ExternalOutput").ap()

    B = Builder(nc)
    ev = B.ev
    with ExitStack() as st:
        H.x = st.enter_context(nc.sbuf_tensor("x_stream", [128, 16 * D_], F32,
                                              side="right"))
        xh = st.enter_context(nc.sbuf_tensor("xh", [128, 9 * D_], BF16))
        xh8 = st.enter_context(nc.sbuf_tensor("xh8", [128, D_], F32,
                                              side="right"))
        ones1 = st.enter_context(nc.sbuf_tensor("ones1", [128, 128], BF16))
        ones1f = st.enter_context(nc.sbuf_tensor("ones1f", [1, 128], F32))
        idb2 = st.enter_context(nc.sbuf_tensor("idb2", [128, 128], BF16))
        idf2 = st.enter_context(nc.sbuf_tensor("idf2", [128, 128], F32))
        mlo = st.enter_context(nc.sbuf_tensor("mlo", [128, 128], BF16))
        mhi = st.enter_context(nc.sbuf_tensor("mhi", [128, 128], BF16))
        mpo = st.enter_context(nc.sbuf_tensor("mpo", [128, 128], BF16))
        csts = [
            B.dma("y", ones1[0:1, :], H.ones128, signal=ev("DY"), tag="c2"),
            B.dma("y", ones1f[0:1, :], H.ones128f, signal=ev("DY"), tag="c2f"),
            B.dma("y", idb2[:, :], H.idbf, signal=ev("DY"), tag="c3"),
            B.dma("y", idf2[:, :], H.idf, signal=ev("DY"), tag="c4"),
            B.dma("y", mlo[:, :], H.msk_lo, signal=ev("DY"), tag="c5"),
            B.dma("y", mhi[:, :], H.msk_hi, signal=ev("DY"), tag="c6"),
            B.dma("y", mpo[:, :], H.prevoff, signal=ev("DY"), tag="c7"),
        ]
        H.ev_xready = B.dma("y", _v3(H.x.ap(), D_),
                            H.xin.rearrange("(c p) d -> p c d", p=128),
                            signal=ev("DY"), tag="xload")
        ev_hb = B.dma("g", _v3(xh.ap(), D_),
                      H.halo.rearrange("(c p) d -> p c d", p=128),
                      signal=ev("DG"), tag="hld")
        ev_h8 = B.dma("y", xh8[:, :],
                      H.halo.rearrange("(c p) d -> p c d", p=128)[:, 8, :],
                      signal=ev("DY"), tag="h8ld")
        ev_halo = [ev_hb] * 9 + [ev_h8]
        H.ev_xmid = None
        guard = csts
        _emit_layer(nc, B, st, H, l, xh, xh8, ones1, ones1f, idb2, idf2,
                    (mlo, mhi, mpo), ev_halo, guard)
        if l == 0:
            evs = None
            for c in range(16):
                evs = B.dma("y", H.out.rearrange("(c p) d -> p c d", p=128)[:, c, :],
                            H.x[:, 1024 * c:1024 * (c + 1)],
                            waits=[H.ev_l_done], signal=ev("DY"), tag=f"o{c}")
        else:
            _emit_final_ln(nc, B, st, H, x1dram=H.x1own)
        B.finalize(st)
    return nc


def _emit_final_ln(nc, B, st, H, x1dram):
    ev = B.ev
    with ExitStack() as sf:
        xsc = [sf.enter_context(nc.sbuf_tensor(f"xscF{i}", [128, D_], F32))
               for i in range(2)]
        sqs = sf.enter_context(nc.sbuf_tensor("sqsF", [128, D_], F32))
        stt = sf.enter_context(nc.sbuf_tensor("sttF", [128, 16], F32))
        g_x = [None, None]
        for c in range(16):
            xc = H.x[:, 1024 * c:1024 * (c + 1)]
            lx = B.dma("y", xsc[c % 2][:, :],
                       x1dram.rearrange("(c p) d -> p c d", p=128)[:, c, :],
                       waits=[w for w in (H.ev_l_done, g_x[c % 2]) if w],
                       signal=ev("DY"), tag=f"fx{c}")
            ra = B.add("v", lambda e, xc=xc, c=c: e.tensor_add(
                xc, xc, xsc[c % 2][:, :]), waits=[lx], signal=ev("V"),
                tag=f"fra{c}")
            g_x[c % 2] = ra
            B.add("v", lambda e, xc=xc: e.reduce_sum(stt[:, 0:1], xc, axis=AX.X),
                  waits=[ra], tag=f"fls{c}")
            ev_sq = B.add("s", lambda e, xc=xc: e.activation(
                sqs[:, :], xc, AF.Square, accum_out=stt[:, 1:2]),
                waits=[ra], signal=ev("S"), tag=f"flq{c}")
            B.add("v", lambda e: e.tensor_scalar_mul(stt[:, 2:3], stt[:, 0:1],
                                                     -1.0 / D_), tag="f1")
            B.add("v", lambda e: e.tensor_scalar_mul(stt[:, 3:4], stt[:, 1:2],
                                                     1.0 / D_),
                  waits=[ev_sq], tag="f2")
            B.add("v", lambda e: e.tensor_mul(stt[:, 4:5], stt[:, 2:3],
                                              stt[:, 2:3]), tag="f3")
            ev_var = B.add("v", lambda e: e.scalar_tensor_tensor(
                stt[:, 5:6], stt[:, 3:4], LNEPS, stt[:, 4:5],
                ALU.add, ALU.subtract), signal=ev("V"), tag="f4")
            ev_sd = B.add("s", lambda e: e.activation(stt[:, 6:7], stt[:, 5:6],
                                                      AF.Sqrt),
                          waits=[ev_var], signal=ev("S"), tag="f5")
            B.add("v", lambda e: e.reciprocal(stt[:, 7:8], stt[:, 6:7]),
                  waits=[ev_sd], tag="f6")
            ev_ln = B.add("v", lambda e, xc=xc: e.tensor_scalar(
                xc, xc, stt[:, 2:3], stt[:, 7:8], ALU.add, ALU.mult),
                signal=ev("V"), tag=f"flap{c}")
            B.dma("y", H.out.rearrange("(c p) d -> p c d", p=128)[:, c, :],
                  xc, waits=[ev_ln], signal=ev("DY"), tag=f"fout{c}")


NO_RD = False

def _build(stage):
    nc = bass.Bass("TRN2", target_bir_lowering=False,
                   detect_race_conditions=not NO_RD)
    H = NS()
    H.stage = stage
    H.emb = _din(nc, "emb_full", [S_, D_], F32)
    H.embq = _din(nc, "embq", [T_, D_], F32)
    H.embo = _din(nc, "emb_own", [T_, D_], F32)
    H.WqT = _din(nc, "WqT", [D_, D_], F32)
    H.WkT = _din(nc, "WkT", [D_, D_], F32)
    H.WvT = _din(nc, "WvT", [D_, D_], F32)
    H.WrT = _din(nc, "WrT", [D_, D_], F32)
    H.rc1f = _din(nc, "rc1f", [256, D_], F32)
    H.rs1f = _din(nc, "rs1f", [256, D_], F32)
    H.rcq = _din(nc, "rcq", [128, D_], F32)
    H.rsq = _din(nc, "rsq", [128, D_], F32)
    H.idbf = _din(nc, "identbf", [128, 128], BF16)
    H.idf = _din(nc, "identf32", [128, 128], F32)
    H.out = nc.dram_tensor("out", [T_, D_], F32, kind="ExternalOutput").ap()
    H.vscr = nc.dram_tensor("vscr", [S_, D_], BF16, kind="Internal").ap()
    H.hoff = 0
    H.wqkv = _din(nc, "wqkv", [2, D_, 1536], F32)
    H.wo = _din(nc, "woM", [2, 512, D_], F32)
    H.w1P = _din(nc, "w1P", [2, D_, HID2], F32)
    H.b1P = _din(nc, "b1P", [2, HID2], F32)
    H.w2P = _din(nc, "w2P", [2, 22 * 128, D_], F32)
    H.b2 = _din(nc, "b2", [2, D_], F32)
    H.lng = _din(nc, "lng", [2, D_], F32)
    H.lnb = _din(nc, "lnb", [2, D_], F32)
    H.qsw = _din(nc, "qsw", [2, 512], F32)
    H.ksw = _din(nc, "ksw", [2, 512], F32)
    H.xtab = _din(nc, "xtab", [3200, 2048], BF16)
    H.gate = _din(nc, "gate", [128, 1], F32)
    H.wsend = _din(nc, "wsend", [128, 1], F32)
    H.ones128 = _din(nc, "ones128", [1, 128], BF16)
    H.ones128f = _din(nc, "ones128f", [1, 128], F32)
    H.msk_lo = _din(nc, "msk_lo", [128, 128], BF16)
    H.msk_hi = _din(nc, "msk_hi", [128, 128], BF16)
    H.prevoff = _din(nc, "prevoff", [128, 128], BF16)
    H.fthr = _din(nc, "fthr", [1, 16], mybir.dt.int32)
    H.fmask = _din(nc, "fmask", [1, 16], mybir.dt.int32)
    H.ixoff = _din(nc, "ixoff", [128, 19], mybir.dt.int32)
    H.epochs = _din(nc, "epochs", [128, 6], mybir.dt.int32)

    B = Builder(nc)
    with ExitStack() as st:
        _emit_phase1(nc, B, st, H)
        if stage != "p1":
            _emit_phase2(nc, B, st, H)
        B.finalize(st)
    return nc



def _rope1_tables():
    inv = 1.0 / (10000.0 ** (np.arange(0, ROT1, 2, dtype=np.float64) / ROT1))
    r = np.arange(256)
    cb = np.arange(16)
    d = np.arange(64)
    p = 16 * (r[:, None, None] % 256) + cb[None, :, None] + 0 * d[None, None, :]
    fi = np.zeros(64)
    fi[0:ROT1:2] = inv
    fi[1:ROT1:2] = inv
    ang = p * fi[None, None, :]
    cos = np.cos(ang)
    sin = np.sin(ang)
    cos[:, :, ROT1:] = 1.0
    ssin = np.zeros_like(sin)
    ssin[:, :, 0:ROT1:2] = -sin[:, :, 0:ROT1:2]
    ssin[:, :, 1:ROT1:2] = sin[:, :, 1:ROT1:2]
    return (cos.reshape(256, 1024).astype(np.float32),
            ssin.reshape(256, 1024).astype(np.float32))


def _prep_core_inputs(inp, b, u):
    m = {}
    emb = np.ascontiguousarray(np.asarray(inp["L0_attn_emb"][b], np.float32))
    m["emb_full"] = emb
    m["emb_own"] = np.ascontiguousarray(emb[T_ * u:T_ * (u + 1)])
    qrows = np.zeros((T_, D_), np.float32)
    for h in range(16):
        qrows[128 * h:128 * (h + 1)] = emb[256 * h + 128 * u:256 * h + 128 * u + 128]
    m["embq"] = qrows
    m["WqT"] = np.ascontiguousarray(np.asarray(inp["L0_attn_Wq"], np.float32).T)
    m["WkT"] = np.ascontiguousarray(np.asarray(inp["L0_attn_Wk"], np.float32).T)
    m["WvT"] = np.ascontiguousarray(np.asarray(inp["L0_attn_Wv"], np.float32).T)
    m["WrT"] = np.ascontiguousarray(np.asarray(inp["L0_attn_Wr"], np.float32).T)
    c1, s1 = _rope1_tables()
    m["rc1f"], m["rs1f"] = c1, s1
    m["rcq"] = np.ascontiguousarray(c1[128 * u:128 * (u + 1)])
    m["rsq"] = np.ascontiguousarray(s1[128 * u:128 * (u + 1)])
    m["identbf"] = np.eye(128, dtype=ml_dtypes.bfloat16)
    m["identf32"] = np.eye(128, dtype=np.float32)
    m["wqkv"] = np.ascontiguousarray(np.asarray(inp["L0_ffn_wqkv"], np.float32))
    m["woM"] = np.ascontiguousarray(np.asarray(inp["L0_ffn_wo"], np.float32))
    w1 = np.asarray(inp["L0_ffn_w1"], np.float32)
    b1 = np.asarray(inp["L0_ffn_b1"], np.float32)
    w1p = np.zeros((2, D_, HID2), np.float32)
    b1p = np.zeros((2, HID2), np.float32)
    for k in range(NBLK):
        na = min(64, HID0 - 64 * k)
        w1p[:, :, 128 * k:128 * k + na] = w1[:, :, 64 * k:64 * k + na]
        b1p[:, 128 * k:128 * k + na] = b1[:, 64 * k:64 * k + na]
        w1p[:, :, 128 * k + 64:128 * k + 64 + na] = \
            w1[:, :, HID0 + 64 * k:HID0 + 64 * k + na]
        b1p[:, 128 * k + 64:128 * k + 64 + na] = \
            b1[:, HID0 + 64 * k:HID0 + 64 * k + na]
    m["w1P"] = w1p
    m["b1P"] = b1p
    w2 = np.asarray(inp["L0_ffn_w2"], np.float32)
    w2p = np.zeros((2, 22 * 128, D_), np.float32)
    for blk in range(NBLK):
        mt, sh = divmod(blk, 2)
        n = min(64, HID0 - 64 * blk)
        w2p[:, 128 * mt + 64 * sh:128 * mt + 64 * sh + n] = \
            w2[:, 64 * blk:64 * blk + n]
    m["w2P"] = w2p
    m["b2"] = np.ascontiguousarray(np.asarray(inp["L0_ffn_b2"], np.float32))
    m["lng"] = np.ascontiguousarray(np.asarray(inp["L0_ffn_ln_g"], np.float32))
    m["lnb"] = np.ascontiguousarray(np.asarray(inp["L0_ffn_ln_b"], np.float32))
    m["qsw"] = np.ascontiguousarray(
        np.tile(np.asarray(inp["L0_ffn_q_scale"], np.float32), (1, 8)))
    m["ksw"] = np.ascontiguousarray(
        np.tile(np.asarray(inp["L0_ffn_k_scale"], np.float32), (1, 8)))
    m["xtab"] = _xtab(u).astype(ml_dtypes.bfloat16)
    m["gate"] = np.full((128, 1), 1.0 if u == 1 else 0.0, np.float32)
    m["wsend"] = np.full((128, 1), 1.0 if u == 0 else 0.0, np.float32)
    m["ones128"] = np.ones((1, 128), ml_dtypes.bfloat16)
    m["ones128f"] = np.ones((1, 128), np.float32)
    qp = np.arange(128)[:, None]
    kp = np.arange(128)[None, :]
    m["msk_lo"] = np.where(kp >= qp, 0.0, NEG).astype(ml_dtypes.bfloat16)
    m["msk_hi"] = np.where(kp <= qp, 0.0, NEG).astype(ml_dtypes.bfloat16)
    m["prevoff"] = np.full((128, 128), 0.0 if u == 1 else NEG,
                           np.float32).astype(ml_dtypes.bfloat16)
    c = 2 * b + u
    MG = MAGIC
    pr = np.arange(128)
    sl = c & 1
    ixo = np.zeros((128, 19), np.int32)
    for j in range(9):
        ixo[:, j] = 1152 * sl + 128 * j + pr
        ixo[:, 9 + j] = 1152 * (1 - sl) + 128 * j + pr
    ixo[:, 18] = 128 * sl + pr
    m["ixoff"] = ixo
    m["fthr"] = np.array([[MG[0]] * 8 + [MG[1]] * 8], np.int32)
    m["fmask"] = np.full((1, 16), -1, np.int32)
    ep = np.zeros((128, 6), np.int32)
    ep[:, 0] = _EPOCH[0]
    ep[:, 1] = _EPOCH[0]
    ep[:, 2] = _EPOCH[0] + 1
    ep[:, 3] = _EPOCH[0] + 1
    m["epochs"] = ep
    return m


def _xtab(u):
    DH = 64
    j = np.arange(0, DH, 2, dtype=np.float64)
    inv = 1.0 / (10000.0 ** (j / DH))
    t0 = 2048 * u - 1152
    t = np.arange(t0, t0 + 3200, dtype=np.float64)
    fr = t[:, None] * inv[None, :]
    fr = np.concatenate([fr, fr], axis=1)
    base = (j + 0.4 * DH) / (1.4 * DH)
    power = (t - S_ // 2) / float(WIN // 2)
    sc = base[None, :] ** power[:, None]
    sc = np.concatenate([sc, sc], axis=1)
    cos, sin = np.cos(fr), np.sin(fr)
    sgn = np.ones(DH)
    sgn[0:32] = -1.0
    qc = cos * sc
    qs = sgn * sin * sc
    kc = cos / sc
    ks = sgn * sin / sc
    rep = lambda a: np.tile(a, (1, 8)).astype(np.float32)
    return np.ascontiguousarray(
        np.concatenate([rep(qc), rep(qs), rep(kc), rep(ks)], axis=1))


def _run(nc, in_maps):
    used = nc._my_inputs
    maps = [{k: np.ascontiguousarray(v) for k, v in m.items() if k in used}
            for m in in_maps]
    res = run_bass_kernel_spmd(nc, maps, core_ids=list(range(8)))
    out = np.zeros((B_, S_, D_), np.float32)
    for c in range(8):
        b, u = divmod(c, 2)
        out[b, T_ * u:T_ * (u + 1)] = res.results[c]["out"]
    return out


def kernel(**inputs):
    _fresh_epoch()
    in_maps = []
    for c in range(8):
        b, u = divmod(c, 2)
        in_maps.append(_prep_core_inputs(inputs, b, u))
    if STAGE == "full":
        return _run(_build("full"), in_maps)
    x1 = _run(_build("p1"), in_maps)
    def _halo(xcur, b, u):
        if u == 0:
            return np.zeros((1152, D_), np.float32)
        return np.ascontiguousarray(xcur[b, 2048 - 1152:2048])
    for c in range(8):
        b, u = divmod(c, 2)
        in_maps[c]["xin"] = np.ascontiguousarray(x1[b, T_ * u:T_ * (u + 1)])
        in_maps[c]["halo"] = _halo(x1, b, u)
    x0o = _run(_build_layer(0), in_maps)
    for c in range(8):
        b, u = divmod(c, 2)
        in_maps[c]["xin"] = np.ascontiguousarray(x0o[b, T_ * u:T_ * (u + 1)])
        in_maps[c]["halo"] = _halo(x0o, b, u)
        in_maps[c]["x1own"] = np.ascontiguousarray(x1[b, T_ * u:T_ * (u + 1)])
    return _run(_build_layer(1), in_maps)


# version 17 (final)
# speedup vs baseline: 1.1305x; 1.1305x over previous
import math
import time
from contextlib import ExitStack

import numpy as np
import ml_dtypes

import concourse.bass as bass
import concourse.mybir as mybir
from concourse.bass_utils import run_bass_kernel_spmd

F32 = mybir.dt.float32
BF16 = mybir.dt.bfloat16
AF = mybir.ActivationFunctionType
ALU = mybir.AluOpType
AX = mybir.AxisListType

B_, S_, D_ = 4, 4096, 1024
T_ = 2048
NH1 = 16
ROT1 = 32
WIN = 512
HID0 = 2730
NBLK = 43
HID2 = 2 * 64 * NBLK
NEG = -30000.0
MAGIC = [0x4D474301, 0x4D474302]
_EPOCH = [(int(time.time_ns() // 1000) & 0x0FFFFFFF) | 0x10000000]


def _fresh_epoch():
    _EPOCH[0] = ((int(time.time_ns() // 1000) + _EPOCH[0] * 2 + 16)
                 & 0x0FFFFFFF) | 0x10000000
    return _EPOCH[0]
POLL_CORES = 8
LNEPS = 1e-5

STAGE = "full"


class _Ev:
    __slots__ = ("key", "val", "sem")
    def __init__(self, key):
        self.key = key
        self.val = None
        self.sem = key

class _Op:
    __slots__ = ("fn", "waits", "signal", "step", "tag")
    def __init__(self, fn, waits, signal, step, tag):
        self.fn = fn; self.waits = waits; self.signal = signal
        self.step = step; self.tag = tag

_COMPUTE_SEM = {"t": "T", "v": "V", "s": "S", "g": "G"}

DMA_POOL = {"DY": 12, "DG": 8, "DS": 4, "C": 2}

class Builder:

    def __init__(self, nc):
        self.nc = nc
        self.q = {k: [] for k in ("t", "v", "s", "g", "y")}
        self.compute_keys = ["T", "V", "S", "G"]
        self.sem_names = list(self.compute_keys)
        for k, n in DMA_POOL.items():
            self.sem_names += [f"{k}{i}" for i in range(n)]
        self.recent = {k: [] for k in list(DMA_POOL) + self.compute_keys}
        self.last = {k: None for k in list(DMA_POOL) + self.compute_keys}

    def ev(self, key):
        return _Ev(key)

    def add(self, eng, fn, waits=(), signal=None, tag=""):
        if signal is not None:
            r = self.recent[signal.key]
            r.append(signal)
            keep = DMA_POOL.get(signal.key, 1)
            del r[:-keep]
            self.last[signal.key] = signal
        self.q[eng].append(_Op(fn, [w for w in waits if w is not None],
                               signal, 0, tag))
        return signal

    def dma(self, eng, out, in_, waits=(), signal=None, tag="", **kw):
        assert eng in ("y", "g", "s", "v")
        def fn(e, out=out, in_=in_, kw=kw):
            return e.dma_start(out=out, in_=in_, **kw)
        return self.add(eng, fn, waits, signal, tag)

    def barrier(self, tag="bar"):
        waits = [e for r in self.recent.values() for e in r]
        evs = []
        for eng, key in (("t", "T"), ("v", "V"), ("s", "S"), ("g", "G")):
            evs.append(self.add(eng, lambda e: e.nop(), waits,
                                signal=self.ev(key), tag=tag + eng))
        self.add("y", lambda e: e.nop(), waits, tag=tag + "y")
        return evs

    def finalize(self, stack):
        waits = [e for r in self.recent.values() for e in r]
        self.add("g", lambda e: e.nop(), waits, tag="final_drain")
        nc = self.nc
        sems = {n: stack.enter_context(nc.semaphore(name=f"sem{n}"))
                for n in self.sem_names + ["PL0", "PL1", "PLB"]}
        self.sems = sems
        counts = {n: 0 for n in self.sem_names}
        pool_i = {k: 0 for k in DMA_POOL}
        for eng in ("v", "s", "g"):
            key = _COMPUTE_SEM[eng]
            prev = None
            for op in self.q[eng]:
                if op.signal is not None and op.signal.key in DMA_POOL:
                    continue
                if op.signal is None:
                    op.signal = _Ev(key)
                if prev is not None:
                    op.waits.append(prev)
                prev = op.signal
        for eng in ("t", "v", "s", "g", "y"):
            for op in self.q[eng]:
                sig = op.signal
                if sig is None:
                    continue
                if sig.key in DMA_POOL:
                    i = pool_i[sig.key]
                    pool_i[sig.key] = (i + 1) % DMA_POOL[sig.key]
                    name = f"{sig.key}{i}"
                    if counts[name] > 0:
                        prev = _Ev(sig.key)
                        prev.sem = name
                        prev.val = counts[name]
                        op.waits.append(prev)
                    op.step = 16
                else:
                    name = sig.key
                    op.step = 1
                counts[name] += op.step
                sig.sem = name
                sig.val = counts[name]
        self._check()
        block = stack.enter_context(nc.Block())
        watermark = {eng: {n: 0 for n in self.sem_names}
                     for eng in ("t", "v", "s", "g", "y")}

        def emit(eng_name, engine):
            wm = watermark[eng_name]
            own = _COMPUTE_SEM.get(eng_name)
            for op in self.q[eng_name]:
                for evt in op.waits:
                    assert evt.val is not None, f"unsignaled event: {op.tag}"
                    if evt.val > wm[evt.sem]:
                        engine.wait_ge(sems[evt.sem], evt.val)
                        wm[evt.sem] = evt.val
                ins = op.fn(engine)
                if op.signal is not None:
                    ins.then_inc(sems[op.signal.sem], op.step)

        @block.tensor
        def _(e): emit("t", e)
        @block.vector
        def _(e): emit("v", e)
        @block.scalar
        def _(e): emit("s", e)
        @block.gpsimd
        def _(e): emit("g", e)
        @block.sync
        def _(e): emit("y", e)

    def _check(self):
        idx = {k: 0 for k in self.q}
        done = {n: 0 for n in self.sem_names}
        progress = True
        while progress:
            progress = False
            for eng, ops in self.q.items():
                while idx[eng] < len(ops):
                    op = ops[idx[eng]]
                    if any(e.val is None or done[e.sem] < e.val for e in op.waits):
                        break
                    if op.signal is not None:
                        done[op.signal.sem] += op.step
                    idx[eng] += 1
                    progress = True
        stuck = {e: (i, len(self.q[e])) for e, i in idx.items() if i < len(self.q[e])}
        if stuck:
            lines = []
            for e, (i, n) in stuck.items():
                op = self.q[e][i]
                lines.append(f"  {e}: op {i}/{n} tag={op.tag!r} waits="
                             f"{[(x.sem, x.val, done[x.sem]) for x in op.waits]}")
            raise RuntimeError("builder deadlock:\n" + "\n".join(lines))


class NS:
    pass


def _v3(ap, inner):
    return ap.rearrange("p (c d) -> p c d", d=inner)



def _din(nc, name, shape, dtype):
    if not hasattr(nc, "_my_inputs"):
        nc._my_inputs = set()
    nc._my_inputs.add(name)
    return nc.dram_tensor(name, shape, dtype, kind="ExternalInput").ap()



def _emit_phase1(nc, B, st, H):
    ev = B.ev
    s1 = ExitStack()
    H.O1 = s1.enter_context(nc.sbuf_tensor("O1t", [128, 16 * D_], BF16))
    rcf = s1.enter_context(nc.sbuf_tensor("rcfs", [128, 2 * D_], F32))
    rsf = s1.enter_context(nc.sbuf_tensor("rsfs", [128, 2 * D_], F32))
    rcq = s1.enter_context(nc.sbuf_tensor("rcqs", [128, D_], F32))
    rsq = s1.enter_context(nc.sbuf_tensor("rsqs", [128, D_], F32))
    idb = s1.enter_context(nc.sbuf_tensor("idb", [128, 128], BF16))
    idf32 = s1.enter_context(nc.sbuf_tensor("idf32", [128, 128], F32))

    tb = []
    for dst, src in ((rcf, H.rc1f), (rsf, H.rs1f)):
        tb.append(B.dma("y", _v3(dst.ap(), D_), src.rearrange("(c p) d -> p c d", p=128),
                        signal=ev("DY"), tag="tbl"))
    tb.append(B.dma("y", rcq[:, :], H.rcq, signal=ev("DY"), tag="tbl"))
    tb.append(B.dma("y", rsq[:, :], H.rsq, signal=ev("DY"), tag="tbl"))
    ev_idb = B.dma("y", idb[:, :], H.idbf, signal=ev("DY"), tag="idb")
    ev_idf = B.dma("y", idf32[:, :], H.idf, signal=ev("DY"), tag="idf")
    ev_tbl = tb

    with ExitStack() as sa:
        Wqb = sa.enter_context(nc.sbuf_tensor("Wqb", [128, 8 * D_], BF16))
        Wkb = sa.enter_context(nc.sbuf_tensor("Wkb", [128, 8 * D_], BF16))
        Wvb = sa.enter_context(nc.sbuf_tensor("Wvb", [128, 8 * D_], BF16))
        ebf = sa.enter_context(nc.sbuf_tensor("ebf", [128, 4 * D_], BF16))
        ebq = sa.enter_context(nc.sbuf_tensor("ebq", [128, 2 * D_], BF16))
        embT = sa.enter_context(nc.sbuf_tensor("embT", [128, 32 * 128], BF16))
        qembT = sa.enter_context(nc.sbuf_tensor("qembT", [128, 16 * 128], BF16))
        kro = sa.enter_context(nc.sbuf_tensor("kro", [128, 4 * D_], BF16))
        qro = sa.enter_context(nc.sbuf_tensor("qro", [128, 2 * D_], BF16))
        vbf = sa.enter_context(nc.sbuf_tensor("vbf", [128, 4 * D_], BF16))
        t1f = sa.enter_context(nc.sbuf_tensor("t1f", [128, 512], F32))
        t2f = sa.enter_context(nc.sbuf_tensor("t2f", [128, 512], F32))
        kt = sa.enter_context(nc.sbuf_tensor("kt", [128, 4096], BF16))
        qt = sa.enter_context(nc.sbuf_tensor("qt", [128, 2048], BF16))
        vt = [sa.enter_context(nc.sbuf_tensor(f"vt{i}", [128, 32 * 65], BF16))
              for i in range(2)]
        pab = sa.enter_context(nc.sbuf_tensor("pab", [128, 4 * 1024], BF16))
        svc = sa.enter_context(nc.sbuf_tensor("svc", [128, 1024], F32))
        rsum = sa.enter_context(nc.sbuf_tensor("rsum", [128, 16], F32))
        PT = sa.enter_context(nc.psum_tensor("PT", [128, 2048], BF16))
        PP = sa.enter_context(nc.psum_tensor("PP", [128, 512], F32))
        SP = sa.enter_context(nc.psum_tensor("SP", [128, 1024], F32))
        SVp = sa.enter_context(nc.psum_tensor("SVp", [128, 1024], F32))
        EP = sa.enter_context(nc.psum_tensor("EP", [128, 512], F32))

        ev_w = []
        for W, src in ((Wqb, H.WqT), (Wkb, H.WkT), (Wvb, H.WvT)):
            ev_w.append(B.dma("g", _v3(W.ap(), D_),
                              src.rearrange("(m p) o -> p m o", p=128),
                              signal=ev("DG"), tag="wld"))
        ev_vones = None
        for i in range(2):
            ev_vones = B.add("v", lambda e, t=vt[i]: e.memset(t[:, :], 1.0),
                             signal=ev("V"), tag="vones")
        ev_svz = B.add("v", lambda e: e.memset(svc[:, :], 0.0),
                       signal=ev("V"), tag="svz")

        kt3 = kt.ap().rearrange("p (t c) -> p t c", c=16)
        qt3 = qt.ap().rearrange("p (t c) -> p t c", c=16)

        g = NS()
        g.ebf = g.ebq = None
        g.embT = g.qembT = None
        g.kro = g.qro = None
        g.vbf = None
        g.kt = g.qt = None
        g.vt = [ev_vones, ev_vones]
        g.PT = [ev_idb, ev_idb]
        g.PP = None
        g.SPexp = [None, None]
        g.SVepi = None
        g.EP = ev_idf
        g.pab = [None] * 4
        g.O1done = None

        def rope(dst_ap, psrc, cslc, sslc, tagp, waits=()):
            B.add("v", lambda e: e.tensor_mul(t1f[:, :], psrc, cslc),
                  waits=waits, tag=tagp + "a")
            pv = psrc.rearrange("p (b two) -> p b two", two=2)
            sv_ = sslc.rearrange("p (b two) -> p b two", two=2)
            t2v = t2f.ap().rearrange("p (b two) -> p b two", two=2)
            B.add("v", lambda e: e.tensor_mul(t2v[:, :, 0], pv[:, :, 1], sv_[:, :, 0]),
                  tag=tagp + "b")
            B.add("v", lambda e: e.tensor_mul(t2v[:, :, 1], pv[:, :, 0], sv_[:, :, 1]),
                  tag=tagp + "c")
            return B.add("v", lambda e: e.tensor_add(dst_ap, t1f[:, :], t2f[:, :]),
                         signal=ev("V"), tag=tagp + "d")

        for a in range(8):
            e_ebf = B.dma("g", _v3(ebf.ap(), D_),
                          H.emb[512 * a:512 * (a + 1), :]
                          .rearrange("(c p) d -> p c d", p=128),
                          waits=(g.ebf or []), signal=ev("DG"), tag=f"ebf{a}")
            e_ebq = B.dma("g", _v3(ebq.ap(), D_),
                          H.embq[256 * a:256 * (a + 1), :]
                          .rearrange("(c p) d -> p c d", p=128),
                          waits=(g.ebq or []), signal=ev("DG"), tag=f"ebq{a}")
            lastT = None
            for c4 in range(4):
                for half in range(2):
                    par = (2 * c4 + half) % 2
                    tw = [e_ebf, g.PT[par], g.embT]
                    for mm in range(4):
                        m = 4 * half + mm
                        lastT = B.add(
                            "t", lambda e, c4=c4, m=m, mm=mm, par=par: e.transpose(
                                PT[:, 1024 * par + 128 * mm:1024 * par + 128 * (mm + 1)],
                                ebf[:, 1024 * c4 + 128 * m:1024 * c4 + 128 * (m + 1)],
                                idb[:, :]),
                            waits=tw if mm == 0 else (),
                            signal=ev("T") if mm == 3 else None, tag=f"eT{a}")
                    eng_ = "v" if par == 0 else "s"
                    g.PT[par] = B.add(
                        eng_, lambda e, c4=c4, half=half, par=par, eng_=eng_:
                        (e.tensor_copy if eng_ == "v" else
                         lambda d, s_: e.activation(d, s_, AF.Copy))(
                            embT[:, (8 * c4 + 4 * half) * 128:(8 * c4 + 4 * half + 4) * 128],
                            PT[:, 1024 * par:1024 * par + 512]),
                        waits=[lastT], signal=ev("V" if par == 0 else "S"),
                        tag=f"eTe{a}")
            ev_embT = [g.PT[0], g.PT[1]]
            for ch in range(2):
                for half in range(2):
                    par = (2 * ch + half) % 2
                    tw = [e_ebq, g.PT[par], g.qembT]
                    for mm in range(4):
                        m = 4 * half + mm
                        lastT = B.add(
                            "t", lambda e, ch=ch, m=m, mm=mm, par=par: e.transpose(
                                PT[:, 1024 * par + 128 * mm:1024 * par + 128 * (mm + 1)],
                                ebq[:, 1024 * ch + 128 * m:1024 * ch + 128 * (m + 1)],
                                idb[:, :]),
                            waits=tw if mm == 0 else (),
                            signal=ev("T") if mm == 3 else None, tag=f"qT{a}")
                    eng_ = "v" if par == 0 else "s"
                    g.PT[par] = B.add(
                        eng_, lambda e, ch=ch, half=half, par=par, eng_=eng_:
                        (e.tensor_copy if eng_ == "v" else
                         lambda d, s_: e.activation(d, s_, AF.Copy))(
                            qembT[:, (8 * ch + 4 * half) * 128:(8 * ch + 4 * half + 4) * 128],
                            PT[:, 1024 * par:1024 * par + 512]),
                        waits=[lastT], signal=ev("V" if par == 0 else "S"),
                        tag=f"qTe{a}")
            ev_qembT = [g.PT[0], g.PT[1]]

            ev_kro = ev_qro = ev_vevac = None
            for c4 in range(4):
                for which, Wb in (("k", Wkb), ("v", Wvb)):
                    for eb in range(2):
                        lastm = None
                        for m in range(8):
                            lastm = B.add(
                                "t", lambda e, c4=c4, m=m, eb=eb, Wb=Wb: e.matmul(
                                    PP[:, :],
                                    embT[:, (8 * c4 + m) * 128:(8 * c4 + m + 1) * 128],
                                    Wb[:, 1024 * m + 512 * eb:1024 * m + 512 * (eb + 1)],
                                    start=(m == 0), stop=(m == 7)),
                                waits=ev_embT + [g.PP] + ev_w if m == 0 else (),
                                signal=ev("T") if m == 7 else None, tag=f"kv{a}")
                        if which == "k":
                            g.PP = rope(kro[:, 1024 * c4 + 512 * eb:1024 * c4 + 512 * (eb + 1)],
                                        PP[:, :],
                                        rcf[:, 1024 * (c4 % 2) + 512 * eb:1024 * (c4 % 2) + 512 * (eb + 1)],
                                        rsf[:, 1024 * (c4 % 2) + 512 * eb:1024 * (c4 % 2) + 512 * (eb + 1)],
                                        f"krp{a}",
                                        waits=[lastm] + (g.kro or []) + ev_tbl)
                            ev_kro = g.PP
                        else:
                            g.PP = B.add(
                                "s", lambda e, c4=c4, eb=eb: e.copy(
                                    vbf[:, 1024 * c4 + 512 * eb:1024 * c4 + 512 * (eb + 1)],
                                    PP[:, :]),
                                waits=[lastm, g.vbf], signal=ev("S"), tag=f"vev{a}")
                            ev_vevac = g.PP
            for ch in range(2):
                for eb in range(2):
                    lastm = None
                    for m in range(8):
                        lastm = B.add(
                            "t", lambda e, ch=ch, m=m, eb=eb: e.matmul(
                                PP[:, :],
                                qembT[:, (8 * ch + m) * 128:(8 * ch + m + 1) * 128],
                                Wqb[:, 1024 * m + 512 * eb:1024 * m + 512 * (eb + 1)],
                                start=(m == 0), stop=(m == 7)),
                            waits=ev_qembT + [g.PP] + ev_w if m == 0 else (),
                            signal=ev("T") if m == 7 else None, tag=f"qp{a}")
                    g.PP = rope(qro[:, 1024 * ch + 512 * eb:1024 * ch + 512 * (eb + 1)],
                                PP[:, :],
                                rcq[:, 512 * eb:512 * (eb + 1)],
                                rsq[:, 512 * eb:512 * (eb + 1)], f"qrp{a}",
                                waits=[lastm] + (g.qro or []) + ev_tbl)
                    ev_qro = g.PP

            ev_vs = B.dma("y", H.vscr[512 * a:512 * (a + 1), :]
                          .rearrange("(c p) d -> p c d", p=128),
                          _v3(vbf.ap(), D_), waits=[ev_vevac], signal=ev("DY"), tag=f"vs{a}")
            g.vbf = ev_vs
            ev_vt = []
            for i in range(2):
                h4 = 2 * a + i
                src = H.vscr[256 * h4:256 * (h4 + 1), :] \
                    .rearrange("(cc pp) (pl d) -> (pp pl) cc d", pp=8, pl=16)
                dst = vt[i].ap().rearrange("p (cc e) -> p cc e", e=65)[:, :, 0:64]
                ev_vt.append(B.dma("y", dst, src, waits=[ev_vs, g.vt[i]],
                                   signal=ev("DY"), tag=f"vt{a}{i}"))

            ev_ktb = ev_qtb = None
            for cb in range(16):
                par = cb % 2
                lastT = None
                first = True
                for base, c4s in ((0, (0, 1)), (64, (2, 3))):
                    for jc, c4 in enumerate(c4s):
                        last = (base == 64 and jc == 1)
                        lastT = B.add(
                            "t", lambda e, cb=cb, c4=c4, base=base, jc=jc, par=par:
                            e.transpose(
                                PT[base:base + 64,
                                   1024 * par + 128 * jc:1024 * par + 128 * (jc + 1)],
                                kro[:, 1024 * c4 + 64 * cb:1024 * c4 + 64 * (cb + 1)],
                                idb[:, :],
                                tile_position=(0, base)),
                            waits=[ev_kro, g.PT[par], g.kt] if first else (),
                            signal=ev("T") if last else None, tag=f"ktT{a}")
                        first = False
                eng_ = "v" if par == 0 else "s"
                g.PT[par] = B.add(
                    eng_, lambda e, cb=cb, par=par, eng_=eng_:
                    (e.tensor_copy if eng_ == "v" else
                     lambda d, s_: e.activation(d, s_, AF.Copy))(
                        kt3[:, 0:256, cb], PT[:, 1024 * par:1024 * par + 256]),
                    waits=[lastT], signal=ev("V" if par == 0 else "S"),
                    tag=f"ktE{a}")
            ev_ktb = [g.PT[0], g.PT[1]]
            for cb in range(16):
                par = cb % 2
                lastT = None
                for j, (base, ch) in enumerate(((0, 0), (64, 1))):
                    lastT = B.add(
                        "t", lambda e, cb=cb, ch=ch, base=base, par=par:
                        e.transpose(
                            PT[base:base + 64, 1024 * par:1024 * par + 128],
                            qro[:, 1024 * ch + 64 * cb:1024 * ch + 64 * (cb + 1)],
                            idb[:, :],
                            tile_position=(0, base)),
                        waits=[ev_qro, g.PT[par], g.qt] if j == 0 else (),
                        signal=ev("T") if j == 1 else None, tag=f"qtT{a}")
                eng_ = "v" if par == 0 else "s"
                g.PT[par] = B.add(
                    eng_, lambda e, cb=cb, par=par, eng_=eng_:
                    (e.tensor_copy if eng_ == "v" else
                     lambda d, s_: e.activation(d, s_, AF.Copy))(
                        qt3[:, 0:128, cb], PT[:, 1024 * par:1024 * par + 128]),
                    waits=[lastT], signal=ev("V" if par == 0 else "S"),
                    tag=f"qtE{a}")
            ev_qtb = [g.PT[0], g.PT[1]]

            ev_s = {}
            ev_e = {}
            ev_v = {}
            last_sv = [None, None]
            prev_pair_sv = [g.vt[0], g.vt[1]]

            def emit_sv(qpb, cc):
                slot = cc % 4
                for i in range(2):
                    sw = [ev_e[(qpb, cc, i)]]
                    if cc == 0:
                        sw = sw + [ev_vt[i], g.SVepi]
                    ev_v[(qpb, cc, i)] = B.add(
                        "t", lambda e, cc=cc, slot=slot, i=i, qpb=qpb: e.matmul(
                            SVp[0:65, 512 * i:512 * (i + 1)],
                            vt[i][:, 65 * cc:65 * cc + 65],
                            pab[:, 1024 * slot + 512 * i:1024 * slot + 512 * (i + 1)],
                            start=(cc == 0), stop=(cc == 31)),
                        waits=sw, signal=ev("T"), tag=f"sv{a}")

            for qpb in range(4):
                for c in range(32):
                    for i in range(2):
                        w0 = (ev_ktb + ev_qtb + [g.SPexp[i]]
                              if (qpb == 0 and c == 0) else [])
                        if c > 0:
                            w0 = w0 + [ev_e.get((qpb, c - 1, i))]
                        elif qpb > 0:
                            w0 = w0 + [ev_e.get((qpb - 1, 31, i))]
                        ev_s[(qpb, c, i)] = B.add(
                            "t", lambda e, c=c, qpb=qpb, i=i: e.matmul(
                                SP[:, 512 * i:512 * (i + 1)],
                                kt[64 * i:64 * (i + 1), 128 * c:128 * (c + 1)],
                                qt[64 * i:64 * (i + 1), 512 * qpb:512 * (qpb + 1)],
                                start=True, stop=True),
                            waits=w0, signal=ev("T"), tag=f"s{i}{a}")
                    if c > 0:
                        emit_sv(qpb, c - 1)
                    slot = c % 4
                    for i in range(2):
                        ew = [ev_s[(qpb, c, i)]]
                        if (qpb, c - 4, i) in ev_v:
                            ew.append(ev_v[(qpb, c - 4, i)])
                        elif c < 4 and qpb > 0:
                            ew.append(ev_v[(qpb - 1, 28 + c, i)])
                        elif c < 4:
                            ew.append(prev_pair_sv[i])
                        ev_e[(qpb, c, i)] = B.add(
                            "s", lambda e, slot=slot, i=i: e.activation(
                                pab[:, 1024 * slot + 512 * i:
                                    1024 * slot + 512 * (i + 1)],
                                SP[:, 512 * i:512 * (i + 1)],
                                AF.Exp, scale=0.25),
                            waits=ew, signal=ev("S"), tag=f"ex{a}")
                emit_sv(qpb, 31)
                last_sv = [ev_v[(qpb, 31, 0)], ev_v[(qpb, 31, 1)]]

                EPr = EP.ap().rearrange("p (j x) -> p j x", x=128)
                ecp = []
                for i in range(2):
                    ecp.append(B.add(
                        "v", lambda e, i=i: e.tensor_copy(
                            svc[0:65, 512 * i:512 * (i + 1)],
                            SVp[0:65, 512 * i:512 * (i + 1)]),
                        waits=[last_sv[i], ev_svz], signal=ev("V"),
                        tag=f"svc{a}"))
                g.SVepi = ecp[-1]
                for i in range(2):
                    lastT = None
                    for j in range(4):
                        lastT = B.add(
                            "t", lambda e, i=i, j=j: e.transpose(
                                EP[:, 128 * j:128 * (j + 1)],
                                svc[:, 512 * i + 128 * j:512 * i + 128 * (j + 1)],
                                idf32[:, :]),
                            waits=[ecp[i], ev_idf, g.EP] if j == 0 else (),
                            signal=ev("T") if j == 3 else None,
                            tag=f"epT{a}")
                    ro = 8 * i
                    B.add("v", lambda e, ro=ro: e.tensor_copy(
                        rsum[:, ro:ro + 4], EPr[:, :, 64]),
                        waits=[lastT], tag=f"rs{a}{i}")
                    B.add("v", lambda e, ro=ro: e.reciprocal(
                        rsum[:, ro + 4:ro + 8], rsum[:, ro:ro + 4]),
                        tag=f"rr{a}{i}")
                    evo = None
                    for j in range(4):
                        cidx = 4 * qpb + j
                        hg = H.hoff + 2 * a + i
                        evo = B.add(
                            "v", lambda e, j=j, cidx=cidx, hg=hg, ro=ro:
                            e.tensor_single_scalar(
                                H.O1[:, 1024 * cidx + 64 * hg:1024 * cidx + 64 * (hg + 1)],
                                EP[:, 128 * j:128 * j + 64],
                                rsum[:, ro + 4 + j:ro + 5 + j], ALU.mult),
                            signal=ev("V") if j == 3 else None, tag=f"o1{a}{i}")
                    g.EP = evo
                g.O1done = g.EP

            g.ebf = ev_embT
            g.ebq = ev_qembT
            g.embT = ev_s[(3, 31, 1)]
            g.qembT = ev_s[(3, 31, 1)]
            g.kro = ev_ktb
            g.qro = ev_qtb
            g.kt = ev_s[(3, 31, 1)]
            g.qt = ev_s[(3, 31, 1)]
            g.vt = [last_sv[0], last_sv[1]]
            g.SPexp = [ev_e[(3, 31, 0)], ev_e[(3, 31, 1)]]

        H.ev_p1_attn_done = g.O1done

    bar = B.barrier("p1bar")
    ev = B.ev
    H.ev_ln_chunk = {}
    H.x = st.enter_context(nc.sbuf_tensor("x_stream", [128, 16 * D_], F32,
                                          side="right"))
    with ExitStack() as sb:
        WrTb = sb.enter_context(nc.sbuf_tensor("WrTb", [128, 8 * D_], BF16))
        sqs = sb.enter_context(nc.sbuf_tensor("sqs", [128, D_], F32))
        stt = sb.enter_context(nc.sbuf_tensor("stt", [128, 16], F32))
        PQ = [sb.enter_context(nc.psum_tensor(f"PQ{i}", [128, 512], F32))
              for i in range(2)]

        ev_wr = B.dma("g", _v3(WrTb.ap(), D_),
                      H.WrT.rearrange("(m p) o -> p m o", p=128),
                      waits=bar, signal=ev("DG"), tag="wrld")
        ev_xl = B.dma("y", _v3(H.x.ap(), D_),
                      H.embo.rearrange("(c p) d -> p c d", p=128),
                      waits=bar, signal=ev("DY"), tag="xload")

        g_pq = [bar[0], bar[0]]
        ev_ln = None
        for qloc in range(2):
            for sc in range(8):
                cidx = 8 * qloc + sc
                ev_mm = {}
                for eb in range(2):
                    pq = PQ[eb]
                    lastm = None
                    for jc in range(8):
                        lastm = B.add(
                            "t", lambda e, qloc=qloc, sc=sc, eb=eb, jc=jc, pq=pq: e.matmul(
                                pq[:, :],
                                WrTb[:, 1024 * jc + 128 * sc:1024 * jc + 128 * (sc + 1)],
                                H.O1[:, 1024 * (8 * qloc + jc) + 512 * eb:
                                     1024 * (8 * qloc + jc) + 512 * (eb + 1)],
                                start=(jc == 0), stop=(jc == 7)),
                            waits=[ev_wr, H.ev_p1_attn_done, g_pq[eb]] if jc == 0 else (),
                            signal=ev("T") if jc == 7 else None, tag=f"wr{qloc}{sc}")
                    ev_mm[eb] = lastm
                ra = []
                for eb in range(2):
                    ra.append(B.add(
                        "v", lambda e, cidx=cidx, eb=eb: e.tensor_add(
                            H.x[:, 1024 * cidx + 512 * eb:1024 * cidx + 512 * (eb + 1)],
                            H.x[:, 1024 * cidx + 512 * eb:1024 * cidx + 512 * (eb + 1)],
                            PQ[eb][:, :]),
                        waits=[ev_mm[eb], ev_xl], signal=ev("V"), tag=f"ra{cidx}"))
                    g_pq[eb] = ra[-1]
                xc = H.x[:, 1024 * cidx:1024 * (cidx + 1)]
                B.add("v", lambda e, xc=xc: e.reduce_sum(stt[:, 0:1], xc, axis=AX.X),
                      waits=[ra[1]], tag=f"ls{cidx}")
                ev_sq = B.add("s", lambda e, xc=xc: e.activation(
                    sqs[:, :], xc, AF.Square, accum_out=stt[:, 1:2]),
                    waits=[ra[1]], signal=ev("S"), tag=f"lq{cidx}")
                B.add("v", lambda e: e.tensor_scalar_mul(stt[:, 2:3], stt[:, 0:1],
                                                         -1.0 / D_), tag="l1")
                B.add("v", lambda e: e.tensor_scalar_mul(stt[:, 3:4], stt[:, 1:2],
                                                         1.0 / D_),
                      waits=[ev_sq], tag="l2")
                B.add("v", lambda e: e.tensor_mul(stt[:, 4:5], stt[:, 2:3],
                                                  stt[:, 2:3]), tag="l3")
                ev_var = B.add("v", lambda e: e.scalar_tensor_tensor(
                    stt[:, 5:6], stt[:, 3:4], LNEPS, stt[:, 4:5],
                    ALU.add, ALU.subtract),
                    signal=ev("V"), tag="l4")
                ev_sd = B.add("s", lambda e: e.activation(stt[:, 6:7], stt[:, 5:6],
                                                          AF.Sqrt),
                              waits=[ev_var], signal=ev("S"), tag="l5")
                B.add("v", lambda e: e.reciprocal(stt[:, 7:8], stt[:, 6:7]),
                      waits=[ev_sd], tag="l6")
                ev_ln = B.add("v", lambda e, xc=xc: e.tensor_scalar(
                    xc, xc, stt[:, 2:3], stt[:, 7:8], ALU.add, ALU.mult),
                    signal=ev("V"), tag=f"lapp{cidx}")
                H.ev_ln_chunk[cidx] = ev_ln
                if H.stage == "p1":
                    H.ev_x1 = B.dma(
                        "y", H.out.rearrange("(c p) d -> p c d", p=128)[:, cidx, :],
                        xc, waits=[ev_ln], signal=ev("DY"), tag=f"xout{cidx}")
        H.ev_x1_ln = ev_ln
    s1.close()
    H.p1bar = B.barrier("p1end")
    return H



def _emit_phase2(nc, B, st, H):
    ev = B.ev
    RG = [[0, 1], [2, 3], [4, 5], [6, 7]]
    xh = st.enter_context(nc.sbuf_tensor("xh", [128, 9 * D_], BF16))
    xh8 = st.enter_context(nc.sbuf_tensor("xh8", [128, D_], F32,
                                           side="right"))
    gateg = st.enter_context(nc.sbuf_tensor("gateg", [128, 1], F32))
    wsendg = st.enter_context(nc.sbuf_tensor("wsendg", [128, 1], F32))
    ones1 = st.enter_context(nc.sbuf_tensor("ones1", [128, 128], BF16))
    ones1f = st.enter_context(nc.sbuf_tensor("ones1f", [1, 128], F32))
    idb2 = st.enter_context(nc.sbuf_tensor("idb2", [128, 128], BF16))
    idf2 = st.enter_context(nc.sbuf_tensor("idf2", [128, 128], F32))
    mlo = st.enter_context(nc.sbuf_tensor("mlo", [128, 128], BF16))
    mhi = st.enter_context(nc.sbuf_tensor("mhi", [128, 128], BF16))
    mpo = st.enter_context(nc.sbuf_tensor("mpo", [128, 128], BF16))
    x1scr = nc.dram_tensor("x1scr", [T_, D_], F32, kind="Internal").ap()
    H.shmx = [nc.dram_tensor(f"shmx{i}", [2 * 1152, D_], F32, kind="Internal",
                             addr_space="Shared").ap() for i in range(2)]
    H.shmf = nc.dram_tensor("shmf", [2 * 128, 2], mybir.dt.int32,
                            kind="Internal", addr_space="Shared").ap()
    H.fsb = st.enter_context(nc.sbuf_tensor("fsb", [1, 4], mybir.dt.int32))
    H.xstg = [st.enter_context(nc.sbuf_tensor(f"xstg{i}", [128, D_], BF16,
                                              side="right"))
              for i in range(2)]
    H.fsbt = st.enter_context(nc.sbuf_tensor("fsbt", [1, 16], mybir.dt.int32))
    H.fsbm = st.enter_context(nc.sbuf_tensor("fsbm", [1, 16], mybir.dt.int32))
    H.ixo = st.enter_context(nc.sbuf_tensor("ixo", [128, 19], mybir.dt.int32))
    H.epochv = st.enter_context(nc.sbuf_tensor("epochv", [128, 6],
                                               mybir.dt.int32))

    e_cst = [
        B.dma("y", gateg[:, :], H.gate, waits=H.p1bar, signal=ev("DY"), tag="c0"),
        B.dma("y", wsendg[:, :], H.wsend, waits=H.p1bar, signal=ev("DY"), tag="c1"),
        B.dma("y", ones1[0:1, :], H.ones128, waits=H.p1bar, signal=ev("DY"), tag="c2"),
        B.dma("y", ones1f[0:1, :], H.ones128f, waits=H.p1bar, signal=ev("DY"), tag="c2f"),
        B.dma("y", idb2[:, :], H.idbf, waits=H.p1bar, signal=ev("DY"), tag="c3"),
        B.dma("y", idf2[:, :], H.idf, waits=H.p1bar, signal=ev("DY"), tag="c4"),
        B.dma("y", mlo[:, :], H.msk_lo, waits=H.p1bar, signal=ev("DY"), tag="c5"),
        B.dma("y", mhi[:, :], H.msk_hi, waits=H.p1bar, signal=ev("DY"), tag="c6"),
        B.dma("y", mpo[:, :], H.prevoff, waits=H.p1bar, signal=ev("DY"), tag="c7"),
    ]
    H.e_tbl = [
        B.dma("y", H.fsbt[:, :], H.fthr, waits=H.p1bar, signal=ev("DY"),
              tag="cft"),
        B.dma("y", H.fsbm[:, :], H.fmask, waits=H.p1bar, signal=ev("DY"),
              tag="cfm"),
        B.dma("y", H.ixo[:, :], H.ixoff, waits=H.p1bar, signal=ev("DY"),
              tag="cxo"),
        B.dma("y", H.epochv[:, :], H.epochs, waits=H.p1bar, signal=ev("DY"),
              tag="cep"),
    ]
    e_cst += H.e_tbl
    ev_x1s = B.dma("y", x1scr.rearrange("(c p) d -> p c d", p=128),
                   _v3(H.x.ap(), D_), waits=[H.ev_x1_ln], signal=ev("DY"),
                   tag="x1save")

    def mkpoll(l, accept, wcol):
        def pollfn(e, l=l, accept=accept, wcol=wcol):
            sem = B.sems[f"PL{l}"]
            fh = H.shmf.rearrange("(s p) w -> p s w", p=128)[0:1, :, :]
            with ExitStack() as rs:
                rc = rs.enter_context(e.register(f"rcP{l}"))
                rt = rs.enter_context(e.register(f"rtP{l}"))
                rv = rs.enter_context(e.register(f"rvP{l}"))
                rw = rs.enter_context(e.register(f"rwP{l}"))
                e.reg_load(rw, H.epochv[0:1, wcol:wcol + 1])
                e.reg_mov(rc, 1)
                e.reg_mov(rt, 0)
                with e.While(rc):
                    e.reg_alu(rt, rt, 16, ALU.add)
                    e.dma_start(out=H.fsb.ap()
                                .rearrange("p (s w) -> p s w", w=2)[0:1, :, :],
                                in_=fh, allow_slow_non_contiguous=True
                                ).then_inc(sem, 16)
                    e.wait_ge(sem, rt)
                    e.reg_mov(rc, 0)
                    for k in accept:
                        e.reg_load(rv, H.fsb[0:1, k:k + 1])
                        e.reg_alu(rv, rv, rw, ALU.bitwise_xor)
                        e.reg_alu(rc, rc, rv, ALU.bitwise_or)
            return e.nop()
        return pollfn

    ev_coll_done = None
    for l in range(2):
        H.ev_xready = H.ev_x1_ln if l == 0 else H.ev_l_done
        wev = []
        for j, c in enumerate(range(7, 16)):
            wsrc = H.ev_ln_chunk.get(c) if l == 0 else H.ev_xready
            wev.append(B.add(
                "g", lambda e, j=j, c=c, l=l: e.indirect_dma_start(
                    H.shmx[l][:, :],
                    bass.IndirectOffsetOnAxis(ap=H.ixo[:, j:j + 1], axis=0),
                    H.x[:, 1024 * c:1024 * (c + 1)], None),
                waits=[wsrc] + H.e_tbl, signal=ev("DG"),
                tag=f"xw{l}{c}"))
        ev_fw = B.add(
            "g", lambda e, l=l: e.indirect_dma_start(
                H.shmf[:, :],
                bass.IndirectOffsetOnAxis(ap=H.ixo[:, 18:19], axis=0),
                H.epochv[:, 2 * l:2 * l + 2], None),
            waits=wev, signal=ev("DG"), tag=f"fw{l}")

        ev_poll = B.add("g", mkpoll(l, [0 + l, 2 + l], 2 * l),
                        waits=[ev_fw], signal=ev("G"), tag=f"poll{l}")
        gh = []
        g_stage = [ev_coll_done, ev_coll_done]
        for j in range(9):
            sb = xh8 if j == 8 else H.xstg[j % 2]
            rdev = B.add(
                "g", lambda e, j=j, l=l, sb=sb: e.indirect_dma_start(
                    sb[:, :], None,
                    H.shmx[l][:, :],
                    bass.IndirectOffsetOnAxis(ap=H.ixo[:, 9 + j:10 + j], axis=0)),
                waits=[ev_poll, g_stage[j % 2]], signal=ev("DG"),
                tag=f"xr{l}{j}")
            cst = B.add(
                "v", lambda e, j=j, sb=sb: e.tensor_scalar_mul(
                    xh[:, 1024 * j:1024 * (j + 1)], sb[:, :], gateg[:, 0:1]),
                waits=[rdev, e_cst[0], H.ev_xready], signal=ev("V"),
                tag=f"gx{l}{j}")
            g_stage[j % 2] = cst
            gh.append(cst)
        ev_h8g = B.add("v", lambda e: e.tensor_scalar_mul(
            xh8[:, :], xh8[:, :], gateg[:, 0:1]), waits=[gh[-1]],
            signal=ev("V"), tag=f"g8{l}")
        ev_halo = gh + [ev_h8g]
        ev_coll_done = ev_h8g

        guard = H.p1bar if l == 0 else [H.ev_l_done]
        _emit_layer(nc, B, st, H, l, xh, xh8, ones1, ones1f, idb2, idf2,
                    (mlo, mhi, mpo), ev_halo, guard)

    with ExitStack() as sf:
        xsc = [sf.enter_context(nc.sbuf_tensor(f"xsc{i}", [128, D_], F32))
               for i in range(2)]
        sqs = sf.enter_context(nc.sbuf_tensor("sqsF", [128, D_], F32))
        stt = sf.enter_context(nc.sbuf_tensor("sttF", [128, 16], F32))
        ev_out = None
        g_x = [None, None]
        for c in range(16):
            xc = H.x[:, 1024 * c:1024 * (c + 1)]
            lx = B.dma("y", xsc[c % 2][:, :],
                       x1scr.rearrange("(c p) d -> p c d", p=128)[:, c, :],
                       waits=[ev_x1s, H.ev_l_done, g_x[c % 2]],
                       signal=ev("DY"), tag=f"fx{c}")
            ra = B.add("v", lambda e, xc=xc, c=c: e.tensor_add(
                xc, xc, xsc[c % 2][:, :]), waits=[lx], signal=ev("V"),
                tag=f"fra{c}")
            g_x[c % 2] = ra
            B.add("v", lambda e, xc=xc: e.reduce_sum(stt[:, 0:1], xc, axis=AX.X),
                  waits=[ra], tag=f"fls{c}")
            ev_sq = B.add("s", lambda e, xc=xc: e.activation(
                sqs[:, :], xc, AF.Square, accum_out=stt[:, 1:2]),
                waits=[ra], signal=ev("S"), tag=f"flq{c}")
            B.add("v", lambda e: e.tensor_scalar_mul(stt[:, 2:3], stt[:, 0:1],
                                                     -1.0 / D_), tag="f1")
            B.add("v", lambda e: e.tensor_scalar_mul(stt[:, 3:4], stt[:, 1:2],
                                                     1.0 / D_),
                  waits=[ev_sq], tag="f2")
            B.add("v", lambda e: e.tensor_mul(stt[:, 4:5], stt[:, 2:3],
                                              stt[:, 2:3]), tag="f3")
            ev_var = B.add("v", lambda e: e.scalar_tensor_tensor(
                stt[:, 5:6], stt[:, 3:4], LNEPS, stt[:, 4:5],
                ALU.add, ALU.subtract), signal=ev("V"), tag="f4")
            ev_sd = B.add("s", lambda e: e.activation(stt[:, 6:7], stt[:, 5:6],
                                                      AF.Sqrt),
                          waits=[ev_var], signal=ev("S"), tag="f5")
            B.add("v", lambda e: e.reciprocal(stt[:, 7:8], stt[:, 6:7]),
                  waits=[ev_sd], tag="f6")
            ev_ln = B.add("v", lambda e, xc=xc: e.tensor_scalar(
                xc, xc, stt[:, 2:3], stt[:, 7:8], ALU.add, ALU.mult),
                signal=ev("V"), tag=f"flap{c}")
            ev_out = B.dma(
                "y", H.out.rearrange("(c p) d -> p c d", p=128)[:, c, :],
                xc, waits=[ev_ln], signal=ev("DY"), tag=f"fout{c}")
        H.ev_out = ev_out


def _emit_layer(nc, B, st, H, l, xh, xh8, ones1, ones1f, idb2, idf2, masks, ev_halo,
                guard):
    ev = B.ev
    mlo, mhi, mpo = masks
    NKC, NQC, NIC = 24, 20, 25

    with ExitStack() as sl:
        qbf = sl.enter_context(nc.sbuf_tensor(f"qbf{l}", [128, NQC * 512], BF16))
        kbf = sl.enter_context(nc.sbuf_tensor(f"kbf{l}", [128, NKC * 512], BF16))
        vly = sl.enter_context(nc.sbuf_tensor(f"vly{l}", [128, NKC * 520], BF16))
        ev_vones = B.add("v", lambda e: e.memset(vly[:, :], 1.0),
                         signal=ev("V"), tag=f"von{l}")

        qsb = sl.enter_context(nc.sbuf_tensor(f"qsb{l}", [128, 512], BF16))
        ksb = sl.enter_context(nc.sbuf_tensor(f"ksb{l}", [128, 512], BF16))
        qrow = sl.enter_context(nc.sbuf_tensor(f"qrow{l}", [1, 512], F32))
        with nc.psum_tensor(f"pbc{l}", [128, 512], F32) as PB:
            prev = None
            for dst, srcr in ((qsb, H.qsw), (ksb, H.ksw)):
                e_qr = B.dma("y", qrow[0:1, :], srcr[l:l + 1, :],
                             waits=[prev] + list(guard), signal=ev("DY"),
                             tag=f"qr{l}")
                mm = B.add("t", lambda e: e.matmul(
                    PB[:, :], ones1f[0:1, :], qrow[0:1, :],
                    start=True, stop=True),
                    waits=[e_qr, prev], signal=ev("T"), tag=f"bc{l}")
                prev = B.add("v", lambda e, dst=dst: e.tensor_copy(dst[:, :],
                                                                   PB[:, :]),
                             waits=[mm], signal=ev("V"), tag=f"bce{l}")
            ev_bc = prev

        with ExitStack() as sA:
            wqb = sA.enter_context(nc.sbuf_tensor(f"wqb{l}", [128, 8 * 1536], BF16))
            xTt = [sA.enter_context(nc.sbuf_tensor(f"xT{l}{i}", [128, 8 * 129], BF16))
                   for i in range(2)]
            xbs = sA.enter_context(nc.sbuf_tensor(f"xbs{l}", [128, D_], BF16))
            xT9 = sA.enter_context(nc.sbuf_tensor(f"xT9{l}", [128, 8 * 129], BF16))
            xtw = [sA.enter_context(nc.sbuf_tensor(f"xtw{l}0", [128, 1024], BF16))] * 2
            qraw = sA.enter_context(nc.sbuf_tensor(f"qraw{l}", [128, 1024], F32))
            zq = sA.enter_context(nc.sbuf_tensor(f"zq{l}", [128, 512], BF16))
            t1p = qraw
            t2p = sA.enter_context(nc.sbuf_tensor(f"t2p{l}", [128, 512], BF16))
            nrm = sA.enter_context(nc.sbuf_tensor(f"nrm{l}", [128, 64], F32))
            sqs = sA.enter_context(nc.sbuf_tensor(f"sqs{l}", [128, 1024], BF16))
            PT2 = sA.enter_context(nc.psum_tensor(f"PT2{l}", [128, 2048], BF16))
            PP2 = sA.enter_context(nc.psum_tensor(f"PP2{l}", [128, 1536], F32))

            ev_w = B.dma("g", wqb.ap().rearrange("p (m o) -> p m o", o=1536),
                         H.wqkv[l].rearrange("(m p) o -> p m o", p=128),
                         waits=guard, signal=ev("DG"), tag=f"wq{l}")
            g_PT = [ev_bc, ev_bc]
            g_xT = [None, None]
            g_blk = {0: guard[0], 1: guard[0], 2: guard[0]}
            g_chain = guard[0]
            g_chain2 = {0: guard[0], 1: guard[0]}
            g_srd = {}
            chain_par = 0
            ev_q_done = ev_k_done = ev_v_done = None
            tile9_ev = None
            prev_tile = {}

            def emit_tile(i, buf, srcw, xsrc, gxt, seed):
                lastT = None
                for half in range(2):
                    par = half
                    for mm_ in range(4):
                        m = 4 * half + mm_
                        lastT = B.add(
                            "t", lambda e, m=m, mm_=mm_, par=par, xsrc=xsrc:
                            e.transpose(
                                PT2[:, 1024 * par + 128 * mm_:1024 * par + 128 * (mm_ + 1)],
                                xsrc[:, 128 * m:128 * (m + 1)],
                                idb2[:, :]),
                            waits=(srcw + [g_PT[par], g_srd.get(id(buf))])
                            if mm_ == 0 else (),
                            signal=ev("T") if mm_ == 3 else None,
                            tag=f"xT{l}{i}")
                    if par == 0:
                        g_PT[par] = B.add(
                            "v", lambda e, half=half, par=par, buf=buf:
                            e.tensor_copy(
                                buf.ap().rearrange("p (m c) -> p m c", c=129)
                                [:, 4 * half:4 * half + 4, 1:129],
                                PT2.ap().rearrange("p (m c) -> p m c", c=128)
                                [:, 8 * par:8 * par + 4, :]),
                            waits=[lastT, gxt], signal=ev("V"), tag=f"xTe{l}{i}")
                    else:
                        g_PT[par] = B.add(
                            "s", lambda e, half=half, par=par, buf=buf:
                            e.activation(
                                buf.ap().rearrange("p (m c) -> p m c", c=129)
                                [:, 4 * half:4 * half + 4, 1:129],
                                PT2.ap().rearrange("p (m c) -> p m c", c=128)
                                [:, 8 * par:8 * par + 4, :], AF.Copy),
                            waits=[lastT, gxt], signal=ev("S"), tag=f"xTe{l}{i}")
                evt = g_PT[1]
                if seed is not None:
                    sbuf, sev = seed
                    evt = B.add("v", lambda e, buf=buf, sbuf=sbuf: e.tensor_copy(
                        buf.ap().rearrange("p (m c) -> p m c", c=129)[:, :, 0],
                        sbuf.ap().rearrange("p (m c) -> p m c", c=129)[:, :, 128]),
                        waits=[evt, sev], signal=ev("V"), tag=f"seed{l}{i}")
                    g_srd[id(sbuf)] = evt
                return evt

            wln0 = (getattr(H, "ev_ln_chunk", {}).get(0, H.ev_xready)
                    if l == 0 else H.ev_xready)
            e_src9 = B.add("s", lambda e: e.activation(
                xbs[:, :], H.x[:, 0:1024], AF.Copy), waits=[wln0],
                signal=ev("S"), tag=f"xc{l}9")
            tile9_ev = emit_tile(9, xT9, [e_src9], xbs[:, :], None, None)
            prev_tile[9] = (xT9, tile9_ev)

            order = list(range(10, NIC)) + list(range(0, 9)) + [9]

            def emit_chains(i, buf, ev_tr, ip):
                nonlocal ev_q_done, ev_k_done, ev_v_done, chain_par, g_chain
                kc = i - 1
                qc = i - 5
                blks = ([0] if qc >= 0 else []) + [1, 2]
                ev_mm = {}
                for blk in blks:
                    lastm = None
                    for m in range(8):
                        lastm = B.add(
                            "t", lambda e, m=m, blk=blk, buf=buf: e.matmul(
                                PP2[:, 512 * blk:512 * (blk + 1)],
                                buf[:, 129 * m + (1 if m < 4 else 0):
                                    129 * m + (1 if m < 4 else 0) + 128],
                                wqb[:, 1536 * m + 512 * blk:1536 * m + 512 * (blk + 1)],
                                start=(m == 0), stop=(m == 7)),
                            waits=[ev_tr, ev_w, g_blk[blk]] if m == 0 else (),
                            signal=ev("T") if m == 7 else None, tag=f"qkv{l}{i}")
                    ev_mm[blk] = lastm
                g_xT[ip] = ev_mm[blks[-1]]
                xw = xtw[0]
                for which, blk, dstb, dc, sb, tabof in (
                        ("v", 2, None, kc, None, 0),
                        ("k", 1, kbf, kc, ksb, 1024),
                        ("q", 0, qbf, qc, qsb, 0)):
                    if which == "q" and qc < 0:
                        continue
                    psl = PP2[:, 512 * blk:512 * (blk + 1)]
                    if which == "v":
                        ev_v_done = B.add(
                            "s", lambda e, psl=psl, dc=dc: e.activation(
                                vly.ap().rearrange("p (c e) -> p c e", e=65)
                                [:, 8 * dc:8 * (dc + 1), 0:64],
                                psl.rearrange("p (c e) -> p c e", e=64),
                                AF.Copy),
                            waits=[ev_mm[2], ev_vones], signal=ev("S"),
                            tag=f"vev{l}{i}")
                        g_blk[2] = ev_v_done
                        continue
                    par = chain_par
                    chain_par ^= 1
                    qrw = qraw[:, 512 * par:512 * (par + 1)]
                    t1s = t1p[:, 512 * par:512 * (par + 1)]
                    sqsl = sqs[:, 512 * par:512 * (par + 1)]
                    nr = nrm[:, 32 * par:32 * (par + 1)]
                    ev_evc = B.add("s", lambda e, psl=psl, qrw=qrw: e.activation(
                        qrw, psl, AF.Copy),
                        waits=[ev_mm[blk], g_chain2[par]], signal=ev("S"),
                        tag=f"ev{l}{i}{which}")
                    g_blk[blk] = ev_evc
                    B.add("v", lambda e, sb=sb, qrw=qrw: e.tensor_mul(
                        zq[:, :], qrw, sb[:, :]),
                        waits=[ev_bc, ev_evc], tag=f"z{l}{i}{which}")
                    ev_sq = B.add("s", lambda e, qrw=qrw, sqsl=sqsl: e.activation(
                        sqsl, qrw, AF.Square),
                        waits=[ev_evc], signal=ev("S"), tag=f"ss{l}{i}{which}")
                    ev_red = B.add("v", lambda e, sqsl=sqsl, nr=nr: e.reduce_sum(
                        nr[:, 0:8],
                        sqsl.rearrange("p (h d) -> p h d", d=64), axis=AX.X),
                        waits=[ev_sq], signal=ev("V"), tag=f"rd{l}{i}{which}")
                    ev_nr2 = B.add("s", lambda e, nr=nr: e.activation(
                        nr[:, 8:16], nr[:, 0:8], AF.Sqrt),
                        waits=[ev_red], signal=ev("S"), tag=f"sq{l}{i}")
                    B.add("v", lambda e, nr=nr: e.tensor_scalar_max(
                        nr[:, 16:24], nr[:, 8:16], 1e-12), waits=[ev_nr2],
                        tag=f"cl{l}{i}")
                    B.add("v", lambda e, nr=nr: e.reciprocal(
                        nr[:, 24:32], nr[:, 16:24]),
                          tag=f"rc{l}{i}")
                    e_xt = B.dma(
                        "y", xw.ap().rearrange("p (f d) -> p f d", d=512),
                        H.xtab[128 * i:128 * (i + 1), tabof:tabof + 1024]
                        .rearrange("p (f d) -> p f d", d=512),
                        waits=[g_chain], signal=ev("DY"), tag=f"xt{l}{i}{which}")
                    co = xw[:, 0:512]
                    si = xw[:, 512:1024]
                    zv = zq.ap().rearrange("p (h s d) -> p h s d", s=2, d=32)
                    sv_ = si.rearrange("p (h s d) -> p h s d", s=2, d=32)
                    t2v = t2p.ap().rearrange("p (h s d) -> p h s d", s=2, d=32)
                    B.add("v", lambda e, co=co, t1s=t1s: e.tensor_mul(
                        t1s, zq[:, :], co),
                          waits=[e_xt, ev_nr2], tag=f"r1{l}{i}")
                    B.add("v", lambda e, sv_=sv_, zv=zv: e.tensor_mul(
                        t2v[:, :, 0, :], zv[:, :, 1, :], sv_[:, :, 0, :]),
                        tag=f"r2{l}{i}")
                    B.add("v", lambda e, sv_=sv_, zv=zv: e.tensor_mul(
                        t2v[:, :, 1, :], zv[:, :, 0, :], sv_[:, :, 1, :]),
                        tag=f"r3{l}{i}")
                    ev_r4 = B.add("v", lambda e, t1s=t1s: e.tensor_add(
                        t1s, t1s, t2p[:, :]),
                        signal=ev("V"), tag=f"r4{l}{i}")
                    lastn = None
                    for hh in range(8):
                        lastn = B.add(
                            "v", lambda e, hh=hh, dstb=dstb, dc=dc, t1s=t1s,
                            nr=nr: e.tensor_single_scalar(
                                dstb[:, 512 * dc + 64 * hh:512 * dc + 64 * (hh + 1)],
                                t1s[:, 64 * hh:64 * (hh + 1)],
                                nr[:, 24 + hh:25 + hh], ALU.mult),
                            waits=[ev_r4] if hh == 0 else (),
                            signal=ev("V") if hh == 7 else None,
                            tag=f"rs{l}{i}{which}")
                    if which == "k":
                        ev_k_done = lastn
                    else:
                        ev_q_done = lastn
                    g_chain = lastn
                    g_chain2[par] = lastn

            pending = None
            for idx, i in enumerate(order):
                ip = idx % 2
                buf = xTt[ip]
                if i == 9:
                    buf = xT9
                    ev_tr = B.add("v", lambda e: e.tensor_copy(
                        xT9.ap().rearrange("p (m c) -> p m c", c=129)[:, :, 0],
                        prev_tile[8][0].ap()
                        .rearrange("p (m c) -> p m c", c=129)[:, :, 128]),
                        waits=[tile9_ev, prev_tile[8][1]], signal=ev("V"),
                        tag=f"seed{l}9f")
                    g_srd[id(prev_tile[8][0])] = ev_tr
                else:
                    if i <= 8:
                        xsrc = xh[:, 1024 * i:1024 * (i + 1)]
                        srcw = [ev_halo[i]]
                    else:
                        wln = (getattr(H, "ev_ln_chunk", {}).get(i - 9, H.ev_xready)
                               if l == 0 else H.ev_xready)
                        xbw = [wln, tile9_ev] if idx == 0 else [wln]
                        e_src = B.add("s", lambda e, i=i: e.activation(
                            xbs[:, :], H.x[:, 1024 * (i - 9):1024 * (i - 8)],
                            AF.Copy), waits=xbw, signal=ev("S"),
                            tag=f"xc{l}{i}")
                        xsrc = xbs[:, :]
                        srcw = [e_src]
                    seed = prev_tile.get(i - 1)
                    ev_tr = emit_tile(i, buf, srcw, xsrc, g_xT[ip], seed)
                    prev_tile[i] = (buf, ev_tr)
                if pending is not None:
                    emit_chains(*pending)
                pending = None if i == 0 else (i, buf, ev_tr, ip)
            emit_chains(*pending)
            ev_stageA = [ev_q_done, ev_k_done, ev_v_done]
        obf = sl.enter_context(nc.sbuf_tensor(f"obf{l}", [128, NQC * 512], BF16))
        with ExitStack() as sC:
            qt2 = [sC.enter_context(nc.sbuf_tensor(f"qt2{l}0", [65, NQC * 128], BF16))] * 2
            kt2 = [sC.enter_context(nc.sbuf_tensor(f"kt2{l}0", [65, NKC * 128], BF16))] * 2
            qab = sC.enter_context(nc.sbuf_tensor(f"qab{l}", [64, 2560], BF16))
            kvx = sC.enter_context(nc.sbuf_tensor(f"kvx{l}", [64, 8], BF16))
            pw = [sC.enter_context(nc.sbuf_tensor(f"pw{l}{i}", [128, 640], BF16))
                  for i in range(2)]
            svc2 = sC.enter_context(nc.sbuf_tensor(f"svc2{l}", [128, 512], F32))
            rs2 = sC.enter_context(nc.sbuf_tensor(f"rs2{l}", [128, 16], F32))
            PT2b = sC.enter_context(nc.psum_tensor(f"PT2b{l}", [128, 2048], BF16))
            SP2 = sC.enter_context(nc.psum_tensor(f"SP2{l}", [128, 640], F32))
            SVq = sC.enter_context(nc.psum_tensor(f"SVq{l}", [65, 512], F32))
            EP2 = sC.enter_context(nc.psum_tensor(f"EP2{l}", [128, 512], F32))
            Cp = sC.enter_context(nc.psum_tensor(f"Cp{l}", [65, 512], F32))

            ev_k1 = B.add("v", lambda e: e.memset(kt2[0][64:65, :], 1.0),
                          waits=list(ev_stageA), signal=ev("V"), tag=f"k1a{l}")
            ev_k1b = ev_k1
            ev_svz2 = B.add("v", lambda e: e.memset(svc2[:, :], 0.0),
                            waits=list(ev_stageA), signal=ev("V"),
                            tag=f"svz{l}")
            g_PT = [ev_stageA[0], ev_stageA[1]]
            g_qt = [ev_k1, ev_k1b]
            g_pw = [[None, None], [None, None]]
            g_sp = None
            g_sp1 = None
            g_sv = ev_svz2
            g_ep = None
            g_cm = {w: None for w in range(5)}
            g_cw = None
            ev_obf = None
            for h in range(8):
                hb = h % 2
                qth, kth = qt2[hb], kt2[hb]
                ev_bq = ev_bk = None
                for (src, nch, dst) in ((qbf, NQC, qth), (kbf, NKC, kth)):
                    gi = 0
                    for c0 in range(0, nch, 4):
                        par = gi % 2
                        gi += 1
                        nn = min(4, nch - c0)
                        lastT = None
                        for j in range(nn):
                            lastT = B.add(
                                "t", lambda e, c0=c0, j=j, par=par, src=src, h=h:
                                e.transpose(
                                    PT2b[0:64, 1024 * par + 128 * j:1024 * par + 128 * (j + 1)],
                                    src[:, 512 * (c0 + j) + 64 * h:512 * (c0 + j) + 64 * (h + 1)],
                                    idb2[:, :]),
                                waits=[g_PT[par], g_qt[hb], ev_stageA[0],
                                       ev_stageA[1]] if j == 0 else (),
                                signal=ev("T") if j == nn - 1 else None,
                                tag=f"bT{l}{h}")
                        if par == 0:
                            g_PT[par] = B.add(
                                "v", lambda e, c0=c0, nn=nn, par=par, dst=dst:
                                e.tensor_copy(
                                    dst[0:64, 128 * c0:128 * (c0 + nn)],
                                    PT2b[0:64, 1024 * par:1024 * par + 128 * nn]),
                                waits=[lastT], signal=ev("V"), tag=f"bTe{l}{h}")
                        else:
                            g_PT[par] = B.add(
                                "s", lambda e, c0=c0, nn=nn, par=par, dst=dst:
                                e.activation(
                                    dst[0:64, 128 * c0:128 * (c0 + nn)],
                                    PT2b[0:64, 1024 * par:1024 * par + 128 * nn],
                                    AF.Copy),
                                waits=[lastT], signal=ev("S"), tag=f"bTe{l}{h}")
                    if src is qbf:
                        ev_bq = [g_PT[0], g_PT[1]]
                    else:
                        ev_bk = [g_PT[0], g_PT[1]]
                ev_cw = {}
                for w in range(5):
                    ev_kv = B.add("v", lambda e, kth=kth, w=w: e.reduce_max(
                        kvx[0:64, w:w + 1], kth[0:64, 512 * w:512 * w + 1024],
                        axis=AX.X, apply_absolute_value=True),
                        waits=ev_bk + [g_cm[w]], signal=ev("V"),
                        tag=f"kv{l}{h}{w}")
                    ev_qa = B.add("s", lambda e, qth=qth, w=w: e.activation(
                        qab[0:64, 512 * w:512 * (w + 1)],
                        qth[0:64, 512 * w:512 * (w + 1)], AF.Abs),
                        waits=ev_bq + [g_cm[w]], signal=ev("S"),
                        tag=f"qa{l}{h}{w}")
                    ev_cm = B.add("t", lambda e, w=w: e.matmul(
                        Cp[64:65, :], kvx[0:64, w:w + 1],
                        qab[0:64, 512 * w:512 * (w + 1)],
                        start=True, stop=True, tile_position=(0, 64)),
                        waits=[ev_kv, ev_qa, g_cw], signal=ev("T"),
                        tag=f"cm{l}{h}{w}")
                    g_cm[w] = ev_cm
                    ev_cw[w] = g_cw = B.add(
                        "s", lambda e, qth=qth, w=w: e.activation(
                            qth[64:65, 512 * w:512 * (w + 1)], Cp[64:65, :],
                            AF.Copy, scale=-1.0),
                        waits=[ev_cm], signal=ev("S"), tag=f"cw{l}{h}{w}")
                for w in range(5):
                    for b in range(4):
                        bp = b % 2
                        lastm = None
                        ev_s3 = None
                        for a in range(b, b + 5):
                            sslot = a - b
                            mk = []
                            if a == b:
                                mk.append(mlo)
                            if a == b + 4:
                                mk.append(mhi)
                            if w == 1 and a <= 3:
                                mk.append(mpo)
                            w_st = ()
                            if a == b:
                                w_st = [ev_cw[w], g_sp, g_pw[bp][0]]
                            elif a == b + 4:
                                w_st = [g_sp1, g_pw[bp][1]]
                            sig_a = a in (b + 3, b + 4)
                            lastm = B.add(
                                "t", lambda e, w=w, a=a, b=b, sslot=sslot,
                                kth=kth, qth=qth, stp=(not mk): e.matmul(
                                    SP2[:, 128 * sslot:128 * (sslot + 1)],
                                    kth[0:65, 128 * (4 * w + a):128 * (4 * w + a + 1)],
                                    qth[0:65, 128 * (4 * w + b):128 * (4 * w + b + 1)],
                                    start=True, stop=stp),
                                waits=w_st,
                                signal=ev("T") if (sig_a and not mk) else None,
                                tag=f"st{l}{h}{w}")
                            for q_, mt in enumerate(mk):
                                fin = (sig_a and q_ == len(mk) - 1)
                                lastm = B.add(
                                    "t", lambda e, sslot=sslot, mt=mt,
                                    last=(q_ == len(mk) - 1): e.matmul(
                                        SP2[:, 128 * sslot:128 * (sslot + 1)],
                                        mt[:, :], idb2[:, :],
                                        start=False, stop=last),
                                    signal=ev("T") if fin else None,
                                    tag=f"mk{l}{h}{w}")
                            if a == b + 3:
                                ev_s3 = lastm
                        ev_sm = lastm
                        ev_ex = B.add("s", lambda e, bp=bp: e.activation(
                            pw[bp][:, 0:512], SP2[:, 0:512], AF.Exp, scale=8.0),
                            waits=[ev_s3, g_pw[bp][0]], signal=ev("S"),
                            tag=f"xx{l}{h}{w}")
                        g_sp = ev_ex
                        ev_ex1 = B.add("s", lambda e, bp=bp: e.activation(
                            pw[bp][:, 512:640], SP2[:, 512:640], AF.Exp,
                            scale=8.0),
                            waits=[ev_sm, g_pw[bp][1]], signal=ev("S"),
                            tag=f"xy{l}{h}{w}")
                        g_sp1 = ev_ex1
                        lastv = None
                        sv3 = None
                        for a in range(b, b + 5):
                            sslot = a - b
                            svw = ()
                            if a == b:
                                svw = [ev_ex, g_sv]
                            elif a == b + 4:
                                svw = [ev_ex1]
                            lastv = B.add(
                                "t", lambda e, w=w, a=a, b=b, sslot=sslot, h=h, bp=bp:
                                e.matmul(
                                    SVq[0:65, 128 * b:128 * (b + 1)],
                                    vly[:, 520 * (4 * w + a) + 65 * h:
                                        520 * (4 * w + a) + 65 * (h + 1)],
                                    pw[bp][:, 128 * sslot:128 * (sslot + 1)],
                                    start=(a == b), stop=(a == b + 4)),
                                waits=svw,
                                signal=ev("T") if a in (b + 3, b + 4) else None,
                                tag=f"sv2{l}{h}{w}")
                            if a == b + 3:
                                sv3 = lastv
                        last_sv = lastv
                        g_pw[bp] = [sv3, lastv]
                    ecp = B.add("v", lambda e: e.tensor_copy(
                        svc2[0:65, :], SVq[0:65, :]),
                        waits=[last_sv], signal=ev("V"), tag=f"sve{l}{h}{w}")
                    g_sv = ecp
                    lastT = None
                    for j in range(4):
                        lastT = B.add(
                            "t", lambda e, j=j: e.transpose(
                                EP2[:, 128 * j:128 * (j + 1)],
                                svc2[:, 128 * j:128 * (j + 1)], idf2[:, :]),
                            waits=[ecp, g_ep] if j == 0 else (),
                            signal=ev("T") if j == 3 else None, tag=f"eT2{l}{h}{w}")
                    EPr = EP2.ap().rearrange("p (j x) -> p j x", x=128)
                    B.add("v", lambda e, EPr=EPr: e.tensor_copy(
                        rs2[:, 0:4], EPr[:, :, 64]), waits=[lastT],
                        tag=f"er{l}{h}{w}")
                    B.add("v", lambda e: e.reciprocal(rs2[:, 4:8], rs2[:, 0:4]),
                          tag=f"er2{l}{h}{w}")
                    evo = None
                    for j in range(4):
                        qc = 4 * w + j
                        evo = B.add(
                            "v", lambda e, j=j, qc=qc, h=h: e.tensor_single_scalar(
                                obf[:, 512 * qc + 64 * h:512 * qc + 64 * (h + 1)],
                                EP2[:, 128 * j:128 * j + 64],
                                rs2[:, 4 + j:5 + j], ALU.mult),
                            signal=ev("V") if j == 3 else None, tag=f"ob{l}{h}{w}")
                    g_ep = evo
                    ev_obf = evo
                g_qt[hb] = ev_sm
        with ExitStack() as sD:
            oTt = [sD.enter_context(nc.sbuf_tensor(f"oT{l}{i}", [128, 512], BF16))
                   for i in range(2)]
            wob = sD.enter_context(nc.sbuf_tensor(f"wob{l}", [128, 4 * 1024], BF16))
            PT2c = sD.enter_context(nc.psum_tensor(f"PT2c{l}", [128, 2048], BF16))
            PW = [sD.enter_context(nc.psum_tensor(f"PWD{l}{i}", [128, 512], F32))
                  for i in range(2)]
            ev_wo = B.dma("g", wob.ap().rearrange("p (m o) -> p m o", o=1024),
                          H.wo[l].rearrange("(m p) o -> p m o", p=128),
                          waits=[ev_obf], signal=ev("DG"), tag=f"wo{l}")
            g_PT = [ev_obf, ev_obf]
            g_oT = [None, None]
            g_pw2 = [None, None]
            ev_xmid = None
            H.ev_xmid_chunk = {}
            for qc in range(3, NQC):
                par = qc % 2
                buf = oTt[par]
                lastT = None
                for j in range(4):
                    lastT = B.add(
                        "t", lambda e, qc=qc, j=j, par=par: e.transpose(
                            PT2c[:, 1024 * par + 128 * j:1024 * par + 128 * (j + 1)],
                            obf[:, 512 * qc + 128 * j:512 * qc + 128 * (j + 1)],
                            idb2[:, :]),
                        waits=[ev_obf, g_PT[par], g_oT[par]] if j == 0 else (),
                        signal=ev("T") if j == 3 else None, tag=f"oT{l}{qc}")
                g_PT[par] = B.add(
                    "v", lambda e, par=par, buf=buf: e.tensor_copy(
                        buf[:, :], PT2c[:, 1024 * par:1024 * par + 512]),
                    waits=[lastT], signal=ev("V"), tag=f"oTe{l}{qc}")
                ev_mm = []
                for eb in range(2):
                    lastm = None
                    for ic in range(4):
                        lastm = B.add(
                            "t", lambda e, eb=eb, ic=ic, buf=buf: e.matmul(
                                PW[eb][:, :], buf[:, 128 * ic:128 * (ic + 1)],
                                wob[:, 1024 * ic + 512 * eb:1024 * ic + 512 * (eb + 1)],
                                start=(ic == 0), stop=(ic == 3)),
                            waits=[g_PT[par], ev_wo, g_pw2[eb]] if ic == 0 else (),
                            signal=ev("T") if ic == 3 else None, tag=f"wom{l}{qc}")
                    ev_mm.append(lastm)
                g_oT[par] = ev_mm[-1]
                if qc == 3:
                    dst = xh8
                    dof = 0
                else:
                    dst = H.x
                    dof = 1024 * (qc - 4)
                for eb in range(2):
                    ww = [ev_mm[eb]]
                    if qc == 3:
                        ww.append(ev_halo[-1])
                    ev_xmid = B.add(
                        "v", lambda e, dst=dst, dof=dof, eb=eb: e.tensor_add(
                            dst[:, dof + 512 * eb:dof + 512 * (eb + 1)],
                            dst[:, dof + 512 * eb:dof + 512 * (eb + 1)],
                            PW[eb][:, :]),
                        waits=ww, signal=ev("V"), tag=f"xm{l}{qc}")
                    g_pw2[eb] = ev_xmid
                H.ev_xmid_chunk[qc] = ev_xmid
            H.ev_xmid = ev_xmid
    _emit_ffn(nc, B, st, H, l, xh8, ones1, idb2)


def _emit_ffn(nc, B, st, H, l, xh8, ones1, idb2):
    ev = B.ev
    NB2 = 22
    with ExitStack() as sE:
        h2bf = sE.enter_context(nc.sbuf_tensor(f"h2bf{l}", [128, 17 * D_], BF16))
        h2T = sE.enter_context(nc.sbuf_tensor(f"h2T{l}", [128, 8 * 2049], BF16))
        gsb = sE.enter_context(nc.sbuf_tensor(f"gsb{l}", [128, NB2 * 512], BF16))
        NW = 6
        w1s = [sE.enter_context(nc.sbuf_tensor(f"w1s{l}{i}", [128, 1024], BF16))
               for i in range(NW)]
        w2s = [sE.enter_context(nc.sbuf_tensor(f"w2s{l}{i}", [128, 512], BF16))
               for i in range(NW)]
        st2 = sE.enter_context(nc.sbuf_tensor(f"st2{l}", [128, 4 * 18], F32))
        sh2 = sE.enter_context(nc.sbuf_tensor(f"sh2{l}", [128, 2 * 18], F32))
        stm = sE.enter_context(nc.sbuf_tensor(f"stm{l}", [128, 8 * 18], F32))
        b1pp = sE.enter_context(nc.sbuf_tensor(f"b1pp{l}", [128, 43], F32))
        lgb = sE.enter_context(nc.sbuf_tensor(f"lgb{l}", [128, 16], F32))
        gel = sE.enter_context(nc.sbuf_tensor(f"gel{l}", [64, 512], F32))
        b2r = sE.enter_context(nc.sbuf_tensor(f"b2r{l}", [1, 1024], BF16))
        sqs2 = sE.enter_context(nc.sbuf_tensor(f"sqs2{l}", [128, 512], F32))
        PT2d = sE.enter_context(nc.psum_tensor(f"PT2d{l}", [128, 2048], BF16))
        PW1 = sE.enter_context(nc.psum_tensor(f"PW1{l}", [128, 1024], F32))
        PW2 = sE.enter_context(nc.psum_tensor(f"PW2{l}", [128, 2048], F32))

        e_b1 = B.dma("y", b1pp[:, :],
                     H.b1P[l].rearrange("(k p) -> p k", p=128),
                     waits=[H.ev_xmid], signal=ev("DY"), tag=f"b1{l}",
                     allow_slow_non_contiguous=True)
        e_lg = B.dma("y", lgb[:, 0:8],
                     H.lng[l].rearrange("(m p) -> p m", p=128),
                     waits=[H.ev_xmid], signal=ev("DY"), tag=f"lg{l}",
                     allow_slow_non_contiguous=True)
        e_lb = B.dma("y", lgb[:, 8:16],
                     H.lnb[l].rearrange("(m p) -> p m", p=128),
                     waits=[H.ev_xmid], signal=ev("DY"), tag=f"lb{l}",
                     allow_slow_non_contiguous=True)
        e_b2 = B.dma("g", b2r[0:1, :], H.b2[l:l + 1, :],
                     waits=[H.ev_xmid], signal=ev("DG"), tag=f"b2{l}")

        ev_stz = B.add("v", lambda e: e.memset(st2[:, :], 0.0),
                       signal=ev("V"), tag=f"stz{l}")
        ev_st = None
        for c in range(17):
            src = xh8[:, :] if c == 0 else H.x[:, 1024 * (c - 1):1024 * c]
            wmc = H.ev_xmid_chunk.get(c + 3, H.ev_xmid)
            B.add("v", lambda e, src=src, c=c: e.reduce_sum(
                st2[:, c:c + 1], src[:, 0:512], axis=AX.X),
                waits=[wmc], tag=f"s1a{l}{c}")
            B.add("v", lambda e, src=src, c=c: e.reduce_sum(
                st2[:, 18 + c:19 + c], src[:, 512:1024], axis=AX.X),
                tag=f"s1b{l}{c}")
            B.add("s", lambda e, src=src, c=c: e.activation(
                sqs2[:, :], src[:, 0:512], AF.Square,
                accum_out=st2[:, 36 + c:37 + c]),
                waits=[wmc, ev_stz], tag=f"q1a{l}{c}")
            ev_st = B.add("s", lambda e, src=src, c=c: e.activation(
                sqs2[:, :], src[:, 512:1024], AF.Square,
                accum_out=st2[:, 54 + c:55 + c]),
                signal=ev("S") if c == 16 else None, tag=f"q1b{l}{c}")
        ev_z0 = B.add("v", lambda e: e.memset(sh2[:, :], 0.0),
                      signal=ev("V"), tag=f"shz{l}")
        d1 = B.dma("y", sh2[1:128, 0:18], st2[0:127, 18:36],
                   waits=[ev_st, ev_z0], signal=ev("DY"), tag=f"shS{l}")
        d2 = B.dma("y", sh2[0:1, 1:18], st2[127:128, 18:35],
                   waits=[ev_st, ev_z0], signal=ev("DY"), tag=f"shSw{l}")
        d3 = B.dma("y", sh2[1:128, 18:36], st2[0:127, 54:72],
                   waits=[ev_st, ev_z0], signal=ev("DY"), tag=f"shQ{l}")
        d4 = B.dma("y", sh2[0:1, 19:36], st2[127:128, 54:71],
                   waits=[ev_st, ev_z0], signal=ev("DY"), tag=f"shQw{l}")
        B.add("v", lambda e: e.tensor_add(stm[:, 0:18], sh2[:, 0:18],
                                          st2[:, 0:18]),
              waits=[d1, d2], tag=f"cS{l}")
        B.add("v", lambda e: e.tensor_add(stm[:, 18:36], sh2[:, 18:36],
                                          st2[:, 36:54]),
              waits=[d3, d4], tag=f"cQ{l}")
        B.add("v", lambda e: e.tensor_scalar_mul(stm[:, 36:54], stm[:, 0:18],
                                                 -1.0 / D_), tag=f"mu{l}")
        B.add("v", lambda e: e.tensor_scalar_mul(stm[:, 0:18], stm[:, 18:36],
                                                 1.0 / D_), tag=f"e2{l}")
        B.add("v", lambda e: e.tensor_mul(stm[:, 18:36], stm[:, 36:54],
                                          stm[:, 36:54]), tag=f"m2{l}")
        ev_var = B.add("v", lambda e: e.scalar_tensor_tensor(
            stm[:, 0:18], stm[:, 0:18], LNEPS, stm[:, 18:36],
            ALU.add, ALU.subtract), signal=ev("V"), tag=f"vr{l}")
        ev_sd = B.add("s", lambda e: e.activation(stm[:, 18:36], stm[:, 0:18],
                                                  AF.Sqrt),
                      waits=[ev_var], signal=ev("S"), tag=f"sd{l}")
        B.add("v", lambda e: e.reciprocal(stm[:, 54:72], stm[:, 18:36]),
              waits=[ev_sd], tag=f"rs{l}")
        ev_z1 = B.add("v", lambda e: e.memset(stm[:, 72:108], 1.0),
                      signal=ev("V"), tag=f"suz{l}")
        ev_rs = B.last["V"]
        f1 = B.dma("y", stm[0:127, 72:90], stm[1:128, 36:54],
                   waits=[ev_z1, ev_rs], signal=ev("DY"), tag=f"su1{l}")
        f2 = B.dma("y", stm[0:127, 90:108], stm[1:128, 54:72],
                   waits=[ev_z1, ev_rs], signal=ev("DY"), tag=f"su2{l}")
        f3 = B.dma("y", stm[127:128, 72:89], stm[0:1, 37:54],
                   waits=[ev_z1, ev_rs], signal=ev("DY"), tag=f"su3{l}")
        f4 = B.dma("y", stm[127:128, 90:107], stm[0:1, 55:72],
                   waits=[ev_z1, ev_rs], signal=ev("DY"), tag=f"su4{l}")
        ev_h2 = None
        ev_apply = {}
        for c in range(17):
            src = xh8[:, :] if c == 0 else H.x[:, 1024 * (c - 1):1024 * c]
            B.add("v", lambda e, src=src, c=c: e.tensor_scalar(
                h2bf[:, 1024 * c:1024 * c + 512], src[:, 0:512],
                stm[:, 36 + c:37 + c], stm[:, 54 + c:55 + c],
                ALU.add, ALU.mult), waits=[f1, f2, f3, f4], tag=f"apA{l}{c}")
            ev_h2 = B.add("v", lambda e, src=src, c=c: e.tensor_scalar(
                h2bf[:, 1024 * c + 512:1024 * (c + 1)], src[:, 512:1024],
                stm[:, 72 + c:73 + c], stm[:, 90 + c:91 + c],
                ALU.add, ALU.mult),
                signal=ev("V"), tag=f"apB{l}{c}")
            ev_apply[c] = ev_h2
        g_PT = [ev_apply[0], ev_apply[0]]
        ev_h2T = None
        ev_hT_chunk = {}
        for c in range(17):
            for m in range(8):
                par = m % 2
                tt = B.add("t", lambda e, c=c, m=m, par=par: e.transpose(
                    PT2d[:, 1024 * par:1024 * par + 128],
                    h2bf[:, 1024 * c + 128 * m:1024 * c + 128 * (m + 1)],
                    idb2[:, :]),
                    waits=[ev_apply[c], g_PT[par]], signal=ev("T"),
                    tag=f"hT{l}{c}")
                if c == 0:
                    dst = h2T[:, 2049 * m:2049 * m + 1]
                    srcp = PT2d[:, 1024 * par + 127:1024 * par + 128]
                else:
                    dst = h2T[:, 2049 * m + 1 + 128 * (c - 1):
                              2049 * m + 1 + 128 * c]
                    srcp = PT2d[:, 1024 * par:1024 * par + 128]
                g_PT[par] = B.add(
                    "v", lambda e, dst=dst, srcp=srcp, m=m: e.tensor_scalar(
                        dst, srcp, lgb[:, m:m + 1], lgb[:, 8 + m:9 + m],
                        ALU.mult, ALU.add),
                    waits=[tt, e_lg, e_lb], signal=ev("V"), tag=f"hTe{l}{c}")
                ev_h2T = g_PT[par]
            ev_hT_chunk[c] = ev_h2T
        g_w1 = [H.ev_xmid] * NW
        g_w2 = [H.ev_xmid] * NW
        g_prod = None
        g_pw1 = [None, None]
        g_gsb = None
        ev_res = None
        for tb in range(4):
            for k in range(43):
                par = k % NW
                e_w = B.dma("g", w1s[par].ap().rearrange("p (m c) -> p m c", c=128),
                            H.w1P[l][:, 128 * k:128 * (k + 1)]
                            .rearrange("(m p) c -> p m c", p=128),
                            waits=[g_w1[par]], signal=ev("DG"), tag=f"w1d{l}")
                kb = k % 2
                lastm = None
                for m in range(8):
                    off = 2049 * m + 512 * tb + (1 if m < 4 else 0)
                    lastm = B.add(
                        "t", lambda e, m=m, par=par, kb=kb, off=off: e.matmul(
                            PW1[:, 512 * kb:512 * (kb + 1)],
                            w1s[par][:, 128 * m:128 * (m + 1)],
                            h2T[:, off:off + 512],
                            start=(m == 0), stop=(m == 7)),
                        waits=([e_w, ev_hT_chunk[min(4 * tb + 4, 16)],
                                g_pw1[kb]] if m == 0 else ()),
                        signal=ev("T") if m == 7 else None, tag=f"w1m{l}{k}")
                g_w1[par] = lastm
                eg = B.add("s", lambda e, kb=kb, k=k: e.activation(
                    gel[0:64, :], PW1[64:128, 512 * kb:512 * (kb + 1)],
                    AF.Gelu, bias=b1pp[64:128, k:k + 1]),
                    waits=[lastm, e_b1, g_prod], signal=ev("S"),
                    tag=f"gel{l}{k}")
                mt, half = k // 2, k % 2
                g_prod = g_pw1[kb] = B.add(
                    "v", lambda e, kb=kb, k=k, mt=mt, half=half:
                    e.scalar_tensor_tensor(
                        gsb[64 * half:64 * half + 64, 512 * mt:512 * (mt + 1)],
                        PW1[0:64, 512 * kb:512 * (kb + 1)],
                        b1pp[0:64, k:k + 1], gel[0:64, :],
                        ALU.add, ALU.mult),
                    waits=[eg, g_gsb], signal=ev("V"), tag=f"prd{l}{k}")
            ev_gsb = g_pw1[par]
            ev_z = B.add("v", lambda e: e.memset(
                gsb[64:128, 512 * 21:512 * 22], 0.0),
                waits=[ev_gsb], signal=ev("V"), tag=f"gz{l}")
            for eb in range(2):
                for kp2 in range(NB2):
                    par = kp2 % NW
                    e_w2 = B.dma("g", w2s[par][:, :],
                                 H.w2P[l][128 * kp2:128 * (kp2 + 1),
                                          512 * eb:512 * (eb + 1)],
                                 waits=[g_w2[par]], signal=ev("DG"),
                                 tag=f"w2d{l}")
                    lastm = None
                    for ts_ in range(4):
                        lastm = B.add(
                            "t", lambda e, ts_=ts_, kp2=kp2, par=par: e.matmul(
                                PW2[:, 512 * ts_:512 * (ts_ + 1)],
                                gsb[:, 512 * kp2 + 128 * ts_:512 * kp2 + 128 * (ts_ + 1)],
                                w2s[par][:, :],
                                start=(kp2 == 0), stop=False),
                            waits=([e_w2, ev_z, ev_res] if ts_ == 0 else ()),
                            signal=ev("T") if ts_ == 3 else None,
                            tag=f"w2m{l}{kp2}")
                    g_w2[par] = lastm
                lastb = None
                for ts_ in range(4):
                    lastb = B.add(
                        "t", lambda e, ts_=ts_, eb=eb: e.matmul(
                            PW2[:, 512 * ts_:512 * (ts_ + 1)],
                            ones1[0:1, :],
                            b2r[0:1, 512 * eb:512 * (eb + 1)],
                            start=False, stop=True),
                        waits=[e_b2] if ts_ == 0 else (),
                        signal=ev("T") if ts_ == 3 else None, tag=f"b2m{l}")
                for ts_ in range(4):
                    cc = 4 * tb + ts_
                    ev_res = B.add(
                        "v", lambda e, cc=cc, eb=eb, ts_=ts_: e.tensor_add(
                            H.x[:, 1024 * cc + 512 * eb:1024 * cc + 512 * (eb + 1)],
                            H.x[:, 1024 * cc + 512 * eb:1024 * cc + 512 * (eb + 1)],
                            PW2[:, 512 * ts_:512 * (ts_ + 1)]),
                        waits=[lastb], signal=ev("V"), tag=f"xr{l}{cc}{eb}")
            g_gsb = ev_res
        H.ev_l_done = ev_res


def _build_layer(l):
    nc = bass.Bass("TRN2", target_bir_lowering=False)
    H = NS()
    H.xin = _din(nc, "xin", [T_, D_], F32)
    H.halo = _din(nc, "halo", [1152, D_], F32)
    H.wqkv = _din(nc, "wqkv", [2, D_, 1536], F32)
    H.wo = _din(nc, "woM", [2, 512, D_], F32)
    H.w1P = _din(nc, "w1P", [2, D_, HID2], F32)
    H.b1P = _din(nc, "b1P", [2, HID2], F32)
    H.w2P = _din(nc, "w2P", [2, 22 * 128, D_], F32)
    H.b2 = _din(nc, "b2", [2, D_], F32)
    H.lng = _din(nc, "lng", [2, D_], F32)
    H.lnb = _din(nc, "lnb", [2, D_], F32)
    H.qsw = _din(nc, "qsw", [2, 512], F32)
    H.ksw = _din(nc, "ksw", [2, 512], F32)
    H.xtab = _din(nc, "xtab", [3200, 2048], BF16)
    H.msk_lo = _din(nc, "msk_lo", [128, 128], BF16)
    H.msk_hi = _din(nc, "msk_hi", [128, 128], BF16)
    H.prevoff = _din(nc, "prevoff", [128, 128], BF16)
    H.idbf = _din(nc, "identbf", [128, 128], BF16)
    H.idf = _din(nc, "identf32", [128, 128], F32)
    H.ones128 = _din(nc, "ones128", [1, 128], BF16)
    H.ones128f = _din(nc, "ones128f", [1, 128], F32)
    if l == 1:
        H.x1own = _din(nc, "x1own", [T_, D_], F32)
    H.out = nc.dram_tensor("out", [T_, D_], F32, kind="ExternalOutput").ap()

    B = Builder(nc)
    ev = B.ev
    with ExitStack() as st:
        H.x = st.enter_context(nc.sbuf_tensor("x_stream", [128, 16 * D_], F32,
                                              side="right"))
        xh = st.enter_context(nc.sbuf_tensor("xh", [128, 9 * D_], BF16))
        xh8 = st.enter_context(nc.sbuf_tensor("xh8", [128, D_], F32,
                                              side="right"))
        ones1 = st.enter_context(nc.sbuf_tensor("ones1", [128, 128], BF16))
        ones1f = st.enter_context(nc.sbuf_tensor("ones1f", [1, 128], F32))
        idb2 = st.enter_context(nc.sbuf_tensor("idb2", [128, 128], BF16))
        idf2 = st.enter_context(nc.sbuf_tensor("idf2", [128, 128], F32))
        mlo = st.enter_context(nc.sbuf_tensor("mlo", [128, 128], BF16))
        mhi = st.enter_context(nc.sbuf_tensor("mhi", [128, 128], BF16))
        mpo = st.enter_context(nc.sbuf_tensor("mpo", [128, 128], BF16))
        csts = [
            B.dma("y", ones1[0:1, :], H.ones128, signal=ev("DY"), tag="c2"),
            B.dma("y", ones1f[0:1, :], H.ones128f, signal=ev("DY"), tag="c2f"),
            B.dma("y", idb2[:, :], H.idbf, signal=ev("DY"), tag="c3"),
            B.dma("y", idf2[:, :], H.idf, signal=ev("DY"), tag="c4"),
            B.dma("y", mlo[:, :], H.msk_lo, signal=ev("DY"), tag="c5"),
            B.dma("y", mhi[:, :], H.msk_hi, signal=ev("DY"), tag="c6"),
            B.dma("y", mpo[:, :], H.prevoff, signal=ev("DY"), tag="c7"),
        ]
        H.ev_xready = B.dma("y", _v3(H.x.ap(), D_),
                            H.xin.rearrange("(c p) d -> p c d", p=128),
                            signal=ev("DY"), tag="xload")
        ev_hb = B.dma("g", _v3(xh.ap(), D_),
                      H.halo.rearrange("(c p) d -> p c d", p=128),
                      signal=ev("DG"), tag="hld")
        ev_h8 = B.dma("y", xh8[:, :],
                      H.halo.rearrange("(c p) d -> p c d", p=128)[:, 8, :],
                      signal=ev("DY"), tag="h8ld")
        ev_halo = [ev_hb] * 9 + [ev_h8]
        H.ev_xmid = None
        guard = csts
        _emit_layer(nc, B, st, H, l, xh, xh8, ones1, ones1f, idb2, idf2,
                    (mlo, mhi, mpo), ev_halo, guard)
        if l == 0:
            evs = None
            for c in range(16):
                evs = B.dma("y", H.out.rearrange("(c p) d -> p c d", p=128)[:, c, :],
                            H.x[:, 1024 * c:1024 * (c + 1)],
                            waits=[H.ev_l_done], signal=ev("DY"), tag=f"o{c}")
        else:
            _emit_final_ln(nc, B, st, H, x1dram=H.x1own)
        B.finalize(st)
    return nc


def _emit_final_ln(nc, B, st, H, x1dram):
    ev = B.ev
    with ExitStack() as sf:
        xsc = [sf.enter_context(nc.sbuf_tensor(f"xscF{i}", [128, D_], F32))
               for i in range(2)]
        sqs = sf.enter_context(nc.sbuf_tensor("sqsF", [128, D_], F32))
        stt = sf.enter_context(nc.sbuf_tensor("sttF", [128, 16], F32))
        g_x = [None, None]
        for c in range(16):
            xc = H.x[:, 1024 * c:1024 * (c + 1)]
            lx = B.dma("y", xsc[c % 2][:, :],
                       x1dram.rearrange("(c p) d -> p c d", p=128)[:, c, :],
                       waits=[w for w in (H.ev_l_done, g_x[c % 2]) if w],
                       signal=ev("DY"), tag=f"fx{c}")
            ra = B.add("v", lambda e, xc=xc, c=c: e.tensor_add(
                xc, xc, xsc[c % 2][:, :]), waits=[lx], signal=ev("V"),
                tag=f"fra{c}")
            g_x[c % 2] = ra
            B.add("v", lambda e, xc=xc: e.reduce_sum(stt[:, 0:1], xc, axis=AX.X),
                  waits=[ra], tag=f"fls{c}")
            ev_sq = B.add("s", lambda e, xc=xc: e.activation(
                sqs[:, :], xc, AF.Square, accum_out=stt[:, 1:2]),
                waits=[ra], signal=ev("S"), tag=f"flq{c}")
            B.add("v", lambda e: e.tensor_scalar_mul(stt[:, 2:3], stt[:, 0:1],
                                                     -1.0 / D_), tag="f1")
            B.add("v", lambda e: e.tensor_scalar_mul(stt[:, 3:4], stt[:, 1:2],
                                                     1.0 / D_),
                  waits=[ev_sq], tag="f2")
            B.add("v", lambda e: e.tensor_mul(stt[:, 4:5], stt[:, 2:3],
                                              stt[:, 2:3]), tag="f3")
            ev_var = B.add("v", lambda e: e.scalar_tensor_tensor(
                stt[:, 5:6], stt[:, 3:4], LNEPS, stt[:, 4:5],
                ALU.add, ALU.subtract), signal=ev("V"), tag="f4")
            ev_sd = B.add("s", lambda e: e.activation(stt[:, 6:7], stt[:, 5:6],
                                                      AF.Sqrt),
                          waits=[ev_var], signal=ev("S"), tag="f5")
            B.add("v", lambda e: e.reciprocal(stt[:, 7:8], stt[:, 6:7]),
                  waits=[ev_sd], tag="f6")
            ev_ln = B.add("v", lambda e, xc=xc: e.tensor_scalar(
                xc, xc, stt[:, 2:3], stt[:, 7:8], ALU.add, ALU.mult),
                signal=ev("V"), tag=f"flap{c}")
            B.dma("y", H.out.rearrange("(c p) d -> p c d", p=128)[:, c, :],
                  xc, waits=[ev_ln], signal=ev("DY"), tag=f"fout{c}")


NO_RD = True

def _build(stage):
    nc = bass.Bass("TRN2", target_bir_lowering=False,
                   detect_race_conditions=not NO_RD)
    H = NS()
    H.stage = stage
    H.emb = _din(nc, "emb_full", [S_, D_], F32)
    H.embq = _din(nc, "embq", [T_, D_], F32)
    H.embo = _din(nc, "emb_own", [T_, D_], F32)
    H.WqT = _din(nc, "WqT", [D_, D_], F32)
    H.WkT = _din(nc, "WkT", [D_, D_], F32)
    H.WvT = _din(nc, "WvT", [D_, D_], F32)
    H.WrT = _din(nc, "WrT", [D_, D_], F32)
    H.rc1f = _din(nc, "rc1f", [256, D_], F32)
    H.rs1f = _din(nc, "rs1f", [256, D_], F32)
    H.rcq = _din(nc, "rcq", [128, D_], F32)
    H.rsq = _din(nc, "rsq", [128, D_], F32)
    H.idbf = _din(nc, "identbf", [128, 128], BF16)
    H.idf = _din(nc, "identf32", [128, 128], F32)
    H.out = nc.dram_tensor("out", [T_, D_], F32, kind="ExternalOutput").ap()
    H.vscr = nc.dram_tensor("vscr", [S_, D_], BF16, kind="Internal").ap()
    H.hoff = 0
    H.wqkv = _din(nc, "wqkv", [2, D_, 1536], F32)
    H.wo = _din(nc, "woM", [2, 512, D_], F32)
    H.w1P = _din(nc, "w1P", [2, D_, HID2], F32)
    H.b1P = _din(nc, "b1P", [2, HID2], F32)
    H.w2P = _din(nc, "w2P", [2, 22 * 128, D_], F32)
    H.b2 = _din(nc, "b2", [2, D_], F32)
    H.lng = _din(nc, "lng", [2, D_], F32)
    H.lnb = _din(nc, "lnb", [2, D_], F32)
    H.qsw = _din(nc, "qsw", [2, 512], F32)
    H.ksw = _din(nc, "ksw", [2, 512], F32)
    H.xtab = _din(nc, "xtab", [3200, 2048], BF16)
    H.gate = _din(nc, "gate", [128, 1], F32)
    H.wsend = _din(nc, "wsend", [128, 1], F32)
    H.ones128 = _din(nc, "ones128", [1, 128], BF16)
    H.ones128f = _din(nc, "ones128f", [1, 128], F32)
    H.msk_lo = _din(nc, "msk_lo", [128, 128], BF16)
    H.msk_hi = _din(nc, "msk_hi", [128, 128], BF16)
    H.prevoff = _din(nc, "prevoff", [128, 128], BF16)
    H.fthr = _din(nc, "fthr", [1, 16], mybir.dt.int32)
    H.fmask = _din(nc, "fmask", [1, 16], mybir.dt.int32)
    H.ixoff = _din(nc, "ixoff", [128, 19], mybir.dt.int32)
    H.epochs = _din(nc, "epochs", [128, 6], mybir.dt.int32)

    B = Builder(nc)
    with ExitStack() as st:
        _emit_phase1(nc, B, st, H)
        if stage != "p1":
            _emit_phase2(nc, B, st, H)
        B.finalize(st)
    return nc



def _rope1_tables():
    inv = 1.0 / (10000.0 ** (np.arange(0, ROT1, 2, dtype=np.float64) / ROT1))
    r = np.arange(256)
    cb = np.arange(16)
    d = np.arange(64)
    p = 16 * (r[:, None, None] % 256) + cb[None, :, None] + 0 * d[None, None, :]
    fi = np.zeros(64)
    fi[0:ROT1:2] = inv
    fi[1:ROT1:2] = inv
    ang = p * fi[None, None, :]
    cos = np.cos(ang)
    sin = np.sin(ang)
    cos[:, :, ROT1:] = 1.0
    ssin = np.zeros_like(sin)
    ssin[:, :, 0:ROT1:2] = -sin[:, :, 0:ROT1:2]
    ssin[:, :, 1:ROT1:2] = sin[:, :, 1:ROT1:2]
    return (cos.reshape(256, 1024).astype(np.float32),
            ssin.reshape(256, 1024).astype(np.float32))


def _prep_core_inputs(inp, b, u):
    m = {}
    emb = np.ascontiguousarray(np.asarray(inp["L0_attn_emb"][b], np.float32))
    m["emb_full"] = emb
    m["emb_own"] = np.ascontiguousarray(emb[T_ * u:T_ * (u + 1)])
    qrows = np.zeros((T_, D_), np.float32)
    for h in range(16):
        qrows[128 * h:128 * (h + 1)] = emb[256 * h + 128 * u:256 * h + 128 * u + 128]
    m["embq"] = qrows
    m["WqT"] = np.ascontiguousarray(np.asarray(inp["L0_attn_Wq"], np.float32).T)
    m["WkT"] = np.ascontiguousarray(np.asarray(inp["L0_attn_Wk"], np.float32).T)
    m["WvT"] = np.ascontiguousarray(np.asarray(inp["L0_attn_Wv"], np.float32).T)
    m["WrT"] = np.ascontiguousarray(np.asarray(inp["L0_attn_Wr"], np.float32).T)
    c1, s1 = _rope1_tables()
    m["rc1f"], m["rs1f"] = c1, s1
    m["rcq"] = np.ascontiguousarray(c1[128 * u:128 * (u + 1)])
    m["rsq"] = np.ascontiguousarray(s1[128 * u:128 * (u + 1)])
    m["identbf"] = np.eye(128, dtype=ml_dtypes.bfloat16)
    m["identf32"] = np.eye(128, dtype=np.float32)
    m["wqkv"] = np.ascontiguousarray(np.asarray(inp["L0_ffn_wqkv"], np.float32))
    m["woM"] = np.ascontiguousarray(np.asarray(inp["L0_ffn_wo"], np.float32))
    w1 = np.asarray(inp["L0_ffn_w1"], np.float32)
    b1 = np.asarray(inp["L0_ffn_b1"], np.float32)
    w1p = np.zeros((2, D_, HID2), np.float32)
    b1p = np.zeros((2, HID2), np.float32)
    for k in range(NBLK):
        na = min(64, HID0 - 64 * k)
        w1p[:, :, 128 * k:128 * k + na] = w1[:, :, 64 * k:64 * k + na]
        b1p[:, 128 * k:128 * k + na] = b1[:, 64 * k:64 * k + na]
        w1p[:, :, 128 * k + 64:128 * k + 64 + na] = \
            w1[:, :, HID0 + 64 * k:HID0 + 64 * k + na]
        b1p[:, 128 * k + 64:128 * k + 64 + na] = \
            b1[:, HID0 + 64 * k:HID0 + 64 * k + na]
    m["w1P"] = w1p
    m["b1P"] = b1p
    w2 = np.asarray(inp["L0_ffn_w2"], np.float32)
    w2p = np.zeros((2, 22 * 128, D_), np.float32)
    for blk in range(NBLK):
        mt, sh = divmod(blk, 2)
        n = min(64, HID0 - 64 * blk)
        w2p[:, 128 * mt + 64 * sh:128 * mt + 64 * sh + n] = \
            w2[:, 64 * blk:64 * blk + n]
    m["w2P"] = w2p
    m["b2"] = np.ascontiguousarray(np.asarray(inp["L0_ffn_b2"], np.float32))
    m["lng"] = np.ascontiguousarray(np.asarray(inp["L0_ffn_ln_g"], np.float32))
    m["lnb"] = np.ascontiguousarray(np.asarray(inp["L0_ffn_ln_b"], np.float32))
    m["qsw"] = np.ascontiguousarray(
        np.tile(np.asarray(inp["L0_ffn_q_scale"], np.float32), (1, 8)))
    m["ksw"] = np.ascontiguousarray(
        np.tile(np.asarray(inp["L0_ffn_k_scale"], np.float32), (1, 8)))
    m["xtab"] = _xtab(u).astype(ml_dtypes.bfloat16)
    m["gate"] = np.full((128, 1), 1.0 if u == 1 else 0.0, np.float32)
    m["wsend"] = np.full((128, 1), 1.0 if u == 0 else 0.0, np.float32)
    m["ones128"] = np.ones((1, 128), ml_dtypes.bfloat16)
    m["ones128f"] = np.ones((1, 128), np.float32)
    qp = np.arange(128)[:, None]
    kp = np.arange(128)[None, :]
    m["msk_lo"] = np.where(kp >= qp, 0.0, NEG).astype(ml_dtypes.bfloat16)
    m["msk_hi"] = np.where(kp <= qp, 0.0, NEG).astype(ml_dtypes.bfloat16)
    m["prevoff"] = np.full((128, 128), 0.0 if u == 1 else NEG,
                           np.float32).astype(ml_dtypes.bfloat16)
    c = 2 * b + u
    MG = MAGIC
    pr = np.arange(128)
    sl = c & 1
    ixo = np.zeros((128, 19), np.int32)
    for j in range(9):
        ixo[:, j] = 1152 * sl + 128 * j + pr
        ixo[:, 9 + j] = 1152 * (1 - sl) + 128 * j + pr
    ixo[:, 18] = 128 * sl + pr
    m["ixoff"] = ixo
    m["fthr"] = np.array([[MG[0]] * 8 + [MG[1]] * 8], np.int32)
    m["fmask"] = np.full((1, 16), -1, np.int32)
    ep = np.zeros((128, 6), np.int32)
    ep[:, 0] = _EPOCH[0]
    ep[:, 1] = _EPOCH[0]
    ep[:, 2] = _EPOCH[0] + 1
    ep[:, 3] = _EPOCH[0] + 1
    m["epochs"] = ep
    return m


def _xtab(u):
    DH = 64
    j = np.arange(0, DH, 2, dtype=np.float64)
    inv = 1.0 / (10000.0 ** (j / DH))
    t0 = 2048 * u - 1152
    t = np.arange(t0, t0 + 3200, dtype=np.float64)
    fr = t[:, None] * inv[None, :]
    fr = np.concatenate([fr, fr], axis=1)
    base = (j + 0.4 * DH) / (1.4 * DH)
    power = (t - S_ // 2) / float(WIN // 2)
    sc = base[None, :] ** power[:, None]
    sc = np.concatenate([sc, sc], axis=1)
    cos, sin = np.cos(fr), np.sin(fr)
    sgn = np.ones(DH)
    sgn[0:32] = -1.0
    qc = cos * sc
    qs = sgn * sin * sc
    kc = cos / sc
    ks = sgn * sin / sc
    rep = lambda a: np.tile(a, (1, 8)).astype(np.float32)
    return np.ascontiguousarray(
        np.concatenate([rep(qc), rep(qs), rep(kc), rep(ks)], axis=1))


def _run(nc, in_maps):
    used = nc._my_inputs
    maps = [{k: np.ascontiguousarray(v) for k, v in m.items() if k in used}
            for m in in_maps]
    res = run_bass_kernel_spmd(nc, maps, core_ids=list(range(8)))
    out = np.zeros((B_, S_, D_), np.float32)
    for c in range(8):
        b, u = divmod(c, 2)
        out[b, T_ * u:T_ * (u + 1)] = res.results[c]["out"]
    return out


def kernel(**inputs):
    _fresh_epoch()
    in_maps = []
    for c in range(8):
        b, u = divmod(c, 2)
        in_maps.append(_prep_core_inputs(inputs, b, u))
    if STAGE == "full":
        return _run(_build("full"), in_maps)
    x1 = _run(_build("p1"), in_maps)
    def _halo(xcur, b, u):
        if u == 0:
            return np.zeros((1152, D_), np.float32)
        return np.ascontiguousarray(xcur[b, 2048 - 1152:2048])
    for c in range(8):
        b, u = divmod(c, 2)
        in_maps[c]["xin"] = np.ascontiguousarray(x1[b, T_ * u:T_ * (u + 1)])
        in_maps[c]["halo"] = _halo(x1, b, u)
    x0o = _run(_build_layer(0), in_maps)
    for c in range(8):
        b, u = divmod(c, 2)
        in_maps[c]["xin"] = np.ascontiguousarray(x0o[b, T_ * u:T_ * (u + 1)])
        in_maps[c]["halo"] = _halo(x0o, b, u)
        in_maps[c]["x1own"] = np.ascontiguousarray(x1[b, T_ * u:T_ * (u + 1)])
    return _run(_build_layer(1), in_maps)


# version 18
# speedup vs baseline: 1.1829x; 1.0464x over previous
import math
import time
from contextlib import ExitStack

import numpy as np
import ml_dtypes

import concourse.bass as bass
import concourse.mybir as mybir
from concourse.bass_utils import run_bass_kernel_spmd

F32 = mybir.dt.float32
BF16 = mybir.dt.bfloat16
AF = mybir.ActivationFunctionType
ALU = mybir.AluOpType
AX = mybir.AxisListType

B_, S_, D_ = 4, 4096, 1024
T_ = 2048
NH1 = 16
ROT1 = 32
WIN = 512
HID0 = 2730
NBLK = 43
HID2 = 2 * 64 * NBLK
NEG = -30000.0
MAGIC = [0x4D474301, 0x4D474302]
_EPOCH = [(int(time.time_ns() // 1000) & 0x0FFFFFFF) | 0x10000000]


def _fresh_epoch():
    _EPOCH[0] = ((int(time.time_ns() // 1000) + _EPOCH[0] * 2 + 16)
                 & 0x0FFFFFFF) | 0x10000000
    return _EPOCH[0]
POLL_CORES = 8
LNEPS = 1e-5

STAGE = "full"


class _Ev:
    __slots__ = ("key", "val", "sem")
    def __init__(self, key):
        self.key = key
        self.val = None
        self.sem = key

class _Op:
    __slots__ = ("fn", "waits", "signal", "step", "tag")
    def __init__(self, fn, waits, signal, step, tag):
        self.fn = fn; self.waits = waits; self.signal = signal
        self.step = step; self.tag = tag

_COMPUTE_SEM = {"t": "T", "v": "V", "s": "S", "g": "G"}

DMA_POOL = {"DY": 12, "DG": 8, "DS": 4, "C": 2}

class Builder:

    def __init__(self, nc):
        self.nc = nc
        self.q = {k: [] for k in ("t", "v", "s", "g", "y")}
        self.compute_keys = ["T", "V", "S", "G"]
        self.sem_names = list(self.compute_keys)
        for k, n in DMA_POOL.items():
            self.sem_names += [f"{k}{i}" for i in range(n)]
        self.recent = {k: [] for k in list(DMA_POOL) + self.compute_keys}
        self.last = {k: None for k in list(DMA_POOL) + self.compute_keys}

    def ev(self, key):
        return _Ev(key)

    def add(self, eng, fn, waits=(), signal=None, tag=""):
        if signal is not None:
            r = self.recent[signal.key]
            r.append(signal)
            keep = DMA_POOL.get(signal.key, 1)
            del r[:-keep]
            self.last[signal.key] = signal
        self.q[eng].append(_Op(fn, [w for w in waits if w is not None],
                               signal, 0, tag))
        return signal

    def dma(self, eng, out, in_, waits=(), signal=None, tag="", **kw):
        assert eng in ("y", "g", "s", "v")
        def fn(e, out=out, in_=in_, kw=kw):
            return e.dma_start(out=out, in_=in_, **kw)
        return self.add(eng, fn, waits, signal, tag)

    def barrier(self, tag="bar"):
        waits = [e for r in self.recent.values() for e in r]
        evs = []
        for eng, key in (("t", "T"), ("v", "V"), ("s", "S"), ("g", "G")):
            evs.append(self.add(eng, lambda e: e.nop(), waits,
                                signal=self.ev(key), tag=tag + eng))
        self.add("y", lambda e: e.nop(), waits, tag=tag + "y")
        return evs

    def finalize(self, stack):
        waits = [e for r in self.recent.values() for e in r]
        self.add("g", lambda e: e.nop(), waits, tag="final_drain")
        nc = self.nc
        sems = {n: stack.enter_context(nc.semaphore(name=f"sem{n}"))
                for n in self.sem_names + ["PL0", "PL1", "PLB"]}
        self.sems = sems
        counts = {n: 0 for n in self.sem_names}
        pool_i = {k: 0 for k in DMA_POOL}
        for eng in ("v", "s", "g"):
            key = _COMPUTE_SEM[eng]
            prev = None
            for op in self.q[eng]:
                if op.signal is not None and op.signal.key in DMA_POOL:
                    continue
                if op.signal is None:
                    op.signal = _Ev(key)
                if prev is not None:
                    op.waits.append(prev)
                prev = op.signal
        for eng in ("t", "v", "s", "g", "y"):
            for op in self.q[eng]:
                sig = op.signal
                if sig is None:
                    continue
                if sig.key in DMA_POOL:
                    i = pool_i[sig.key]
                    pool_i[sig.key] = (i + 1) % DMA_POOL[sig.key]
                    name = f"{sig.key}{i}"
                    if counts[name] > 0:
                        prev = _Ev(sig.key)
                        prev.sem = name
                        prev.val = counts[name]
                        op.waits.append(prev)
                    op.step = 16
                else:
                    name = sig.key
                    op.step = 1
                counts[name] += op.step
                sig.sem = name
                sig.val = counts[name]
        self._check()
        block = stack.enter_context(nc.Block())
        watermark = {eng: {n: 0 for n in self.sem_names}
                     for eng in ("t", "v", "s", "g", "y")}

        def emit(eng_name, engine):
            wm = watermark[eng_name]
            own = _COMPUTE_SEM.get(eng_name)
            for op in self.q[eng_name]:
                for evt in op.waits:
                    assert evt.val is not None, f"unsignaled event: {op.tag}"
                    if evt.val > wm[evt.sem]:
                        engine.wait_ge(sems[evt.sem], evt.val)
                        wm[evt.sem] = evt.val
                ins = op.fn(engine)
                if op.signal is not None:
                    ins.then_inc(sems[op.signal.sem], op.step)

        @block.tensor
        def _(e): emit("t", e)
        @block.vector
        def _(e): emit("v", e)
        @block.scalar
        def _(e): emit("s", e)
        @block.gpsimd
        def _(e): emit("g", e)
        @block.sync
        def _(e): emit("y", e)

    def _check(self):
        idx = {k: 0 for k in self.q}
        done = {n: 0 for n in self.sem_names}
        progress = True
        while progress:
            progress = False
            for eng, ops in self.q.items():
                while idx[eng] < len(ops):
                    op = ops[idx[eng]]
                    if any(e.val is None or done[e.sem] < e.val for e in op.waits):
                        break
                    if op.signal is not None:
                        done[op.signal.sem] += op.step
                    idx[eng] += 1
                    progress = True
        stuck = {e: (i, len(self.q[e])) for e, i in idx.items() if i < len(self.q[e])}
        if stuck:
            lines = []
            for e, (i, n) in stuck.items():
                op = self.q[e][i]
                lines.append(f"  {e}: op {i}/{n} tag={op.tag!r} waits="
                             f"{[(x.sem, x.val, done[x.sem]) for x in op.waits]}")
            raise RuntimeError("builder deadlock:\n" + "\n".join(lines))


class NS:
    pass


def _v3(ap, inner):
    return ap.rearrange("p (c d) -> p c d", d=inner)



def _din(nc, name, shape, dtype):
    if not hasattr(nc, "_my_inputs"):
        nc._my_inputs = set()
    nc._my_inputs.add(name)
    return nc.dram_tensor(name, shape, dtype, kind="ExternalInput").ap()



def _emit_phase1(nc, B, st, H):
    ev = B.ev
    s1 = ExitStack()
    H.O1 = s1.enter_context(nc.sbuf_tensor("O1t", [128, 16 * D_], BF16))
    rcf = s1.enter_context(nc.sbuf_tensor("rcfs", [128, 2 * D_], F32))
    rsf = s1.enter_context(nc.sbuf_tensor("rsfs", [128, 2 * D_], F32))
    rcq = s1.enter_context(nc.sbuf_tensor("rcqs", [128, D_], F32))
    rsq = s1.enter_context(nc.sbuf_tensor("rsqs", [128, D_], F32))
    idb = s1.enter_context(nc.sbuf_tensor("idb", [128, 128], BF16))
    idf32 = s1.enter_context(nc.sbuf_tensor("idf32", [128, 128], F32))

    tb = []
    for dst, src in ((rcf, H.rc1f), (rsf, H.rs1f)):
        tb.append(B.dma("y", _v3(dst.ap(), D_), src.rearrange("(c p) d -> p c d", p=128),
                        signal=ev("DY"), tag="tbl"))
    tb.append(B.dma("y", rcq[:, :], H.rcq, signal=ev("DY"), tag="tbl"))
    tb.append(B.dma("y", rsq[:, :], H.rsq, signal=ev("DY"), tag="tbl"))
    ev_idb = B.dma("y", idb[:, :], H.idbf, signal=ev("DY"), tag="idb")
    ev_idf = B.dma("y", idf32[:, :], H.idf, signal=ev("DY"), tag="idf")
    ev_tbl = tb

    with ExitStack() as sa:
        Wqb = sa.enter_context(nc.sbuf_tensor("Wqb", [128, 8 * D_], BF16))
        Wkb = sa.enter_context(nc.sbuf_tensor("Wkb", [128, 8 * D_], BF16))
        Wvb = sa.enter_context(nc.sbuf_tensor("Wvb", [128, 8 * D_], BF16))
        ebf = sa.enter_context(nc.sbuf_tensor("ebf", [128, 4 * D_], BF16))
        ebq = sa.enter_context(nc.sbuf_tensor("ebq", [128, 2 * D_], BF16))
        embT = sa.enter_context(nc.sbuf_tensor("embT", [128, 32 * 128], BF16))
        qembT = sa.enter_context(nc.sbuf_tensor("qembT", [128, 16 * 128], BF16))
        kro = sa.enter_context(nc.sbuf_tensor("kro", [128, 4 * D_], BF16))
        qro = sa.enter_context(nc.sbuf_tensor("qro", [128, 2 * D_], BF16))
        vbf = sa.enter_context(nc.sbuf_tensor("vbf", [128, 4 * D_], BF16))
        t1f = sa.enter_context(nc.sbuf_tensor("t1f", [128, 512], F32))
        t2f = sa.enter_context(nc.sbuf_tensor("t2f", [128, 512], F32))
        kt = sa.enter_context(nc.sbuf_tensor("kt", [128, 4096], BF16))
        qt = sa.enter_context(nc.sbuf_tensor("qt", [128, 2048], BF16))
        vt = [sa.enter_context(nc.sbuf_tensor(f"vt{i}", [128, 32 * 65], BF16))
              for i in range(2)]
        pab = sa.enter_context(nc.sbuf_tensor("pab", [128, 4 * 1024], BF16))
        svc = sa.enter_context(nc.sbuf_tensor("svc", [128, 1024], F32))
        rsum = sa.enter_context(nc.sbuf_tensor("rsum", [128, 16], F32))
        PT = sa.enter_context(nc.psum_tensor("PT", [128, 2048], BF16))
        PPw = sa.enter_context(nc.psum_tensor("PP", [128, 1024], F32))
        SP = sa.enter_context(nc.psum_tensor("SP", [128, 1024], F32))
        SVp = sa.enter_context(nc.psum_tensor("SVp", [128, 1024], F32))
        EP = PPw[:, 512:1024]

        ev_w = []
        for W, src in ((Wqb, H.WqT), (Wkb, H.WkT), (Wvb, H.WvT)):
            ev_w.append(B.dma("g", _v3(W.ap(), D_),
                              src.rearrange("(m p) o -> p m o", p=128),
                              signal=ev("DG"), tag="wld"))
        ev_vones = None
        for i in range(2):
            ev_vones = B.add("v", lambda e, t=vt[i]: e.memset(t[:, :], 1.0),
                             signal=ev("V"), tag="vones")
        ev_svz = B.add("v", lambda e: e.memset(svc[:, :], 0.0),
                       signal=ev("V"), tag="svz")

        kt3 = kt.ap().rearrange("p (t c) -> p t c", c=16)
        qt3 = qt.ap().rearrange("p (t c) -> p t c", c=16)

        g = NS()
        g.ebf = g.ebq = None
        g.embT = g.qembT = None
        g.kro = g.qro = None
        g.vbf = None
        g.kt = g.qt = None
        g.vt = [ev_vones, ev_vones]
        g.PT = [ev_idb, ev_idb]
        g.PPb = [None, None]
        g.SPexp = [None, None]
        g.SVepi = None
        g.EP = ev_idf
        g.pab = [None] * 4
        g.O1done = None

        def rope(dst_ap, psrc, cslc, sslc, tagp, waits=()):
            B.add("v", lambda e: e.tensor_mul(t1f[:, :], psrc, cslc),
                  waits=waits, tag=tagp + "a")
            pv = psrc.rearrange("p (b two) -> p b two", two=2)
            sv_ = sslc.rearrange("p (b two) -> p b two", two=2)
            t2v = t2f.ap().rearrange("p (b two) -> p b two", two=2)
            B.add("v", lambda e: e.tensor_mul(t2v[:, :, 0], pv[:, :, 1], sv_[:, :, 0]),
                  tag=tagp + "b")
            B.add("v", lambda e: e.tensor_mul(t2v[:, :, 1], pv[:, :, 0], sv_[:, :, 1]),
                  tag=tagp + "c")
            return B.add("v", lambda e: e.tensor_add(dst_ap, t1f[:, :], t2f[:, :]),
                         signal=ev("V"), tag=tagp + "d")

        for a in range(8):
            e_ebf = B.dma("g", _v3(ebf.ap(), D_),
                          H.emb[512 * a:512 * (a + 1), :]
                          .rearrange("(c p) d -> p c d", p=128),
                          waits=(g.ebf or []), signal=ev("DG"), tag=f"ebf{a}")
            e_ebq = B.dma("g", _v3(ebq.ap(), D_),
                          H.embq[256 * a:256 * (a + 1), :]
                          .rearrange("(c p) d -> p c d", p=128),
                          waits=(g.ebq or []), signal=ev("DG"), tag=f"ebq{a}")
            lastT = None
            for c4 in range(4):
                for half in range(2):
                    par = (2 * c4 + half) % 2
                    tw = [e_ebf, g.PT[par], g.embT]
                    for mm in range(4):
                        m = 4 * half + mm
                        lastT = B.add(
                            "t", lambda e, c4=c4, m=m, mm=mm, par=par: e.transpose(
                                PT[:, 1024 * par + 128 * mm:1024 * par + 128 * (mm + 1)],
                                ebf[:, 1024 * c4 + 128 * m:1024 * c4 + 128 * (m + 1)],
                                idb[:, :]),
                            waits=tw if mm == 0 else (),
                            signal=ev("T") if mm == 3 else None, tag=f"eT{a}")
                    eng_ = "v" if par == 0 else "s"
                    g.PT[par] = B.add(
                        eng_, lambda e, c4=c4, half=half, par=par, eng_=eng_:
                        (e.tensor_copy if eng_ == "v" else
                         lambda d, s_: e.activation(d, s_, AF.Copy))(
                            embT[:, (8 * c4 + 4 * half) * 128:(8 * c4 + 4 * half + 4) * 128],
                            PT[:, 1024 * par:1024 * par + 512]),
                        waits=[lastT], signal=ev("V" if par == 0 else "S"),
                        tag=f"eTe{a}")
            ev_embT = [g.PT[0], g.PT[1]]
            for ch in range(2):
                for half in range(2):
                    par = (2 * ch + half) % 2
                    tw = [e_ebq, g.PT[par], g.qembT]
                    for mm in range(4):
                        m = 4 * half + mm
                        lastT = B.add(
                            "t", lambda e, ch=ch, m=m, mm=mm, par=par: e.transpose(
                                PT[:, 1024 * par + 128 * mm:1024 * par + 128 * (mm + 1)],
                                ebq[:, 1024 * ch + 128 * m:1024 * ch + 128 * (m + 1)],
                                idb[:, :]),
                            waits=tw if mm == 0 else (),
                            signal=ev("T") if mm == 3 else None, tag=f"qT{a}")
                    eng_ = "v" if par == 0 else "s"
                    g.PT[par] = B.add(
                        eng_, lambda e, ch=ch, half=half, par=par, eng_=eng_:
                        (e.tensor_copy if eng_ == "v" else
                         lambda d, s_: e.activation(d, s_, AF.Copy))(
                            qembT[:, (8 * ch + 4 * half) * 128:(8 * ch + 4 * half + 4) * 128],
                            PT[:, 1024 * par:1024 * par + 512]),
                        waits=[lastT], signal=ev("V" if par == 0 else "S"),
                        tag=f"qTe{a}")
            ev_qembT = [g.PT[0], g.PT[1]]

            ev_kro = ev_qro = ev_vevac = None
            gi = 0
            for c4 in range(4):
                for which, Wb in (("k", Wkb), ("v", Wvb)):
                    for eb in range(2):
                        bk = gi % 2
                        gi += 1
                        PPs = PPw[:, 512 * bk:512 * (bk + 1)]
                        lastm = None
                        for m in range(8):
                            lastm = B.add(
                                "t", lambda e, c4=c4, m=m, eb=eb, Wb=Wb,
                                PPs=PPs: e.matmul(
                                    PPs,
                                    embT[:, (8 * c4 + m) * 128:(8 * c4 + m + 1) * 128],
                                    Wb[:, 1024 * m + 512 * eb:1024 * m + 512 * (eb + 1)],
                                    start=(m == 0), stop=(m == 7)),
                                waits=ev_embT + [g.PPb[bk]] + ev_w
                                if m == 0 else (),
                                signal=ev("T") if m == 7 else None, tag=f"kv{a}")
                        if which == "k":
                            g.PPb[bk] = rope(
                                kro[:, 1024 * c4 + 512 * eb:1024 * c4 + 512 * (eb + 1)],
                                PPs,
                                rcf[:, 1024 * (c4 % 2) + 512 * eb:1024 * (c4 % 2) + 512 * (eb + 1)],
                                rsf[:, 1024 * (c4 % 2) + 512 * eb:1024 * (c4 % 2) + 512 * (eb + 1)],
                                f"krp{a}",
                                waits=[lastm] + (g.kro or []) + ev_tbl)
                            ev_kro = g.PPb[bk]
                        else:
                            g.PPb[bk] = B.add(
                                "s", lambda e, c4=c4, eb=eb, PPs=PPs: e.copy(
                                    vbf[:, 1024 * c4 + 512 * eb:1024 * c4 + 512 * (eb + 1)],
                                    PPs),
                                waits=[lastm, g.vbf], signal=ev("S"), tag=f"vev{a}")
                            ev_vevac = g.PPb[bk]
            for ch in range(2):
                for eb in range(2):
                    bk = gi % 2
                    gi += 1
                    PPs = PPw[:, 512 * bk:512 * (bk + 1)]
                    lastm = None
                    for m in range(8):
                        lastm = B.add(
                            "t", lambda e, ch=ch, m=m, eb=eb, PPs=PPs: e.matmul(
                                PPs,
                                qembT[:, (8 * ch + m) * 128:(8 * ch + m + 1) * 128],
                                Wqb[:, 1024 * m + 512 * eb:1024 * m + 512 * (eb + 1)],
                                start=(m == 0), stop=(m == 7)),
                            waits=ev_qembT + [g.PPb[bk]] + ev_w if m == 0 else (),
                            signal=ev("T") if m == 7 else None, tag=f"qp{a}")
                    g.PPb[bk] = rope(
                        qro[:, 1024 * ch + 512 * eb:1024 * ch + 512 * (eb + 1)],
                        PPs,
                        rcq[:, 512 * eb:512 * (eb + 1)],
                        rsq[:, 512 * eb:512 * (eb + 1)], f"qrp{a}",
                        waits=[lastm] + (g.qro or []) + ev_tbl)
                    ev_qro = g.PPb[bk]

            ev_vs = B.dma("y", H.vscr[512 * a:512 * (a + 1), :]
                          .rearrange("(c p) d -> p c d", p=128),
                          _v3(vbf.ap(), D_), waits=[ev_vevac], signal=ev("DY"), tag=f"vs{a}")
            g.vbf = ev_vs
            ev_vt = []
            for i in range(2):
                h4 = 2 * a + i
                src = H.vscr[256 * h4:256 * (h4 + 1), :] \
                    .rearrange("(cc pp) (pl d) -> (pp pl) cc d", pp=8, pl=16)
                dst = vt[i].ap().rearrange("p (cc e) -> p cc e", e=65)[:, :, 0:64]
                ev_vt.append(B.dma("y", dst, src, waits=[ev_vs, g.vt[i]],
                                   signal=ev("DY"), tag=f"vt{a}{i}"))

            ev_ktb = ev_qtb = None
            for cb in range(16):
                par = cb % 2
                lastT = None
                first = True
                for base, c4s in ((0, (0, 1)), (64, (2, 3))):
                    for jc, c4 in enumerate(c4s):
                        last = (base == 64 and jc == 1)
                        lastT = B.add(
                            "t", lambda e, cb=cb, c4=c4, base=base, jc=jc, par=par:
                            e.transpose(
                                PT[base:base + 64,
                                   1024 * par + 128 * jc:1024 * par + 128 * (jc + 1)],
                                kro[:, 1024 * c4 + 64 * cb:1024 * c4 + 64 * (cb + 1)],
                                idb[:, :],
                                tile_position=(0, base)),
                            waits=[ev_kro, g.PT[par], g.kt] if first else (),
                            signal=ev("T") if last else None, tag=f"ktT{a}")
                        first = False
                eng_ = "v" if par == 0 else "s"
                g.PT[par] = B.add(
                    eng_, lambda e, cb=cb, par=par, eng_=eng_:
                    (e.tensor_copy if eng_ == "v" else
                     lambda d, s_: e.activation(d, s_, AF.Copy))(
                        kt3[:, 0:256, cb], PT[:, 1024 * par:1024 * par + 256]),
                    waits=[lastT], signal=ev("V" if par == 0 else "S"),
                    tag=f"ktE{a}")
            ev_ktb = [g.PT[0], g.PT[1]]
            for cb in range(16):
                par = cb % 2
                lastT = None
                for j, (base, ch) in enumerate(((0, 0), (64, 1))):
                    lastT = B.add(
                        "t", lambda e, cb=cb, ch=ch, base=base, par=par:
                        e.transpose(
                            PT[base:base + 64, 1024 * par:1024 * par + 128],
                            qro[:, 1024 * ch + 64 * cb:1024 * ch + 64 * (cb + 1)],
                            idb[:, :],
                            tile_position=(0, base)),
                        waits=[ev_qro, g.PT[par], g.qt] if j == 0 else (),
                        signal=ev("T") if j == 1 else None, tag=f"qtT{a}")
                eng_ = "v" if par == 0 else "s"
                g.PT[par] = B.add(
                    eng_, lambda e, cb=cb, par=par, eng_=eng_:
                    (e.tensor_copy if eng_ == "v" else
                     lambda d, s_: e.activation(d, s_, AF.Copy))(
                        qt3[:, 0:128, cb], PT[:, 1024 * par:1024 * par + 128]),
                    waits=[lastT], signal=ev("V" if par == 0 else "S"),
                    tag=f"qtE{a}")
            ev_qtb = [g.PT[0], g.PT[1]]

            ev_s = {}
            ev_e = {}
            ev_v = {}
            last_sv = [None, None]
            prev_pair_sv = [g.vt[0], g.vt[1]]

            def emit_sv(qpb, cc):
                slot = cc % 4
                for i in range(2):
                    sw = [ev_e[(qpb, cc, i)]]
                    if cc == 0:
                        sw = sw + [ev_vt[i], g.SVepi]
                    ev_v[(qpb, cc, i)] = B.add(
                        "t", lambda e, cc=cc, slot=slot, i=i, qpb=qpb: e.matmul(
                            SVp[0:65, 512 * i:512 * (i + 1)],
                            vt[i][:, 65 * cc:65 * cc + 65],
                            pab[:, 1024 * slot + 512 * i:1024 * slot + 512 * (i + 1)],
                            start=(cc == 0), stop=(cc == 31)),
                        waits=sw, signal=ev("T"), tag=f"sv{a}")

            for qpb in range(4):
                for c in range(32):
                    for i in range(2):
                        w0 = (ev_ktb + ev_qtb + [g.SPexp[i]]
                              if (qpb == 0 and c == 0) else [])
                        if c > 0:
                            w0 = w0 + [ev_e.get((qpb, c - 1, i))]
                        elif qpb > 0:
                            w0 = w0 + [ev_e.get((qpb - 1, 31, i))]
                        ev_s[(qpb, c, i)] = B.add(
                            "t", lambda e, c=c, qpb=qpb, i=i: e.matmul(
                                SP[:, 512 * i:512 * (i + 1)],
                                kt[64 * i:64 * (i + 1), 128 * c:128 * (c + 1)],
                                qt[64 * i:64 * (i + 1), 512 * qpb:512 * (qpb + 1)],
                                start=True, stop=True),
                            waits=w0, signal=ev("T"), tag=f"s{i}{a}")
                    if c > 0:
                        emit_sv(qpb, c - 1)
                    slot = c % 4
                    for i in range(2):
                        ew = [ev_s[(qpb, c, i)]]
                        if (qpb, c - 4, i) in ev_v:
                            ew.append(ev_v[(qpb, c - 4, i)])
                        elif c < 4 and qpb > 0:
                            ew.append(ev_v[(qpb - 1, 28 + c, i)])
                        elif c < 4:
                            ew.append(prev_pair_sv[i])
                        ev_e[(qpb, c, i)] = B.add(
                            "s", lambda e, slot=slot, i=i: e.activation(
                                pab[:, 1024 * slot + 512 * i:
                                    1024 * slot + 512 * (i + 1)],
                                SP[:, 512 * i:512 * (i + 1)],
                                AF.Exp, scale=0.25),
                            waits=ew, signal=ev("S"), tag=f"ex{a}")
                emit_sv(qpb, 31)
                last_sv = [ev_v[(qpb, 31, 0)], ev_v[(qpb, 31, 1)]]

                EPr = EP.rearrange("p (j x) -> p j x", x=128)
                ecp = []
                for i in range(2):
                    ecp.append(B.add(
                        "v", lambda e, i=i: e.tensor_copy(
                            svc[0:65, 512 * i:512 * (i + 1)],
                            SVp[0:65, 512 * i:512 * (i + 1)]),
                        waits=[last_sv[i], ev_svz], signal=ev("V"),
                        tag=f"svc{a}"))
                g.SVepi = ecp[-1]
                for i in range(2):
                    lastT = None
                    for j in range(4):
                        lastT = B.add(
                            "t", lambda e, i=i, j=j: e.transpose(
                                EP[:, 128 * j:128 * (j + 1)],
                                svc[:, 512 * i + 128 * j:512 * i + 128 * (j + 1)],
                                idf32[:, :]),
                            waits=[ecp[i], ev_idf, g.EP, ev_vevac, ev_qro]
                            if j == 0 else (),
                            signal=ev("T") if j == 3 else None,
                            tag=f"epT{a}")
                    ro = 8 * i
                    B.add("v", lambda e, ro=ro: e.tensor_copy(
                        rsum[:, ro:ro + 4], EPr[:, :, 64]),
                        waits=[lastT], tag=f"rs{a}{i}")
                    B.add("v", lambda e, ro=ro: e.reciprocal(
                        rsum[:, ro + 4:ro + 8], rsum[:, ro:ro + 4]),
                        tag=f"rr{a}{i}")
                    evo = None
                    for j in range(4):
                        cidx = 4 * qpb + j
                        hg = H.hoff + 2 * a + i
                        evo = B.add(
                            "v", lambda e, j=j, cidx=cidx, hg=hg, ro=ro:
                            e.tensor_single_scalar(
                                H.O1[:, 1024 * cidx + 64 * hg:1024 * cidx + 64 * (hg + 1)],
                                EP[:, 128 * j:128 * j + 64],
                                rsum[:, ro + 4 + j:ro + 5 + j], ALU.mult),
                            signal=ev("V") if j == 3 else None, tag=f"o1{a}{i}")
                    g.EP = evo
                g.O1done = g.EP

            g.ebf = ev_embT
            g.ebq = ev_qembT
            g.embT = ev_s[(3, 31, 1)]
            g.qembT = ev_s[(3, 31, 1)]
            g.kro = ev_ktb
            g.qro = ev_qtb
            g.kt = ev_s[(3, 31, 1)]
            g.qt = ev_s[(3, 31, 1)]
            g.vt = [last_sv[0], last_sv[1]]
            g.SPexp = [ev_e[(3, 31, 0)], ev_e[(3, 31, 1)]]

        H.ev_p1_attn_done = g.O1done

    bar = B.barrier("p1bar")
    ev = B.ev
    H.ev_ln_chunk = {}
    H.x = st.enter_context(nc.sbuf_tensor("x_stream", [128, 16 * D_], F32,
                                          side="right"))
    with ExitStack() as sb:
        WrTb = sb.enter_context(nc.sbuf_tensor("WrTb", [128, 8 * D_], BF16))
        sqs = sb.enter_context(nc.sbuf_tensor("sqs", [128, D_], F32))
        stt = sb.enter_context(nc.sbuf_tensor("stt", [128, 16], F32))
        PQ = [sb.enter_context(nc.psum_tensor(f"PQ{i}", [128, 512], F32))
              for i in range(2)]

        ev_wr = B.dma("g", _v3(WrTb.ap(), D_),
                      H.WrT.rearrange("(m p) o -> p m o", p=128),
                      waits=bar, signal=ev("DG"), tag="wrld")
        ev_xl = B.dma("y", _v3(H.x.ap(), D_),
                      H.embo.rearrange("(c p) d -> p c d", p=128),
                      waits=bar, signal=ev("DY"), tag="xload")

        g_pq = [bar[0], bar[0]]
        ev_ln = None
        for qloc in range(2):
            for sc in range(8):
                cidx = 8 * qloc + sc
                ev_mm = {}
                for eb in range(2):
                    pq = PQ[eb]
                    lastm = None
                    for jc in range(8):
                        lastm = B.add(
                            "t", lambda e, qloc=qloc, sc=sc, eb=eb, jc=jc, pq=pq: e.matmul(
                                pq[:, :],
                                WrTb[:, 1024 * jc + 128 * sc:1024 * jc + 128 * (sc + 1)],
                                H.O1[:, 1024 * (8 * qloc + jc) + 512 * eb:
                                     1024 * (8 * qloc + jc) + 512 * (eb + 1)],
                                start=(jc == 0), stop=(jc == 7)),
                            waits=[ev_wr, H.ev_p1_attn_done, g_pq[eb]] if jc == 0 else (),
                            signal=ev("T") if jc == 7 else None, tag=f"wr{qloc}{sc}")
                    ev_mm[eb] = lastm
                ra = []
                for eb in range(2):
                    ra.append(B.add(
                        "v", lambda e, cidx=cidx, eb=eb: e.tensor_add(
                            H.x[:, 1024 * cidx + 512 * eb:1024 * cidx + 512 * (eb + 1)],
                            H.x[:, 1024 * cidx + 512 * eb:1024 * cidx + 512 * (eb + 1)],
                            PQ[eb][:, :]),
                        waits=[ev_mm[eb], ev_xl], signal=ev("V"), tag=f"ra{cidx}"))
                    g_pq[eb] = ra[-1]
                xc = H.x[:, 1024 * cidx:1024 * (cidx + 1)]
                B.add("v", lambda e, xc=xc: e.reduce_sum(stt[:, 0:1], xc, axis=AX.X),
                      waits=[ra[1]], tag=f"ls{cidx}")
                ev_sq = B.add("s", lambda e, xc=xc: e.activation(
                    sqs[:, :], xc, AF.Square, accum_out=stt[:, 1:2]),
                    waits=[ra[1]], signal=ev("S"), tag=f"lq{cidx}")
                B.add("v", lambda e: e.tensor_scalar_mul(stt[:, 2:3], stt[:, 0:1],
                                                         -1.0 / D_), tag="l1")
                B.add("v", lambda e: e.tensor_scalar_mul(stt[:, 3:4], stt[:, 1:2],
                                                         1.0 / D_),
                      waits=[ev_sq], tag="l2")
                B.add("v", lambda e: e.tensor_mul(stt[:, 4:5], stt[:, 2:3],
                                                  stt[:, 2:3]), tag="l3")
                ev_var = B.add("v", lambda e: e.scalar_tensor_tensor(
                    stt[:, 5:6], stt[:, 3:4], LNEPS, stt[:, 4:5],
                    ALU.add, ALU.subtract),
                    signal=ev("V"), tag="l4")
                ev_sd = B.add("s", lambda e: e.activation(stt[:, 6:7], stt[:, 5:6],
                                                          AF.Sqrt),
                              waits=[ev_var], signal=ev("S"), tag="l5")
                B.add("v", lambda e: e.reciprocal(stt[:, 7:8], stt[:, 6:7]),
                      waits=[ev_sd], tag="l6")
                ev_ln = B.add("v", lambda e, xc=xc: e.tensor_scalar(
                    xc, xc, stt[:, 2:3], stt[:, 7:8], ALU.add, ALU.mult),
                    signal=ev("V"), tag=f"lapp{cidx}")
                H.ev_ln_chunk[cidx] = ev_ln
                if H.stage == "p1":
                    H.ev_x1 = B.dma(
                        "y", H.out.rearrange("(c p) d -> p c d", p=128)[:, cidx, :],
                        xc, waits=[ev_ln], signal=ev("DY"), tag=f"xout{cidx}")
        H.ev_x1_ln = ev_ln
    s1.close()
    H.p1bar = B.barrier("p1end")
    return H



def _emit_phase2(nc, B, st, H):
    ev = B.ev
    RG = [[0, 1], [2, 3], [4, 5], [6, 7]]
    xh = st.enter_context(nc.sbuf_tensor("xh", [128, 9 * D_], BF16))
    xh8 = st.enter_context(nc.sbuf_tensor("xh8", [128, D_], F32,
                                           side="right"))
    gateg = st.enter_context(nc.sbuf_tensor("gateg", [128, 1], F32))
    wsendg = st.enter_context(nc.sbuf_tensor("wsendg", [128, 1], F32))
    ones1 = st.enter_context(nc.sbuf_tensor("ones1", [128, 128], BF16))
    ones1f = st.enter_context(nc.sbuf_tensor("ones1f", [1, 128], F32))
    idb2 = st.enter_context(nc.sbuf_tensor("idb2", [128, 128], BF16))
    idf2 = st.enter_context(nc.sbuf_tensor("idf2", [128, 128], F32))
    mlo = st.enter_context(nc.sbuf_tensor("mlo", [128, 128], BF16))
    mhi = st.enter_context(nc.sbuf_tensor("mhi", [128, 128], BF16))
    mpo = st.enter_context(nc.sbuf_tensor("mpo", [128, 128], BF16))
    x1scr = nc.dram_tensor("x1scr", [T_, D_], F32, kind="Internal").ap()
    H.shmx = [nc.dram_tensor(f"shmx{i}", [2 * 1152, D_], F32, kind="Internal",
                             addr_space="Shared").ap() for i in range(2)]
    H.shmf = nc.dram_tensor("shmf", [2 * 128, 2], mybir.dt.int32,
                            kind="Internal", addr_space="Shared").ap()
    H.fsb = st.enter_context(nc.sbuf_tensor("fsb", [1, 4], mybir.dt.int32))
    H.xstg = [st.enter_context(nc.sbuf_tensor(f"xstg{i}", [128, D_], BF16,
                                              side="right"))
              for i in range(2)]
    H.fsbt = st.enter_context(nc.sbuf_tensor("fsbt", [1, 16], mybir.dt.int32))
    H.fsbm = st.enter_context(nc.sbuf_tensor("fsbm", [1, 16], mybir.dt.int32))
    H.ixo = st.enter_context(nc.sbuf_tensor("ixo", [128, 19], mybir.dt.int32))
    H.epochv = st.enter_context(nc.sbuf_tensor("epochv", [128, 6],
                                               mybir.dt.int32))

    e_cst = [
        B.dma("y", gateg[:, :], H.gate, waits=H.p1bar, signal=ev("DY"), tag="c0"),
        B.dma("y", wsendg[:, :], H.wsend, waits=H.p1bar, signal=ev("DY"), tag="c1"),
        B.dma("y", ones1[0:1, :], H.ones128, waits=H.p1bar, signal=ev("DY"), tag="c2"),
        B.dma("y", ones1f[0:1, :], H.ones128f, waits=H.p1bar, signal=ev("DY"), tag="c2f"),
        B.dma("y", idb2[:, :], H.idbf, waits=H.p1bar, signal=ev("DY"), tag="c3"),
        B.dma("y", idf2[:, :], H.idf, waits=H.p1bar, signal=ev("DY"), tag="c4"),
        B.dma("y", mlo[:, :], H.msk_lo, waits=H.p1bar, signal=ev("DY"), tag="c5"),
        B.dma("y", mhi[:, :], H.msk_hi, waits=H.p1bar, signal=ev("DY"), tag="c6"),
        B.dma("y", mpo[:, :], H.prevoff, waits=H.p1bar, signal=ev("DY"), tag="c7"),
    ]
    H.e_tbl = [
        B.dma("y", H.fsbt[:, :], H.fthr, waits=H.p1bar, signal=ev("DY"),
              tag="cft"),
        B.dma("y", H.fsbm[:, :], H.fmask, waits=H.p1bar, signal=ev("DY"),
              tag="cfm"),
        B.dma("y", H.ixo[:, :], H.ixoff, waits=H.p1bar, signal=ev("DY"),
              tag="cxo"),
        B.dma("y", H.epochv[:, :], H.epochs, waits=H.p1bar, signal=ev("DY"),
              tag="cep"),
    ]
    e_cst += H.e_tbl
    ev_x1s = B.dma("y", x1scr.rearrange("(c p) d -> p c d", p=128),
                   _v3(H.x.ap(), D_), waits=[H.ev_x1_ln], signal=ev("DY"),
                   tag="x1save")

    def mkpoll(l, accept, wcol):
        def pollfn(e, l=l, accept=accept, wcol=wcol):
            sem = B.sems[f"PL{l}"]
            fh = H.shmf.rearrange("(s p) w -> p s w", p=128)[0:1, :, :]
            with ExitStack() as rs:
                rc = rs.enter_context(e.register(f"rcP{l}"))
                rt = rs.enter_context(e.register(f"rtP{l}"))
                rv = rs.enter_context(e.register(f"rvP{l}"))
                rw = rs.enter_context(e.register(f"rwP{l}"))
                e.reg_load(rw, H.epochv[0:1, wcol:wcol + 1])
                e.reg_mov(rc, 1)
                e.reg_mov(rt, 0)
                with e.While(rc):
                    e.reg_alu(rt, rt, 16, ALU.add)
                    e.dma_start(out=H.fsb.ap()
                                .rearrange("p (s w) -> p s w", w=2)[0:1, :, :],
                                in_=fh, allow_slow_non_contiguous=True
                                ).then_inc(sem, 16)
                    e.wait_ge(sem, rt)
                    e.reg_mov(rc, 0)
                    for k in accept:
                        e.reg_load(rv, H.fsb[0:1, k:k + 1])
                        e.reg_alu(rv, rv, rw, ALU.bitwise_xor)
                        e.reg_alu(rc, rc, rv, ALU.bitwise_or)
            return e.nop()
        return pollfn

    ev_coll_done = None
    for l in range(2):
        H.ev_xready = H.ev_x1_ln if l == 0 else H.ev_l_done
        wev = []
        for j, c in enumerate(range(7, 16)):
            wsrc = H.ev_ln_chunk.get(c) if l == 0 else H.ev_xready
            wev.append(B.add(
                "g", lambda e, j=j, c=c, l=l: e.indirect_dma_start(
                    H.shmx[l][:, :],
                    bass.IndirectOffsetOnAxis(ap=H.ixo[:, j:j + 1], axis=0),
                    H.x[:, 1024 * c:1024 * (c + 1)], None),
                waits=[wsrc] + H.e_tbl, signal=ev("DG"),
                tag=f"xw{l}{c}"))
        ev_fw = B.add(
            "g", lambda e, l=l: e.indirect_dma_start(
                H.shmf[:, :],
                bass.IndirectOffsetOnAxis(ap=H.ixo[:, 18:19], axis=0),
                H.epochv[:, 2 * l:2 * l + 2], None),
            waits=wev, signal=ev("DG"), tag=f"fw{l}")

        ev_poll = B.add("g", mkpoll(l, [0 + l, 2 + l], 2 * l),
                        waits=[ev_fw], signal=ev("G"), tag=f"poll{l}")
        gh = []
        g_stage = [ev_coll_done, ev_coll_done]
        for j in range(9):
            sb = xh8 if j == 8 else H.xstg[j % 2]
            rdev = B.add(
                "g", lambda e, j=j, l=l, sb=sb: e.indirect_dma_start(
                    sb[:, :], None,
                    H.shmx[l][:, :],
                    bass.IndirectOffsetOnAxis(ap=H.ixo[:, 9 + j:10 + j], axis=0)),
                waits=[ev_poll, g_stage[j % 2]], signal=ev("DG"),
                tag=f"xr{l}{j}")
            cst = B.add(
                "v", lambda e, j=j, sb=sb: e.tensor_scalar_mul(
                    xh[:, 1024 * j:1024 * (j + 1)], sb[:, :], gateg[:, 0:1]),
                waits=[rdev, e_cst[0], H.ev_xready], signal=ev("V"),
                tag=f"gx{l}{j}")
            g_stage[j % 2] = cst
            gh.append(cst)
        ev_h8g = B.add("v", lambda e: e.tensor_scalar_mul(
            xh8[:, :], xh8[:, :], gateg[:, 0:1]), waits=[gh[-1]],
            signal=ev("V"), tag=f"g8{l}")
        ev_halo = gh + [ev_h8g]
        ev_coll_done = ev_h8g

        guard = H.p1bar if l == 0 else [H.ev_l_done]
        _emit_layer(nc, B, st, H, l, xh, xh8, ones1, ones1f, idb2, idf2,
                    (mlo, mhi, mpo), ev_halo, guard)

    with ExitStack() as sf:
        xsc = [sf.enter_context(nc.sbuf_tensor(f"xsc{i}", [128, D_], F32))
               for i in range(2)]
        sqs = sf.enter_context(nc.sbuf_tensor("sqsF", [128, D_], F32))
        stt = sf.enter_context(nc.sbuf_tensor("sttF", [128, 16], F32))
        ev_out = None
        g_x = [None, None]
        for c in range(16):
            xc = H.x[:, 1024 * c:1024 * (c + 1)]
            lx = B.dma("y", xsc[c % 2][:, :],
                       x1scr.rearrange("(c p) d -> p c d", p=128)[:, c, :],
                       waits=[ev_x1s, H.ev_l_done, g_x[c % 2]],
                       signal=ev("DY"), tag=f"fx{c}")
            ra = B.add("v", lambda e, xc=xc, c=c: e.tensor_add(
                xc, xc, xsc[c % 2][:, :]), waits=[lx], signal=ev("V"),
                tag=f"fra{c}")
            g_x[c % 2] = ra
            B.add("v", lambda e, xc=xc: e.reduce_sum(stt[:, 0:1], xc, axis=AX.X),
                  waits=[ra], tag=f"fls{c}")
            ev_sq = B.add("s", lambda e, xc=xc: e.activation(
                sqs[:, :], xc, AF.Square, accum_out=stt[:, 1:2]),
                waits=[ra], signal=ev("S"), tag=f"flq{c}")
            B.add("v", lambda e: e.tensor_scalar_mul(stt[:, 2:3], stt[:, 0:1],
                                                     -1.0 / D_), tag="f1")
            B.add("v", lambda e: e.tensor_scalar_mul(stt[:, 3:4], stt[:, 1:2],
                                                     1.0 / D_),
                  waits=[ev_sq], tag="f2")
            B.add("v", lambda e: e.tensor_mul(stt[:, 4:5], stt[:, 2:3],
                                              stt[:, 2:3]), tag="f3")
            ev_var = B.add("v", lambda e: e.scalar_tensor_tensor(
                stt[:, 5:6], stt[:, 3:4], LNEPS, stt[:, 4:5],
                ALU.add, ALU.subtract), signal=ev("V"), tag="f4")
            ev_sd = B.add("s", lambda e: e.activation(stt[:, 6:7], stt[:, 5:6],
                                                      AF.Sqrt),
                          waits=[ev_var], signal=ev("S"), tag="f5")
            B.add("v", lambda e: e.reciprocal(stt[:, 7:8], stt[:, 6:7]),
                  waits=[ev_sd], tag="f6")
            ev_ln = B.add("v", lambda e, xc=xc: e.tensor_scalar(
                xc, xc, stt[:, 2:3], stt[:, 7:8], ALU.add, ALU.mult),
                signal=ev("V"), tag=f"flap{c}")
            ev_out = B.dma(
                "y", H.out.rearrange("(c p) d -> p c d", p=128)[:, c, :],
                xc, waits=[ev_ln], signal=ev("DY"), tag=f"fout{c}")
        H.ev_out = ev_out


def _emit_layer(nc, B, st, H, l, xh, xh8, ones1, ones1f, idb2, idf2, masks, ev_halo,
                guard):
    ev = B.ev
    mlo, mhi, mpo = masks
    NKC, NQC, NIC = 24, 20, 25

    with ExitStack() as sl:
        qbf = sl.enter_context(nc.sbuf_tensor(f"qbf{l}", [128, NQC * 512], BF16))
        kbf = sl.enter_context(nc.sbuf_tensor(f"kbf{l}", [128, NKC * 512], BF16))
        vly = sl.enter_context(nc.sbuf_tensor(f"vly{l}", [128, NKC * 520], BF16))
        ev_vones = B.add("v", lambda e: e.memset(vly[:, :], 1.0),
                         signal=ev("V"), tag=f"von{l}")

        qsb = sl.enter_context(nc.sbuf_tensor(f"qsb{l}", [128, 512], BF16))
        ksb = sl.enter_context(nc.sbuf_tensor(f"ksb{l}", [128, 512], BF16))
        qrow = sl.enter_context(nc.sbuf_tensor(f"qrow{l}", [1, 512], F32))
        with nc.psum_tensor(f"pbc{l}", [128, 512], F32) as PB:
            prev = None
            for dst, srcr in ((qsb, H.qsw), (ksb, H.ksw)):
                e_qr = B.dma("y", qrow[0:1, :], srcr[l:l + 1, :],
                             waits=[prev] + list(guard), signal=ev("DY"),
                             tag=f"qr{l}")
                mm = B.add("t", lambda e: e.matmul(
                    PB[:, :], ones1f[0:1, :], qrow[0:1, :],
                    start=True, stop=True),
                    waits=[e_qr, prev], signal=ev("T"), tag=f"bc{l}")
                prev = B.add("v", lambda e, dst=dst: e.tensor_copy(dst[:, :],
                                                                   PB[:, :]),
                             waits=[mm], signal=ev("V"), tag=f"bce{l}")
            ev_bc = prev

        with ExitStack() as sA:
            wqb = sA.enter_context(nc.sbuf_tensor(f"wqb{l}", [128, 8 * 1536], BF16))
            xTt = [sA.enter_context(nc.sbuf_tensor(f"xT{l}{i}", [128, 8 * 129], BF16))
                   for i in range(2)]
            xbs = sA.enter_context(nc.sbuf_tensor(f"xbs{l}", [128, D_], BF16))
            xT9 = sA.enter_context(nc.sbuf_tensor(f"xT9{l}", [128, 8 * 129], BF16))
            xtw = [sA.enter_context(nc.sbuf_tensor(f"xtw{l}0", [128, 1024], BF16))] * 2
            qraw = sA.enter_context(nc.sbuf_tensor(f"qraw{l}", [128, 1024], F32))
            zq = sA.enter_context(nc.sbuf_tensor(f"zq{l}", [128, 512], BF16))
            t1p = qraw
            t2p = sA.enter_context(nc.sbuf_tensor(f"t2p{l}", [128, 512], BF16))
            nrm = sA.enter_context(nc.sbuf_tensor(f"nrm{l}", [128, 64], F32))
            sqs = sA.enter_context(nc.sbuf_tensor(f"sqs{l}", [128, 1024], BF16))
            PT2 = sA.enter_context(nc.psum_tensor(f"PT2{l}", [128, 2048], BF16))
            PP2 = sA.enter_context(nc.psum_tensor(f"PP2{l}", [128, 1536], F32))

            ev_w = B.dma("g", wqb.ap().rearrange("p (m o) -> p m o", o=1536),
                         H.wqkv[l].rearrange("(m p) o -> p m o", p=128),
                         waits=guard, signal=ev("DG"), tag=f"wq{l}")
            g_PT = [ev_bc, ev_bc]
            g_xT = [None, None]
            g_blk = {0: guard[0], 1: guard[0], 2: guard[0]}
            g_chain = guard[0]
            g_chain2 = {0: guard[0], 1: guard[0]}
            g_srd = {}
            chain_par = 0
            ev_q_done = ev_k_done = ev_v_done = None
            tile9_ev = None
            prev_tile = {}

            def emit_tile(i, buf, srcw, xsrc, gxt, seed):
                lastT = None
                for half in range(2):
                    par = half
                    for mm_ in range(4):
                        m = 4 * half + mm_
                        lastT = B.add(
                            "t", lambda e, m=m, mm_=mm_, par=par, xsrc=xsrc:
                            e.transpose(
                                PT2[:, 1024 * par + 128 * mm_:1024 * par + 128 * (mm_ + 1)],
                                xsrc[:, 128 * m:128 * (m + 1)],
                                idb2[:, :]),
                            waits=(srcw + [g_PT[par], g_srd.get(id(buf))])
                            if mm_ == 0 else (),
                            signal=ev("T") if mm_ == 3 else None,
                            tag=f"xT{l}{i}")
                    if par == 0:
                        g_PT[par] = B.add(
                            "v", lambda e, half=half, par=par, buf=buf:
                            e.tensor_copy(
                                buf.ap().rearrange("p (m c) -> p m c", c=129)
                                [:, 4 * half:4 * half + 4, 1:129],
                                PT2.ap().rearrange("p (m c) -> p m c", c=128)
                                [:, 8 * par:8 * par + 4, :]),
                            waits=[lastT, gxt], signal=ev("V"), tag=f"xTe{l}{i}")
                    else:
                        g_PT[par] = B.add(
                            "s", lambda e, half=half, par=par, buf=buf:
                            e.activation(
                                buf.ap().rearrange("p (m c) -> p m c", c=129)
                                [:, 4 * half:4 * half + 4, 1:129],
                                PT2.ap().rearrange("p (m c) -> p m c", c=128)
                                [:, 8 * par:8 * par + 4, :], AF.Copy),
                            waits=[lastT, gxt], signal=ev("S"), tag=f"xTe{l}{i}")
                evt = g_PT[1]
                if seed is not None:
                    sbuf, sev = seed
                    evt = B.add("v", lambda e, buf=buf, sbuf=sbuf: e.tensor_copy(
                        buf.ap().rearrange("p (m c) -> p m c", c=129)[:, :, 0],
                        sbuf.ap().rearrange("p (m c) -> p m c", c=129)[:, :, 128]),
                        waits=[evt, sev], signal=ev("V"), tag=f"seed{l}{i}")
                    g_srd[id(sbuf)] = evt
                return evt

            wln0 = (getattr(H, "ev_ln_chunk", {}).get(0, H.ev_xready)
                    if l == 0 else H.ev_xready)
            e_src9 = B.add("s", lambda e: e.activation(
                xbs[:, :], H.x[:, 0:1024], AF.Copy), waits=[wln0],
                signal=ev("S"), tag=f"xc{l}9")
            tile9_ev = emit_tile(9, xT9, [e_src9], xbs[:, :], None, None)
            prev_tile[9] = (xT9, tile9_ev)

            order = list(range(10, NIC)) + list(range(0, 9)) + [9]

            def emit_chains(i, buf, ev_tr, ip):
                nonlocal ev_q_done, ev_k_done, ev_v_done, chain_par, g_chain
                kc = i - 1
                qc = i - 5
                blks = ([0] if qc >= 0 else []) + [1, 2]
                ev_mm = {}
                for blk in blks:
                    lastm = None
                    for m in range(8):
                        lastm = B.add(
                            "t", lambda e, m=m, blk=blk, buf=buf: e.matmul(
                                PP2[:, 512 * blk:512 * (blk + 1)],
                                buf[:, 129 * m + (1 if m < 4 else 0):
                                    129 * m + (1 if m < 4 else 0) + 128],
                                wqb[:, 1536 * m + 512 * blk:1536 * m + 512 * (blk + 1)],
                                start=(m == 0), stop=(m == 7)),
                            waits=[ev_tr, ev_w, g_blk[blk]] if m == 0 else (),
                            signal=ev("T") if m == 7 else None, tag=f"qkv{l}{i}")
                    ev_mm[blk] = lastm
                g_xT[ip] = ev_mm[blks[-1]]
                xw = xtw[0]
                for which, blk, dstb, dc, sb, tabof in (
                        ("v", 2, None, kc, None, 0),
                        ("k", 1, kbf, kc, ksb, 1024),
                        ("q", 0, qbf, qc, qsb, 0)):
                    if which == "q" and qc < 0:
                        continue
                    psl = PP2[:, 512 * blk:512 * (blk + 1)]
                    if which == "v":
                        ev_v_done = B.add(
                            "s", lambda e, psl=psl, dc=dc: e.activation(
                                vly.ap().rearrange("p (c e) -> p c e", e=65)
                                [:, 8 * dc:8 * (dc + 1), 0:64],
                                psl.rearrange("p (c e) -> p c e", e=64),
                                AF.Copy),
                            waits=[ev_mm[2], ev_vones], signal=ev("S"),
                            tag=f"vev{l}{i}")
                        g_blk[2] = ev_v_done
                        continue
                    par = chain_par
                    chain_par ^= 1
                    qrw = qraw[:, 512 * par:512 * (par + 1)]
                    t1s = t1p[:, 512 * par:512 * (par + 1)]
                    sqsl = sqs[:, 512 * par:512 * (par + 1)]
                    nr = nrm[:, 32 * par:32 * (par + 1)]
                    ev_evc = B.add("s", lambda e, psl=psl, qrw=qrw: e.activation(
                        qrw, psl, AF.Copy),
                        waits=[ev_mm[blk], g_chain2[par]], signal=ev("S"),
                        tag=f"ev{l}{i}{which}")
                    g_blk[blk] = ev_evc
                    B.add("v", lambda e, sb=sb, qrw=qrw: e.tensor_mul(
                        zq[:, :], qrw, sb[:, :]),
                        waits=[ev_bc, ev_evc], tag=f"z{l}{i}{which}")
                    ev_sq = B.add("s", lambda e, qrw=qrw, sqsl=sqsl: e.activation(
                        sqsl, qrw, AF.Square),
                        waits=[ev_evc], signal=ev("S"), tag=f"ss{l}{i}{which}")
                    ev_red = B.add("v", lambda e, sqsl=sqsl, nr=nr: e.reduce_sum(
                        nr[:, 0:8],
                        sqsl.rearrange("p (h d) -> p h d", d=64), axis=AX.X),
                        waits=[ev_sq], signal=ev("V"), tag=f"rd{l}{i}{which}")
                    ev_nr2 = B.add("s", lambda e, nr=nr: e.activation(
                        nr[:, 8:16], nr[:, 0:8], AF.Sqrt),
                        waits=[ev_red], signal=ev("S"), tag=f"sq{l}{i}")
                    B.add("v", lambda e, nr=nr: e.tensor_scalar_max(
                        nr[:, 16:24], nr[:, 8:16], 1e-12), waits=[ev_nr2],
                        tag=f"cl{l}{i}")
                    B.add("v", lambda e, nr=nr: e.reciprocal(
                        nr[:, 24:32], nr[:, 16:24]),
                          tag=f"rc{l}{i}")
                    e_xt = B.dma(
                        "y", xw.ap().rearrange("p (f d) -> p f d", d=512),
                        H.xtab[128 * i:128 * (i + 1), tabof:tabof + 1024]
                        .rearrange("p (f d) -> p f d", d=512),
                        waits=[g_chain], signal=ev("DY"), tag=f"xt{l}{i}{which}")
                    co = xw[:, 0:512]
                    si = xw[:, 512:1024]
                    zv = zq.ap().rearrange("p (h s d) -> p h s d", s=2, d=32)
                    sv_ = si.rearrange("p (h s d) -> p h s d", s=2, d=32)
                    t2v = t2p.ap().rearrange("p (h s d) -> p h s d", s=2, d=32)
                    B.add("v", lambda e, co=co, t1s=t1s: e.tensor_mul(
                        t1s, zq[:, :], co),
                          waits=[e_xt, ev_nr2], tag=f"r1{l}{i}")
                    B.add("v", lambda e, sv_=sv_, zv=zv: e.tensor_mul(
                        t2v[:, :, 0, :], zv[:, :, 1, :], sv_[:, :, 0, :]),
                        tag=f"r2{l}{i}")
                    B.add("v", lambda e, sv_=sv_, zv=zv: e.tensor_mul(
                        t2v[:, :, 1, :], zv[:, :, 0, :], sv_[:, :, 1, :]),
                        tag=f"r3{l}{i}")
                    ev_r4 = B.add("v", lambda e, t1s=t1s: e.tensor_add(
                        t1s, t1s, t2p[:, :]),
                        signal=ev("V"), tag=f"r4{l}{i}")
                    lastn = None
                    for hh in range(8):
                        lastn = B.add(
                            "v", lambda e, hh=hh, dstb=dstb, dc=dc, t1s=t1s,
                            nr=nr: e.tensor_single_scalar(
                                dstb[:, 512 * dc + 64 * hh:512 * dc + 64 * (hh + 1)],
                                t1s[:, 64 * hh:64 * (hh + 1)],
                                nr[:, 24 + hh:25 + hh], ALU.mult),
                            waits=[ev_r4] if hh == 0 else (),
                            signal=ev("V") if hh == 7 else None,
                            tag=f"rs{l}{i}{which}")
                    if which == "k":
                        ev_k_done = lastn
                    else:
                        ev_q_done = lastn
                    g_chain = lastn
                    g_chain2[par] = lastn

            pending = None
            for idx, i in enumerate(order):
                ip = idx % 2
                buf = xTt[ip]
                if i == 9:
                    buf = xT9
                    ev_tr = B.add("v", lambda e: e.tensor_copy(
                        xT9.ap().rearrange("p (m c) -> p m c", c=129)[:, :, 0],
                        prev_tile[8][0].ap()
                        .rearrange("p (m c) -> p m c", c=129)[:, :, 128]),
                        waits=[tile9_ev, prev_tile[8][1]], signal=ev("V"),
                        tag=f"seed{l}9f")
                    g_srd[id(prev_tile[8][0])] = ev_tr
                else:
                    if i <= 8:
                        xsrc = xh[:, 1024 * i:1024 * (i + 1)]
                        srcw = [ev_halo[i]]
                    else:
                        wln = (getattr(H, "ev_ln_chunk", {}).get(i - 9, H.ev_xready)
                               if l == 0 else H.ev_xready)
                        xbw = [wln, tile9_ev] if idx == 0 else [wln]
                        e_src = B.add("s", lambda e, i=i: e.activation(
                            xbs[:, :], H.x[:, 1024 * (i - 9):1024 * (i - 8)],
                            AF.Copy), waits=xbw, signal=ev("S"),
                            tag=f"xc{l}{i}")
                        xsrc = xbs[:, :]
                        srcw = [e_src]
                    seed = prev_tile.get(i - 1)
                    ev_tr = emit_tile(i, buf, srcw, xsrc, g_xT[ip], seed)
                    prev_tile[i] = (buf, ev_tr)
                if pending is not None:
                    emit_chains(*pending)
                pending = None if i == 0 else (i, buf, ev_tr, ip)
            emit_chains(*pending)
            ev_stageA = [ev_q_done, ev_k_done, ev_v_done]
        obf = sl.enter_context(nc.sbuf_tensor(f"obf{l}", [128, NQC * 512], BF16))
        with ExitStack() as sC:
            qt2 = [sC.enter_context(nc.sbuf_tensor(f"qt2{l}0", [65, NQC * 128], BF16))] * 2
            kt2 = [sC.enter_context(nc.sbuf_tensor(f"kt2{l}0", [65, NKC * 128], BF16))] * 2
            qab = sC.enter_context(nc.sbuf_tensor(f"qab{l}", [64, 2560], BF16))
            kvx = sC.enter_context(nc.sbuf_tensor(f"kvx{l}", [64, 8], BF16))
            pw = [sC.enter_context(nc.sbuf_tensor(f"pw{l}{i}", [128, 640], BF16))
                  for i in range(2)]
            svc2 = sC.enter_context(nc.sbuf_tensor(f"svc2{l}", [128, 512], F32))
            rs2 = sC.enter_context(nc.sbuf_tensor(f"rs2{l}", [128, 16], F32))
            PT2b = sC.enter_context(nc.psum_tensor(f"PT2b{l}", [128, 2048], BF16))
            SP2 = sC.enter_context(nc.psum_tensor(f"SP2{l}", [128, 640], F32))
            SVq = sC.enter_context(nc.psum_tensor(f"SVq{l}", [65, 512], F32))
            EP2 = sC.enter_context(nc.psum_tensor(f"EP2{l}", [128, 512], F32))
            Cp = sC.enter_context(nc.psum_tensor(f"Cp{l}", [65, 512], F32))

            ev_k1 = B.add("v", lambda e: e.memset(kt2[0][64:65, :], 1.0),
                          waits=list(ev_stageA), signal=ev("V"), tag=f"k1a{l}")
            ev_k1b = ev_k1
            ev_svz2 = B.add("v", lambda e: e.memset(svc2[:, :], 0.0),
                            waits=list(ev_stageA), signal=ev("V"),
                            tag=f"svz{l}")
            g_PT = [ev_stageA[0], ev_stageA[1]]
            g_qt = [ev_k1, ev_k1b]
            g_pw = [[None, None], [None, None]]
            g_sp = None
            g_sp1 = None
            g_sv = ev_svz2
            g_ep = None
            g_cm = {w: None for w in range(5)}
            g_cw = None
            ev_obf = None
            for h in range(8):
                hb = h % 2
                qth, kth = qt2[hb], kt2[hb]
                ev_bq = ev_bk = None
                for (src, nch, dst) in ((qbf, NQC, qth), (kbf, NKC, kth)):
                    gi = 0
                    for c0 in range(0, nch, 4):
                        par = gi % 2
                        gi += 1
                        nn = min(4, nch - c0)
                        lastT = None
                        for j in range(nn):
                            lastT = B.add(
                                "t", lambda e, c0=c0, j=j, par=par, src=src, h=h:
                                e.transpose(
                                    PT2b[0:64, 1024 * par + 128 * j:1024 * par + 128 * (j + 1)],
                                    src[:, 512 * (c0 + j) + 64 * h:512 * (c0 + j) + 64 * (h + 1)],
                                    idb2[:, :]),
                                waits=[g_PT[par], g_qt[hb], ev_stageA[0],
                                       ev_stageA[1]] if j == 0 else (),
                                signal=ev("T") if j == nn - 1 else None,
                                tag=f"bT{l}{h}")
                        if par == 0:
                            g_PT[par] = B.add(
                                "v", lambda e, c0=c0, nn=nn, par=par, dst=dst:
                                e.tensor_copy(
                                    dst[0:64, 128 * c0:128 * (c0 + nn)],
                                    PT2b[0:64, 1024 * par:1024 * par + 128 * nn]),
                                waits=[lastT], signal=ev("V"), tag=f"bTe{l}{h}")
                        else:
                            g_PT[par] = B.add(
                                "s", lambda e, c0=c0, nn=nn, par=par, dst=dst:
                                e.activation(
                                    dst[0:64, 128 * c0:128 * (c0 + nn)],
                                    PT2b[0:64, 1024 * par:1024 * par + 128 * nn],
                                    AF.Copy),
                                waits=[lastT], signal=ev("S"), tag=f"bTe{l}{h}")
                    if src is qbf:
                        ev_bq = [g_PT[0], g_PT[1]]
                    else:
                        ev_bk = [g_PT[0], g_PT[1]]
                ev_cw = {}
                for w in range(5):
                    ev_kv = B.add("v", lambda e, kth=kth, w=w: e.reduce_max(
                        kvx[0:64, w:w + 1], kth[0:64, 512 * w:512 * w + 1024],
                        axis=AX.X, apply_absolute_value=True),
                        waits=ev_bk + [g_cm[w]], signal=ev("V"),
                        tag=f"kv{l}{h}{w}")
                    ev_qa = B.add("s", lambda e, qth=qth, w=w: e.activation(
                        qab[0:64, 512 * w:512 * (w + 1)],
                        qth[0:64, 512 * w:512 * (w + 1)], AF.Abs),
                        waits=ev_bq + [g_cm[w]], signal=ev("S"),
                        tag=f"qa{l}{h}{w}")
                    ev_cm = B.add("t", lambda e, w=w: e.matmul(
                        Cp[64:65, :], kvx[0:64, w:w + 1],
                        qab[0:64, 512 * w:512 * (w + 1)],
                        start=True, stop=True, tile_position=(0, 64)),
                        waits=[ev_kv, ev_qa, g_cw], signal=ev("T"),
                        tag=f"cm{l}{h}{w}")
                    g_cm[w] = ev_cm
                    ev_cw[w] = g_cw = B.add(
                        "s", lambda e, qth=qth, w=w: e.activation(
                            qth[64:65, 512 * w:512 * (w + 1)], Cp[64:65, :],
                            AF.Copy, scale=-1.0),
                        waits=[ev_cm], signal=ev("S"), tag=f"cw{l}{h}{w}")
                for w in range(5):
                    for b in range(4):
                        bp = b % 2
                        lastm = None
                        ev_s3 = None
                        for a in range(b, b + 5):
                            sslot = a - b
                            mk = []
                            if a == b:
                                mk.append(mlo)
                            if a == b + 4:
                                mk.append(mhi)
                            if w == 1 and a <= 3:
                                mk.append(mpo)
                            w_st = ()
                            if a == b:
                                w_st = [ev_cw[w], g_sp, g_pw[bp][0]]
                            elif a == b + 4:
                                w_st = [g_sp1, g_pw[bp][1]]
                            sig_a = a in (b + 3, b + 4)
                            lastm = B.add(
                                "t", lambda e, w=w, a=a, b=b, sslot=sslot,
                                kth=kth, qth=qth, stp=(not mk): e.matmul(
                                    SP2[:, 128 * sslot:128 * (sslot + 1)],
                                    kth[0:65, 128 * (4 * w + a):128 * (4 * w + a + 1)],
                                    qth[0:65, 128 * (4 * w + b):128 * (4 * w + b + 1)],
                                    start=True, stop=stp),
                                waits=w_st,
                                signal=ev("T") if (sig_a and not mk) else None,
                                tag=f"st{l}{h}{w}")
                            for q_, mt in enumerate(mk):
                                fin = (sig_a and q_ == len(mk) - 1)
                                lastm = B.add(
                                    "t", lambda e, sslot=sslot, mt=mt,
                                    last=(q_ == len(mk) - 1): e.matmul(
                                        SP2[:, 128 * sslot:128 * (sslot + 1)],
                                        mt[:, :], idb2[:, :],
                                        start=False, stop=last),
                                    signal=ev("T") if fin else None,
                                    tag=f"mk{l}{h}{w}")
                            if a == b + 3:
                                ev_s3 = lastm
                        ev_sm = lastm
                        ev_ex = B.add("s", lambda e, bp=bp: e.activation(
                            pw[bp][:, 0:512], SP2[:, 0:512], AF.Exp, scale=8.0),
                            waits=[ev_s3, g_pw[bp][0]], signal=ev("S"),
                            tag=f"xx{l}{h}{w}")
                        g_sp = ev_ex
                        ev_ex1 = B.add("s", lambda e, bp=bp: e.activation(
                            pw[bp][:, 512:640], SP2[:, 512:640], AF.Exp,
                            scale=8.0),
                            waits=[ev_sm, g_pw[bp][1]], signal=ev("S"),
                            tag=f"xy{l}{h}{w}")
                        g_sp1 = ev_ex1
                        lastv = None
                        sv3 = None
                        for a in range(b, b + 5):
                            sslot = a - b
                            svw = ()
                            if a == b:
                                svw = [ev_ex, g_sv]
                            elif a == b + 4:
                                svw = [ev_ex1]
                            lastv = B.add(
                                "t", lambda e, w=w, a=a, b=b, sslot=sslot, h=h, bp=bp:
                                e.matmul(
                                    SVq[0:65, 128 * b:128 * (b + 1)],
                                    vly[:, 520 * (4 * w + a) + 65 * h:
                                        520 * (4 * w + a) + 65 * (h + 1)],
                                    pw[bp][:, 128 * sslot:128 * (sslot + 1)],
                                    start=(a == b), stop=(a == b + 4)),
                                waits=svw,
                                signal=ev("T") if a in (b + 3, b + 4) else None,
                                tag=f"sv2{l}{h}{w}")
                            if a == b + 3:
                                sv3 = lastv
                        last_sv = lastv
                        g_pw[bp] = [sv3, lastv]
                    ecp = B.add("v", lambda e: e.tensor_copy(
                        svc2[0:65, :], SVq[0:65, :]),
                        waits=[last_sv], signal=ev("V"), tag=f"sve{l}{h}{w}")
                    g_sv = ecp
                    lastT = None
                    for j in range(4):
                        lastT = B.add(
                            "t", lambda e, j=j: e.transpose(
                                EP2[:, 128 * j:128 * (j + 1)],
                                svc2[:, 128 * j:128 * (j + 1)], idf2[:, :]),
                            waits=[ecp, g_ep] if j == 0 else (),
                            signal=ev("T") if j == 3 else None, tag=f"eT2{l}{h}{w}")
                    EPr = EP2.ap().rearrange("p (j x) -> p j x", x=128)
                    B.add("v", lambda e, EPr=EPr: e.tensor_copy(
                        rs2[:, 0:4], EPr[:, :, 64]), waits=[lastT],
                        tag=f"er{l}{h}{w}")
                    B.add("v", lambda e: e.reciprocal(rs2[:, 4:8], rs2[:, 0:4]),
                          tag=f"er2{l}{h}{w}")
                    evo = None
                    for j in range(4):
                        qc = 4 * w + j
                        evo = B.add(
                            "v", lambda e, j=j, qc=qc, h=h: e.tensor_single_scalar(
                                obf[:, 512 * qc + 64 * h:512 * qc + 64 * (h + 1)],
                                EP2[:, 128 * j:128 * j + 64],
                                rs2[:, 4 + j:5 + j], ALU.mult),
                            signal=ev("V") if j == 3 else None, tag=f"ob{l}{h}{w}")
                    g_ep = evo
                    ev_obf = evo
                g_qt[hb] = ev_sm
        with ExitStack() as sD:
            oTt = [sD.enter_context(nc.sbuf_tensor(f"oT{l}{i}", [128, 512], BF16))
                   for i in range(2)]
            wob = sD.enter_context(nc.sbuf_tensor(f"wob{l}", [128, 4 * 1024], BF16))
            PT2c = sD.enter_context(nc.psum_tensor(f"PT2c{l}", [128, 2048], BF16))
            PW = [sD.enter_context(nc.psum_tensor(f"PWD{l}{i}", [128, 512], F32))
                  for i in range(2)]
            ev_wo = B.dma("g", wob.ap().rearrange("p (m o) -> p m o", o=1024),
                          H.wo[l].rearrange("(m p) o -> p m o", p=128),
                          waits=[ev_obf], signal=ev("DG"), tag=f"wo{l}")
            g_PT = [ev_obf, ev_obf]
            g_oT = [None, None]
            g_pw2 = [None, None]
            ev_xmid = None
            H.ev_xmid_chunk = {}
            for qc in range(3, NQC):
                par = qc % 2
                buf = oTt[par]
                lastT = None
                for j in range(4):
                    lastT = B.add(
                        "t", lambda e, qc=qc, j=j, par=par: e.transpose(
                            PT2c[:, 1024 * par + 128 * j:1024 * par + 128 * (j + 1)],
                            obf[:, 512 * qc + 128 * j:512 * qc + 128 * (j + 1)],
                            idb2[:, :]),
                        waits=[ev_obf, g_PT[par], g_oT[par]] if j == 0 else (),
                        signal=ev("T") if j == 3 else None, tag=f"oT{l}{qc}")
                g_PT[par] = B.add(
                    "v", lambda e, par=par, buf=buf: e.tensor_copy(
                        buf[:, :], PT2c[:, 1024 * par:1024 * par + 512]),
                    waits=[lastT], signal=ev("V"), tag=f"oTe{l}{qc}")
                ev_mm = []
                for eb in range(2):
                    lastm = None
                    for ic in range(4):
                        lastm = B.add(
                            "t", lambda e, eb=eb, ic=ic, buf=buf: e.matmul(
                                PW[eb][:, :], buf[:, 128 * ic:128 * (ic + 1)],
                                wob[:, 1024 * ic + 512 * eb:1024 * ic + 512 * (eb + 1)],
                                start=(ic == 0), stop=(ic == 3)),
                            waits=[g_PT[par], ev_wo, g_pw2[eb]] if ic == 0 else (),
                            signal=ev("T") if ic == 3 else None, tag=f"wom{l}{qc}")
                    ev_mm.append(lastm)
                g_oT[par] = ev_mm[-1]
                if qc == 3:
                    dst = xh8
                    dof = 0
                else:
                    dst = H.x
                    dof = 1024 * (qc - 4)
                for eb in range(2):
                    ww = [ev_mm[eb]]
                    if qc == 3:
                        ww.append(ev_halo[-1])
                    ev_xmid = B.add(
                        "v", lambda e, dst=dst, dof=dof, eb=eb: e.tensor_add(
                            dst[:, dof + 512 * eb:dof + 512 * (eb + 1)],
                            dst[:, dof + 512 * eb:dof + 512 * (eb + 1)],
                            PW[eb][:, :]),
                        waits=ww, signal=ev("V"), tag=f"xm{l}{qc}")
                    g_pw2[eb] = ev_xmid
                H.ev_xmid_chunk[qc] = ev_xmid
            H.ev_xmid = ev_xmid
    _emit_ffn(nc, B, st, H, l, xh8, ones1, idb2)


def _emit_ffn(nc, B, st, H, l, xh8, ones1, idb2):
    ev = B.ev
    NB2 = 22
    with ExitStack() as sE:
        h2bf = sE.enter_context(nc.sbuf_tensor(f"h2bf{l}", [128, 17 * D_], BF16))
        h2T = sE.enter_context(nc.sbuf_tensor(f"h2T{l}", [128, 8 * 2049], BF16))
        gsb = sE.enter_context(nc.sbuf_tensor(f"gsb{l}", [128, NB2 * 512], BF16))
        NW = 6
        w1s = [sE.enter_context(nc.sbuf_tensor(f"w1s{l}{i}", [128, 1024], BF16))
               for i in range(NW)]
        w2s = [sE.enter_context(nc.sbuf_tensor(f"w2s{l}{i}", [128, 512], BF16))
               for i in range(NW)]
        st2 = sE.enter_context(nc.sbuf_tensor(f"st2{l}", [128, 4 * 18], F32))
        sh2 = sE.enter_context(nc.sbuf_tensor(f"sh2{l}", [128, 2 * 18], F32))
        stm = sE.enter_context(nc.sbuf_tensor(f"stm{l}", [128, 8 * 18], F32))
        b1pp = sE.enter_context(nc.sbuf_tensor(f"b1pp{l}", [128, 43], F32))
        lgb = sE.enter_context(nc.sbuf_tensor(f"lgb{l}", [128, 16], F32))
        gel = sE.enter_context(nc.sbuf_tensor(f"gel{l}", [64, 512], F32))
        b2r = sE.enter_context(nc.sbuf_tensor(f"b2r{l}", [1, 1024], BF16))
        sqs2 = sE.enter_context(nc.sbuf_tensor(f"sqs2{l}", [128, 512], F32))
        PT2d = sE.enter_context(nc.psum_tensor(f"PT2d{l}", [128, 2048], BF16))
        PW1 = sE.enter_context(nc.psum_tensor(f"PW1{l}", [128, 1024], F32))
        PW2 = sE.enter_context(nc.psum_tensor(f"PW2{l}", [128, 2048], F32))

        e_b1 = B.dma("y", b1pp[:, :],
                     H.b1P[l].rearrange("(k p) -> p k", p=128),
                     waits=[H.ev_xmid], signal=ev("DY"), tag=f"b1{l}",
                     allow_slow_non_contiguous=True)
        e_lg = B.dma("y", lgb[:, 0:8],
                     H.lng[l].rearrange("(m p) -> p m", p=128),
                     waits=[H.ev_xmid], signal=ev("DY"), tag=f"lg{l}",
                     allow_slow_non_contiguous=True)
        e_lb = B.dma("y", lgb[:, 8:16],
                     H.lnb[l].rearrange("(m p) -> p m", p=128),
                     waits=[H.ev_xmid], signal=ev("DY"), tag=f"lb{l}",
                     allow_slow_non_contiguous=True)
        e_b2 = B.dma("g", b2r[0:1, :], H.b2[l:l + 1, :],
                     waits=[H.ev_xmid], signal=ev("DG"), tag=f"b2{l}")

        ev_stz = B.add("v", lambda e: e.memset(st2[:, :], 0.0),
                       signal=ev("V"), tag=f"stz{l}")
        ev_st = None
        for c in range(17):
            src = xh8[:, :] if c == 0 else H.x[:, 1024 * (c - 1):1024 * c]
            wmc = H.ev_xmid_chunk.get(c + 3, H.ev_xmid)
            B.add("v", lambda e, src=src, c=c: e.reduce_sum(
                st2[:, c:c + 1], src[:, 0:512], axis=AX.X),
                waits=[wmc], tag=f"s1a{l}{c}")
            B.add("v", lambda e, src=src, c=c: e.reduce_sum(
                st2[:, 18 + c:19 + c], src[:, 512:1024], axis=AX.X),
                tag=f"s1b{l}{c}")
            B.add("s", lambda e, src=src, c=c: e.activation(
                sqs2[:, :], src[:, 0:512], AF.Square,
                accum_out=st2[:, 36 + c:37 + c]),
                waits=[wmc, ev_stz], tag=f"q1a{l}{c}")
            ev_st = B.add("s", lambda e, src=src, c=c: e.activation(
                sqs2[:, :], src[:, 512:1024], AF.Square,
                accum_out=st2[:, 54 + c:55 + c]),
                signal=ev("S") if c == 16 else None, tag=f"q1b{l}{c}")
        ev_z0 = B.add("v", lambda e: e.memset(sh2[:, :], 0.0),
                      signal=ev("V"), tag=f"shz{l}")
        d1 = B.dma("y", sh2[1:128, 0:18], st2[0:127, 18:36],
                   waits=[ev_st, ev_z0], signal=ev("DY"), tag=f"shS{l}")
        d2 = B.dma("y", sh2[0:1, 1:18], st2[127:128, 18:35],
                   waits=[ev_st, ev_z0], signal=ev("DY"), tag=f"shSw{l}")
        d3 = B.dma("y", sh2[1:128, 18:36], st2[0:127, 54:72],
                   waits=[ev_st, ev_z0], signal=ev("DY"), tag=f"shQ{l}")
        d4 = B.dma("y", sh2[0:1, 19:36], st2[127:128, 54:71],
                   waits=[ev_st, ev_z0], signal=ev("DY"), tag=f"shQw{l}")
        B.add("v", lambda e: e.tensor_add(stm[:, 0:18], sh2[:, 0:18],
                                          st2[:, 0:18]),
              waits=[d1, d2], tag=f"cS{l}")
        B.add("v", lambda e: e.tensor_add(stm[:, 18:36], sh2[:, 18:36],
                                          st2[:, 36:54]),
              waits=[d3, d4], tag=f"cQ{l}")
        B.add("v", lambda e: e.tensor_scalar_mul(stm[:, 36:54], stm[:, 0:18],
                                                 -1.0 / D_), tag=f"mu{l}")
        B.add("v", lambda e: e.tensor_scalar_mul(stm[:, 0:18], stm[:, 18:36],
                                                 1.0 / D_), tag=f"e2{l}")
        B.add("v", lambda e: e.tensor_mul(stm[:, 18:36], stm[:, 36:54],
                                          stm[:, 36:54]), tag=f"m2{l}")
        ev_var = B.add("v", lambda e: e.scalar_tensor_tensor(
            stm[:, 0:18], stm[:, 0:18], LNEPS, stm[:, 18:36],
            ALU.add, ALU.subtract), signal=ev("V"), tag=f"vr{l}")
        ev_sd = B.add("s", lambda e: e.activation(stm[:, 18:36], stm[:, 0:18],
                                                  AF.Sqrt),
                      waits=[ev_var], signal=ev("S"), tag=f"sd{l}")
        B.add("v", lambda e: e.reciprocal(stm[:, 54:72], stm[:, 18:36]),
              waits=[ev_sd], tag=f"rs{l}")
        ev_z1 = B.add("v", lambda e: e.memset(stm[:, 72:108], 1.0),
                      signal=ev("V"), tag=f"suz{l}")
        ev_rs = B.last["V"]
        f1 = B.dma("y", stm[0:127, 72:90], stm[1:128, 36:54],
                   waits=[ev_z1, ev_rs], signal=ev("DY"), tag=f"su1{l}")
        f2 = B.dma("y", stm[0:127, 90:108], stm[1:128, 54:72],
                   waits=[ev_z1, ev_rs], signal=ev("DY"), tag=f"su2{l}")
        f3 = B.dma("y", stm[127:128, 72:89], stm[0:1, 37:54],
                   waits=[ev_z1, ev_rs], signal=ev("DY"), tag=f"su3{l}")
        f4 = B.dma("y", stm[127:128, 90:107], stm[0:1, 55:72],
                   waits=[ev_z1, ev_rs], signal=ev("DY"), tag=f"su4{l}")
        ev_h2 = None
        ev_apply = {}
        for c in range(17):
            src = xh8[:, :] if c == 0 else H.x[:, 1024 * (c - 1):1024 * c]
            B.add("v", lambda e, src=src, c=c: e.tensor_scalar(
                h2bf[:, 1024 * c:1024 * c + 512], src[:, 0:512],
                stm[:, 36 + c:37 + c], stm[:, 54 + c:55 + c],
                ALU.add, ALU.mult), waits=[f1, f2, f3, f4], tag=f"apA{l}{c}")
            ev_h2 = B.add("v", lambda e, src=src, c=c: e.tensor_scalar(
                h2bf[:, 1024 * c + 512:1024 * (c + 1)], src[:, 512:1024],
                stm[:, 72 + c:73 + c], stm[:, 90 + c:91 + c],
                ALU.add, ALU.mult),
                signal=ev("V"), tag=f"apB{l}{c}")
            ev_apply[c] = ev_h2
        g_PT = [ev_apply[0], ev_apply[0]]
        ev_h2T = None
        ev_hT_chunk = {}
        for c in range(17):
            for m in range(8):
                par = m % 2
                tt = B.add("t", lambda e, c=c, m=m, par=par: e.transpose(
                    PT2d[:, 1024 * par:1024 * par + 128],
                    h2bf[:, 1024 * c + 128 * m:1024 * c + 128 * (m + 1)],
                    idb2[:, :]),
                    waits=[ev_apply[c], g_PT[par]], signal=ev("T"),
                    tag=f"hT{l}{c}")
                if c == 0:
                    dst = h2T[:, 2049 * m:2049 * m + 1]
                    srcp = PT2d[:, 1024 * par + 127:1024 * par + 128]
                else:
                    dst = h2T[:, 2049 * m + 1 + 128 * (c - 1):
                              2049 * m + 1 + 128 * c]
                    srcp = PT2d[:, 1024 * par:1024 * par + 128]
                g_PT[par] = B.add(
                    "v", lambda e, dst=dst, srcp=srcp, m=m: e.tensor_scalar(
                        dst, srcp, lgb[:, m:m + 1], lgb[:, 8 + m:9 + m],
                        ALU.mult, ALU.add),
                    waits=[tt, e_lg, e_lb], signal=ev("V"), tag=f"hTe{l}{c}")
                ev_h2T = g_PT[par]
            ev_hT_chunk[c] = ev_h2T
        g_w1 = [H.ev_xmid] * NW
        g_w2 = [H.ev_xmid] * NW
        g_prod = None
        g_pw1 = [None, None]
        g_gsb = None
        ev_res = None
        for tb in range(4):
            for k in range(43):
                par = k % NW
                e_w = B.dma("g", w1s[par].ap().rearrange("p (m c) -> p m c", c=128),
                            H.w1P[l][:, 128 * k:128 * (k + 1)]
                            .rearrange("(m p) c -> p m c", p=128),
                            waits=[g_w1[par]], signal=ev("DG"), tag=f"w1d{l}")
                kb = k % 2
                lastm = None
                for m in range(8):
                    off = 2049 * m + 512 * tb + (1 if m < 4 else 0)
                    lastm = B.add(
                        "t", lambda e, m=m, par=par, kb=kb, off=off: e.matmul(
                            PW1[:, 512 * kb:512 * (kb + 1)],
                            w1s[par][:, 128 * m:128 * (m + 1)],
                            h2T[:, off:off + 512],
                            start=(m == 0), stop=(m == 7)),
                        waits=([e_w, ev_hT_chunk[min(4 * tb + 4, 16)],
                                g_pw1[kb]] if m == 0 else ()),
                        signal=ev("T") if m == 7 else None, tag=f"w1m{l}{k}")
                g_w1[par] = lastm
                eg = B.add("s", lambda e, kb=kb, k=k: e.activation(
                    gel[0:64, :], PW1[64:128, 512 * kb:512 * (kb + 1)],
                    AF.Gelu, bias=b1pp[64:128, k:k + 1]),
                    waits=[lastm, e_b1, g_prod], signal=ev("S"),
                    tag=f"gel{l}{k}")
                mt, half = k // 2, k % 2
                g_prod = g_pw1[kb] = B.add(
                    "v", lambda e, kb=kb, k=k, mt=mt, half=half:
                    e.scalar_tensor_tensor(
                        gsb[64 * half:64 * half + 64, 512 * mt:512 * (mt + 1)],
                        PW1[0:64, 512 * kb:512 * (kb + 1)],
                        b1pp[0:64, k:k + 1], gel[0:64, :],
                        ALU.add, ALU.mult),
                    waits=[eg, g_gsb], signal=ev("V"), tag=f"prd{l}{k}")
            ev_gsb = g_pw1[par]
            ev_z = B.add("v", lambda e: e.memset(
                gsb[64:128, 512 * 21:512 * 22], 0.0),
                waits=[ev_gsb], signal=ev("V"), tag=f"gz{l}")
            for eb in range(2):
                for kp2 in range(NB2):
                    par = kp2 % NW
                    e_w2 = B.dma("g", w2s[par][:, :],
                                 H.w2P[l][128 * kp2:128 * (kp2 + 1),
                                          512 * eb:512 * (eb + 1)],
                                 waits=[g_w2[par]], signal=ev("DG"),
                                 tag=f"w2d{l}")
                    lastm = None
                    for ts_ in range(4):
                        lastm = B.add(
                            "t", lambda e, ts_=ts_, kp2=kp2, par=par: e.matmul(
                                PW2[:, 512 * ts_:512 * (ts_ + 1)],
                                gsb[:, 512 * kp2 + 128 * ts_:512 * kp2 + 128 * (ts_ + 1)],
                                w2s[par][:, :],
                                start=(kp2 == 0), stop=False),
                            waits=([e_w2, ev_z, ev_res] if ts_ == 0 else ()),
                            signal=ev("T") if ts_ == 3 else None,
                            tag=f"w2m{l}{kp2}")
                    g_w2[par] = lastm
                lastb = None
                for ts_ in range(4):
                    lastb = B.add(
                        "t", lambda e, ts_=ts_, eb=eb: e.matmul(
                            PW2[:, 512 * ts_:512 * (ts_ + 1)],
                            ones1[0:1, :],
                            b2r[0:1, 512 * eb:512 * (eb + 1)],
                            start=False, stop=True),
                        waits=[e_b2] if ts_ == 0 else (),
                        signal=ev("T") if ts_ == 3 else None, tag=f"b2m{l}")
                for ts_ in range(4):
                    cc = 4 * tb + ts_
                    ev_res = B.add(
                        "v", lambda e, cc=cc, eb=eb, ts_=ts_: e.tensor_add(
                            H.x[:, 1024 * cc + 512 * eb:1024 * cc + 512 * (eb + 1)],
                            H.x[:, 1024 * cc + 512 * eb:1024 * cc + 512 * (eb + 1)],
                            PW2[:, 512 * ts_:512 * (ts_ + 1)]),
                        waits=[lastb], signal=ev("V"), tag=f"xr{l}{cc}{eb}")
            g_gsb = ev_res
        H.ev_l_done = ev_res


def _build_layer(l):
    nc = bass.Bass("TRN2", target_bir_lowering=False)
    H = NS()
    H.xin = _din(nc, "xin", [T_, D_], F32)
    H.halo = _din(nc, "halo", [1152, D_], F32)
    H.wqkv = _din(nc, "wqkv", [2, D_, 1536], F32)
    H.wo = _din(nc, "woM", [2, 512, D_], F32)
    H.w1P = _din(nc, "w1P", [2, D_, HID2], F32)
    H.b1P = _din(nc, "b1P", [2, HID2], F32)
    H.w2P = _din(nc, "w2P", [2, 22 * 128, D_], F32)
    H.b2 = _din(nc, "b2", [2, D_], F32)
    H.lng = _din(nc, "lng", [2, D_], F32)
    H.lnb = _din(nc, "lnb", [2, D_], F32)
    H.qsw = _din(nc, "qsw", [2, 512], F32)
    H.ksw = _din(nc, "ksw", [2, 512], F32)
    H.xtab = _din(nc, "xtab", [3200, 2048], BF16)
    H.msk_lo = _din(nc, "msk_lo", [128, 128], BF16)
    H.msk_hi = _din(nc, "msk_hi", [128, 128], BF16)
    H.prevoff = _din(nc, "prevoff", [128, 128], BF16)
    H.idbf = _din(nc, "identbf", [128, 128], BF16)
    H.idf = _din(nc, "identf32", [128, 128], F32)
    H.ones128 = _din(nc, "ones128", [1, 128], BF16)
    H.ones128f = _din(nc, "ones128f", [1, 128], F32)
    if l == 1:
        H.x1own = _din(nc, "x1own", [T_, D_], F32)
    H.out = nc.dram_tensor("out", [T_, D_], F32, kind="ExternalOutput").ap()

    B = Builder(nc)
    ev = B.ev
    with ExitStack() as st:
        H.x = st.enter_context(nc.sbuf_tensor("x_stream", [128, 16 * D_], F32,
                                              side="right"))
        xh = st.enter_context(nc.sbuf_tensor("xh", [128, 9 * D_], BF16))
        xh8 = st.enter_context(nc.sbuf_tensor("xh8", [128, D_], F32,
                                              side="right"))
        ones1 = st.enter_context(nc.sbuf_tensor("ones1", [128, 128], BF16))
        ones1f = st.enter_context(nc.sbuf_tensor("ones1f", [1, 128], F32))
        idb2 = st.enter_context(nc.sbuf_tensor("idb2", [128, 128], BF16))
        idf2 = st.enter_context(nc.sbuf_tensor("idf2", [128, 128], F32))
        mlo = st.enter_context(nc.sbuf_tensor("mlo", [128, 128], BF16))
        mhi = st.enter_context(nc.sbuf_tensor("mhi", [128, 128], BF16))
        mpo = st.enter_context(nc.sbuf_tensor("mpo", [128, 128], BF16))
        csts = [
            B.dma("y", ones1[0:1, :], H.ones128, signal=ev("DY"), tag="c2"),
            B.dma("y", ones1f[0:1, :], H.ones128f, signal=ev("DY"), tag="c2f"),
            B.dma("y", idb2[:, :], H.idbf, signal=ev("DY"), tag="c3"),
            B.dma("y", idf2[:, :], H.idf, signal=ev("DY"), tag="c4"),
            B.dma("y", mlo[:, :], H.msk_lo, signal=ev("DY"), tag="c5"),
            B.dma("y", mhi[:, :], H.msk_hi, signal=ev("DY"), tag="c6"),
            B.dma("y", mpo[:, :], H.prevoff, signal=ev("DY"), tag="c7"),
        ]
        H.ev_xready = B.dma("y", _v3(H.x.ap(), D_),
                            H.xin.rearrange("(c p) d -> p c d", p=128),
                            signal=ev("DY"), tag="xload")
        ev_hb = B.dma("g", _v3(xh.ap(), D_),
                      H.halo.rearrange("(c p) d -> p c d", p=128),
                      signal=ev("DG"), tag="hld")
        ev_h8 = B.dma("y", xh8[:, :],
                      H.halo.rearrange("(c p) d -> p c d", p=128)[:, 8, :],
                      signal=ev("DY"), tag="h8ld")
        ev_halo = [ev_hb] * 9 + [ev_h8]
        H.ev_xmid = None
        guard = csts
        _emit_layer(nc, B, st, H, l, xh, xh8, ones1, ones1f, idb2, idf2,
                    (mlo, mhi, mpo), ev_halo, guard)
        if l == 0:
            evs = None
            for c in range(16):
                evs = B.dma("y", H.out.rearrange("(c p) d -> p c d", p=128)[:, c, :],
                            H.x[:, 1024 * c:1024 * (c + 1)],
                            waits=[H.ev_l_done], signal=ev("DY"), tag=f"o{c}")
        else:
            _emit_final_ln(nc, B, st, H, x1dram=H.x1own)
        B.finalize(st)
    return nc


def _emit_final_ln(nc, B, st, H, x1dram):
    ev = B.ev
    with ExitStack() as sf:
        xsc = [sf.enter_context(nc.sbuf_tensor(f"xscF{i}", [128, D_], F32))
               for i in range(2)]
        sqs = sf.enter_context(nc.sbuf_tensor("sqsF", [128, D_], F32))
        stt = sf.enter_context(nc.sbuf_tensor("sttF", [128, 16], F32))
        g_x = [None, None]
        for c in range(16):
            xc = H.x[:, 1024 * c:1024 * (c + 1)]
            lx = B.dma("y", xsc[c % 2][:, :],
                       x1dram.rearrange("(c p) d -> p c d", p=128)[:, c, :],
                       waits=[w for w in (H.ev_l_done, g_x[c % 2]) if w],
                       signal=ev("DY"), tag=f"fx{c}")
            ra = B.add("v", lambda e, xc=xc, c=c: e.tensor_add(
                xc, xc, xsc[c % 2][:, :]), waits=[lx], signal=ev("V"),
                tag=f"fra{c}")
            g_x[c % 2] = ra
            B.add("v", lambda e, xc=xc: e.reduce_sum(stt[:, 0:1], xc, axis=AX.X),
                  waits=[ra], tag=f"fls{c}")
            ev_sq = B.add("s", lambda e, xc=xc: e.activation(
                sqs[:, :], xc, AF.Square, accum_out=stt[:, 1:2]),
                waits=[ra], signal=ev("S"), tag=f"flq{c}")
            B.add("v", lambda e: e.tensor_scalar_mul(stt[:, 2:3], stt[:, 0:1],
                                                     -1.0 / D_), tag="f1")
            B.add("v", lambda e: e.tensor_scalar_mul(stt[:, 3:4], stt[:, 1:2],
                                                     1.0 / D_),
                  waits=[ev_sq], tag="f2")
            B.add("v", lambda e: e.tensor_mul(stt[:, 4:5], stt[:, 2:3],
                                              stt[:, 2:3]), tag="f3")
            ev_var = B.add("v", lambda e: e.scalar_tensor_tensor(
                stt[:, 5:6], stt[:, 3:4], LNEPS, stt[:, 4:5],
                ALU.add, ALU.subtract), signal=ev("V"), tag="f4")
            ev_sd = B.add("s", lambda e: e.activation(stt[:, 6:7], stt[:, 5:6],
                                                      AF.Sqrt),
                          waits=[ev_var], signal=ev("S"), tag="f5")
            B.add("v", lambda e: e.reciprocal(stt[:, 7:8], stt[:, 6:7]),
                  waits=[ev_sd], tag="f6")
            ev_ln = B.add("v", lambda e, xc=xc: e.tensor_scalar(
                xc, xc, stt[:, 2:3], stt[:, 7:8], ALU.add, ALU.mult),
                signal=ev("V"), tag=f"flap{c}")
            B.dma("y", H.out.rearrange("(c p) d -> p c d", p=128)[:, c, :],
                  xc, waits=[ev_ln], signal=ev("DY"), tag=f"fout{c}")


NO_RD = True

def _build(stage):
    nc = bass.Bass("TRN2", target_bir_lowering=False,
                   detect_race_conditions=not NO_RD)
    H = NS()
    H.stage = stage
    H.emb = _din(nc, "emb_full", [S_, D_], F32)
    H.embq = _din(nc, "embq", [T_, D_], F32)
    H.embo = _din(nc, "emb_own", [T_, D_], F32)
    H.WqT = _din(nc, "WqT", [D_, D_], F32)
    H.WkT = _din(nc, "WkT", [D_, D_], F32)
    H.WvT = _din(nc, "WvT", [D_, D_], F32)
    H.WrT = _din(nc, "WrT", [D_, D_], F32)
    H.rc1f = _din(nc, "rc1f", [256, D_], F32)
    H.rs1f = _din(nc, "rs1f", [256, D_], F32)
    H.rcq = _din(nc, "rcq", [128, D_], F32)
    H.rsq = _din(nc, "rsq", [128, D_], F32)
    H.idbf = _din(nc, "identbf", [128, 128], BF16)
    H.idf = _din(nc, "identf32", [128, 128], F32)
    H.out = nc.dram_tensor("out", [T_, D_], F32, kind="ExternalOutput").ap()
    H.vscr = nc.dram_tensor("vscr", [S_, D_], BF16, kind="Internal").ap()
    H.hoff = 0
    H.wqkv = _din(nc, "wqkv", [2, D_, 1536], F32)
    H.wo = _din(nc, "woM", [2, 512, D_], F32)
    H.w1P = _din(nc, "w1P", [2, D_, HID2], F32)
    H.b1P = _din(nc, "b1P", [2, HID2], F32)
    H.w2P = _din(nc, "w2P", [2, 22 * 128, D_], F32)
    H.b2 = _din(nc, "b2", [2, D_], F32)
    H.lng = _din(nc, "lng", [2, D_], F32)
    H.lnb = _din(nc, "lnb", [2, D_], F32)
    H.qsw = _din(nc, "qsw", [2, 512], F32)
    H.ksw = _din(nc, "ksw", [2, 512], F32)
    H.xtab = _din(nc, "xtab", [3200, 2048], BF16)
    H.gate = _din(nc, "gate", [128, 1], F32)
    H.wsend = _din(nc, "wsend", [128, 1], F32)
    H.ones128 = _din(nc, "ones128", [1, 128], BF16)
    H.ones128f = _din(nc, "ones128f", [1, 128], F32)
    H.msk_lo = _din(nc, "msk_lo", [128, 128], BF16)
    H.msk_hi = _din(nc, "msk_hi", [128, 128], BF16)
    H.prevoff = _din(nc, "prevoff", [128, 128], BF16)
    H.fthr = _din(nc, "fthr", [1, 16], mybir.dt.int32)
    H.fmask = _din(nc, "fmask", [1, 16], mybir.dt.int32)
    H.ixoff = _din(nc, "ixoff", [128, 19], mybir.dt.int32)
    H.epochs = _din(nc, "epochs", [128, 6], mybir.dt.int32)

    B = Builder(nc)
    with ExitStack() as st:
        _emit_phase1(nc, B, st, H)
        if stage != "p1":
            _emit_phase2(nc, B, st, H)
        B.finalize(st)
    return nc



def _rope1_tables():
    inv = 1.0 / (10000.0 ** (np.arange(0, ROT1, 2, dtype=np.float64) / ROT1))
    r = np.arange(256)
    cb = np.arange(16)
    d = np.arange(64)
    p = 16 * (r[:, None, None] % 256) + cb[None, :, None] + 0 * d[None, None, :]
    fi = np.zeros(64)
    fi[0:ROT1:2] = inv
    fi[1:ROT1:2] = inv
    ang = p * fi[None, None, :]
    cos = np.cos(ang)
    sin = np.sin(ang)
    cos[:, :, ROT1:] = 1.0
    ssin = np.zeros_like(sin)
    ssin[:, :, 0:ROT1:2] = -sin[:, :, 0:ROT1:2]
    ssin[:, :, 1:ROT1:2] = sin[:, :, 1:ROT1:2]
    return (cos.reshape(256, 1024).astype(np.float32),
            ssin.reshape(256, 1024).astype(np.float32))


def _prep_core_inputs(inp, b, u):
    m = {}
    emb = np.ascontiguousarray(np.asarray(inp["L0_attn_emb"][b], np.float32))
    m["emb_full"] = emb
    m["emb_own"] = np.ascontiguousarray(emb[T_ * u:T_ * (u + 1)])
    qrows = np.zeros((T_, D_), np.float32)
    for h in range(16):
        qrows[128 * h:128 * (h + 1)] = emb[256 * h + 128 * u:256 * h + 128 * u + 128]
    m["embq"] = qrows
    m["WqT"] = np.ascontiguousarray(np.asarray(inp["L0_attn_Wq"], np.float32).T)
    m["WkT"] = np.ascontiguousarray(np.asarray(inp["L0_attn_Wk"], np.float32).T)
    m["WvT"] = np.ascontiguousarray(np.asarray(inp["L0_attn_Wv"], np.float32).T)
    m["WrT"] = np.ascontiguousarray(np.asarray(inp["L0_attn_Wr"], np.float32).T)
    c1, s1 = _rope1_tables()
    m["rc1f"], m["rs1f"] = c1, s1
    m["rcq"] = np.ascontiguousarray(c1[128 * u:128 * (u + 1)])
    m["rsq"] = np.ascontiguousarray(s1[128 * u:128 * (u + 1)])
    m["identbf"] = np.eye(128, dtype=ml_dtypes.bfloat16)
    m["identf32"] = np.eye(128, dtype=np.float32)
    m["wqkv"] = np.ascontiguousarray(np.asarray(inp["L0_ffn_wqkv"], np.float32))
    m["woM"] = np.ascontiguousarray(np.asarray(inp["L0_ffn_wo"], np.float32))
    w1 = np.asarray(inp["L0_ffn_w1"], np.float32)
    b1 = np.asarray(inp["L0_ffn_b1"], np.float32)
    w1p = np.zeros((2, D_, HID2), np.float32)
    b1p = np.zeros((2, HID2), np.float32)
    for k in range(NBLK):
        na = min(64, HID0 - 64 * k)
        w1p[:, :, 128 * k:128 * k + na] = w1[:, :, 64 * k:64 * k + na]
        b1p[:, 128 * k:128 * k + na] = b1[:, 64 * k:64 * k + na]
        w1p[:, :, 128 * k + 64:128 * k + 64 + na] = \
            w1[:, :, HID0 + 64 * k:HID0 + 64 * k + na]
        b1p[:, 128 * k + 64:128 * k + 64 + na] = \
            b1[:, HID0 + 64 * k:HID0 + 64 * k + na]
    m["w1P"] = w1p
    m["b1P"] = b1p
    w2 = np.asarray(inp["L0_ffn_w2"], np.float32)
    w2p = np.zeros((2, 22 * 128, D_), np.float32)
    for blk in range(NBLK):
        mt, sh = divmod(blk, 2)
        n = min(64, HID0 - 64 * blk)
        w2p[:, 128 * mt + 64 * sh:128 * mt + 64 * sh + n] = \
            w2[:, 64 * blk:64 * blk + n]
    m["w2P"] = w2p
    m["b2"] = np.ascontiguousarray(np.asarray(inp["L0_ffn_b2"], np.float32))
    m["lng"] = np.ascontiguousarray(np.asarray(inp["L0_ffn_ln_g"], np.float32))
    m["lnb"] = np.ascontiguousarray(np.asarray(inp["L0_ffn_ln_b"], np.float32))
    m["qsw"] = np.ascontiguousarray(
        np.tile(np.asarray(inp["L0_ffn_q_scale"], np.float32), (1, 8)))
    m["ksw"] = np.ascontiguousarray(
        np.tile(np.asarray(inp["L0_ffn_k_scale"], np.float32), (1, 8)))
    m["xtab"] = _xtab(u).astype(ml_dtypes.bfloat16)
    m["gate"] = np.full((128, 1), 1.0 if u == 1 else 0.0, np.float32)
    m["wsend"] = np.full((128, 1), 1.0 if u == 0 else 0.0, np.float32)
    m["ones128"] = np.ones((1, 128), ml_dtypes.bfloat16)
    m["ones128f"] = np.ones((1, 128), np.float32)
    qp = np.arange(128)[:, None]
    kp = np.arange(128)[None, :]
    m["msk_lo"] = np.where(kp >= qp, 0.0, NEG).astype(ml_dtypes.bfloat16)
    m["msk_hi"] = np.where(kp <= qp, 0.0, NEG).astype(ml_dtypes.bfloat16)
    m["prevoff"] = np.full((128, 128), 0.0 if u == 1 else NEG,
                           np.float32).astype(ml_dtypes.bfloat16)
    c = 2 * b + u
    MG = MAGIC
    pr = np.arange(128)
    sl = c & 1
    ixo = np.zeros((128, 19), np.int32)
    for j in range(9):
        ixo[:, j] = 1152 * sl + 128 * j + pr
        ixo[:, 9 + j] = 1152 * (1 - sl) + 128 * j + pr
    ixo[:, 18] = 128 * sl + pr
    m["ixoff"] = ixo
    m["fthr"] = np.array([[MG[0]] * 8 + [MG[1]] * 8], np.int32)
    m["fmask"] = np.full((1, 16), -1, np.int32)
    ep = np.zeros((128, 6), np.int32)
    ep[:, 0] = _EPOCH[0]
    ep[:, 1] = _EPOCH[0]
    ep[:, 2] = _EPOCH[0] + 1
    ep[:, 3] = _EPOCH[0] + 1
    m["epochs"] = ep
    return m


def _xtab(u):
    DH = 64
    j = np.arange(0, DH, 2, dtype=np.float64)
    inv = 1.0 / (10000.0 ** (j / DH))
    t0 = 2048 * u - 1152
    t = np.arange(t0, t0 + 3200, dtype=np.float64)
    fr = t[:, None] * inv[None, :]
    fr = np.concatenate([fr, fr], axis=1)
    base = (j + 0.4 * DH) / (1.4 * DH)
    power = (t - S_ // 2) / float(WIN // 2)
    sc = base[None, :] ** power[:, None]
    sc = np.concatenate([sc, sc], axis=1)
    cos, sin = np.cos(fr), np.sin(fr)
    sgn = np.ones(DH)
    sgn[0:32] = -1.0
    qc = cos * sc
    qs = sgn * sin * sc
    kc = cos / sc
    ks = sgn * sin / sc
    rep = lambda a: np.tile(a, (1, 8)).astype(np.float32)
    return np.ascontiguousarray(
        np.concatenate([rep(qc), rep(qs), rep(kc), rep(ks)], axis=1))


def _run(nc, in_maps):
    used = nc._my_inputs
    maps = [{k: np.ascontiguousarray(v) for k, v in m.items() if k in used}
            for m in in_maps]
    res = run_bass_kernel_spmd(nc, maps, core_ids=list(range(8)))
    out = np.zeros((B_, S_, D_), np.float32)
    for c in range(8):
        b, u = divmod(c, 2)
        out[b, T_ * u:T_ * (u + 1)] = res.results[c]["out"]
    return out


def kernel(**inputs):
    _fresh_epoch()
    in_maps = []
    for c in range(8):
        b, u = divmod(c, 2)
        in_maps.append(_prep_core_inputs(inputs, b, u))
    if STAGE == "full":
        return _run(_build("full"), in_maps)
    x1 = _run(_build("p1"), in_maps)
    def _halo(xcur, b, u):
        if u == 0:
            return np.zeros((1152, D_), np.float32)
        return np.ascontiguousarray(xcur[b, 2048 - 1152:2048])
    for c in range(8):
        b, u = divmod(c, 2)
        in_maps[c]["xin"] = np.ascontiguousarray(x1[b, T_ * u:T_ * (u + 1)])
        in_maps[c]["halo"] = _halo(x1, b, u)
    x0o = _run(_build_layer(0), in_maps)
    for c in range(8):
        b, u = divmod(c, 2)
        in_maps[c]["xin"] = np.ascontiguousarray(x0o[b, T_ * u:T_ * (u + 1)])
        in_maps[c]["halo"] = _halo(x0o, b, u)
        in_maps[c]["x1own"] = np.ascontiguousarray(x1[b, T_ * u:T_ * (u + 1)])
    return _run(_build_layer(1), in_maps)
